# version 01
# baseline (speedup 1.0000x reference)
import numpy as np
import concourse.bass as bass
import concourse.mybir as mybir
from concourse.bass_utils import run_bass_kernel_spmd

F32 = mybir.dt.float32
BF16 = mybir.dt.bfloat16
AF = mybir.ActivationFunctionType
AL = mybir.AluOpType
AX = mybir.AxisListType

D = 2048
DC = 16
TL = 1024
SC = 16
HMC = 8
E = 16
HSC = 32
HD = 128
NH = 16
NEG = -1.0e9
SCALE = float(HD) ** -0.5

ENGS = ("sync", "tensor", "vector", "scalar", "gpsimd")
ESEM = {"tensor": "tms", "vector": "vv", "scalar": "ss", "gpsimd": "gg"}
SERIAL = ("vector", "scalar", "gpsimd")


class B:
    def __init__(self, nc, sems):
        self.nc = nc
        self.st = {e: [] for e in ENGS}
        self.sems = sems
        self.cnt = {k: 0 for k in sems}

    def w(self, eng, sem, val):
        if val is None or val <= 0:
            return
        s = self.sems[sem]
        self.st[eng].append(lambda EE, s=s, v=val: EE.wait_ge(s, v))

    def op(self, eng, meth, *a, sig="auto", sigamt=1, serial=True, **kw):
        if eng in SERIAL and serial:
            self.w(eng, ESEM[eng], self.cnt[ESEM[eng]])
        if sig == "auto":
            sig = ESEM.get(eng)
        tgt, s = None, None
        if sig is not None:
            self.cnt[sig] += sigamt
            tgt = self.cnt[sig]
            s = self.sems[sig]

        import traceback
        tb = traceback.extract_stack()[-3]

        def f(EE, meth=meth, a=a, kw=kw, s=s, sigamt=sigamt, tb=tb):
            try:
                inst = getattr(EE, meth)(*a, **kw)
            except Exception:
                print(f"EMIT-FAIL {meth} from {tb.filename}:{tb.lineno}")
                raise
            if s is not None:
                inst.then_inc(s, sigamt)

        self.st[eng].append(f)
        return tgt

    def dma(self, eng, out, in_):
        sem = "dw" if eng == "sync" else "dg"
        self.cnt[sem] += 16
        tgt = self.cnt[sem]
        s = self.sems[sem]

        def f(EE, out=out, in_=in_, s=s):
            EE.dma_start(out=out, in_=in_).then_inc(s, 16)

        self.st[eng].append(f)
        return tgt

    def barrier(self):
        snap = dict(self.cnt)
        for e in ENGS:
            for sem, val in snap.items():
                if sem != "cc":
                    self.w(e, sem, val)

    def now(self, sem):
        return self.cnt[sem]


class WS:

    def __init__(self, b, bufs):
        self.b = b
        self.bufs = bufs
        self.n = len(bufs)
        self.loaded = []
        self.released = []

    def load(self, src_ap):
        i = len(self.loaded)
        if i >= self.n:
            j = i - self.n
            assert j < len(self.released), f"WS: load {i} before release {j}"
            self.b.w("sync", *self.released[j])
        ncols = 1
        for d in src_ap.shape[1:]:
            ncols *= d
        self.loaded.append(
            self.b.dma("sync", self.bufs[i % self.n][:, 0:ncols], src_ap))
        return i

    def use(self, i, eng="tensor"):
        self.b.w(eng, "dw", self.loaded[i])
        return self.bufs[i % self.n]

    def release(self, i, sem, val):
        assert i == len(self.released), "WS: release out of order"
        self.released.append((sem, val))


def build(tap=()):
    nc = bass.Bass("TRN2", target_bir_lowering=False)

    def dram_in(name, shape):
        return nc.dram_tensor(name, list(shape), F32, kind="ExternalInput")

    x0t = dram_in("x0t", (D, TL))
    wl0 = dram_in("wl0", (5, SC, 128, D))
    woutt = dram_in("woutt", (SC, 128, D))
    wrout = dram_in("wrout", (128, DC * 16))
    we1t = dram_in("we1t", (E, HMC, 128, D))
    we3t = dram_in("we3t", (E, HMC, 128, D))
    we2t = dram_in("we2t", (E, DC, 128, HMC * 128))
    ws1t = dram_in("ws1t", (HSC, 128, D))
    ws3t = dram_in("ws3t", (HSC, 128, D))
    ws2t = dram_in("ws2t", (2, HSC, 128, 8 * 128))
    wqt = dram_in("wqt", (NH, 128, D))
    wkt = dram_in("wkt", (128, D))
    wvt = dram_in("wvt", (128, D))
    wot = dram_in("wot", (DC, 128, D))
    wltt = dram_in("wltt", (DC, 128, D))
    cpack = dram_in("cpack", (128, 96))
    ident = dram_in("identm", (128, 128))
    onesr = dram_in("onesr", (1, 128))
    ropet = dram_in("ropet", (64, TL))
    maskt = dram_in("maskt", (128, 4, 128))

    y = nc.dram_tensor("y", [D, TL], F32, kind="ExternalOutput")
    taps = {t: nc.dram_tensor(t, [D, TL], F32, kind="ExternalOutput")
            for t in tap if not t.startswith("q")}
    if "qdbg" in tap:
        taps["qdbg"] = nc.dram_tensor("qdbg", [128, 2048], F32, kind="ExternalOutput")
        taps["thdbg"] = nc.dram_tensor("thdbg", [128, 128], F32, kind="ExternalOutput")
        taps["adbg"] = nc.dram_tensor("adbg", [128, 2048], F32, kind="ExternalOutput")
        taps["bdbg"] = nc.dram_tensor("bdbg", [128, 2048], F32, kind="ExternalOutput")

    cc1i = nc.dram_tensor("cc1i", [128, 32], F32)
    cc1o = nc.dram_tensor("cc1o", [512, 32], F32)
    cc2i = nc.dram_tensor("cc2i", [8, 128, 128], F32)
    cc2o = nc.dram_tensor("cc2o", [32, 128, 128], F32)

    C_RMSW, C_WFB, C_WOUTB, C_BO, C_LB = 0, 16, 32, 48, 64
    C_SCM, C_SCM1, C_HSEL, C_ONE = 80, 83, 86, 90

    import contextlib
    es = contextlib.ExitStack()
    with es:
        resA = es.enter_context(nc.sbuf_tensor("resA", [128, DC, TL], F32))
        bigA = es.enter_context(nc.sbuf_tensor("bigA", [128, 24, TL], F32))
        pool = es.enter_context(nc.sbuf_tensor("pool", [128, 5, TL], F32))
        wsb0 = es.enter_context(nc.sbuf_tensor("wsb0", [128, D], F32))
        wsb1 = es.enter_context(nc.sbuf_tensor("wsb1", [128, D], F32))
        rowA = es.enter_context(nc.sbuf_tensor("rowA", [1, TL], F32))
        rowB = es.enter_context(nc.sbuf_tensor("rowB", [1, TL], F32))
        smisc = es.enter_context(nc.sbuf_tensor("smisc", [128, 112], F32))
        cpk = es.enter_context(nc.sbuf_tensor("cpk", [128, 96], F32))
        identS = es.enter_context(nc.sbuf_tensor("identS", [128, 128], F32))
        onesrS = es.enter_context(nc.sbuf_tensor("onesrS", [1, 128], F32))
        rtmp = es.enter_context(nc.sbuf_tensor("rtmp", [64, 128], F32))
        wrS = es.enter_context(nc.sbuf_tensor("wrS", [128, DC * 16], F32))
        cwS = es.enter_context(nc.sbuf_tensor("cwS", [128, 8, 16], F32))
        P0 = es.enter_context(nc.psum_tensor("P0", [128, 512], F32))
        P1 = es.enter_context(nc.psum_tensor("P1", [128, 512], F32))
        P2 = es.enter_context(nc.psum_tensor("P2", [128, 512], F32))
        P3 = es.enter_context(nc.psum_tensor("P3", [128, 512], F32))
        P4 = es.enter_context(nc.psum_tensor("P4", [128, 512], F32))
        P5 = es.enter_context(nc.psum_tensor("P5", [128, 512], F32))
        P6 = es.enter_context(nc.psum_tensor("P6", [128, 512], F32))
        P7 = es.enter_context(nc.psum_tensor("P7", [128, 512], F32))
        s_dw = es.enter_context(nc.semaphore("dw"))
        s_dg = es.enter_context(nc.semaphore("dg"))
        s_tms = es.enter_context(nc.semaphore("tms"))
        s_vv = es.enter_context(nc.semaphore("vv"))
        s_ss = es.enter_context(nc.semaphore("ss"))
        s_gg = es.enter_context(nc.semaphore("gg"))
        s_cc = es.enter_context(nc.semaphore("cc"))
        block = es.enter_context(nc.Block())

        P = [P0, P1, P2, P3, P4, P5, P6, P7]
        sems = dict(dw=s_dw, dg=s_dg, tms=s_tms, vv=s_vv, ss=s_ss, gg=s_gg, cc=s_cc)
        b = B(nc, sems)
        ws = WS(b, [wsb0, wsb1])

        bf3 = bigA.ap().rearrange("p a t -> p (a t)").bitcast(BF16) \
            .rearrange("p (a t) -> p a t", a=48)
        ogT = bf3[:, 0:SC, :]
        f16 = bf3[:, SC:2 * SC, :]
        z16 = bf3[:, 2 * SC:3 * SC, :]
        xnB = bigA[:, 0:DC, :]
        auxB = bigA[:, DC:24, :].rearrange("p a t -> p (a t)")

        def aux(off, n):
            return auxB[:, off:off + n]

        sq_t = pool[:, 0, :]
        mbc = pool[:, 1, :]
        rbc = pool[:, 2, :]
        srmsbc = pool[:, 3, :]
        x0r = pool[:, 4, :]
        stg = pool[:, 0, :]
        eptm = pool[:, 1, :]
        scanb = pool[:, 2, :].rearrange("p t -> p t").bitcast(BF16)
        gsmA = pool[:, 4, :]
        cwT = pool[0:16, 3, :]
        cwbc = pool[:, 4, :]
        ropeS = pool[0:64, 3, :]
        maskC = pool[:, 4, 0:512].rearrange("p (a m) -> p a m", a=4)
        gsm = pool[:, 4, 512:1024]

        rmsw = cpk[:, C_RMSW:C_RMSW + 16]
        wfb = cpk[:, C_WFB:C_WFB + 16]
        woutb = cpk[:, C_WOUTB:C_WOUTB + 16]
        bo = cpk[:, C_BO:C_BO + 16]
        lb = cpk[:, C_LB:C_LB + 16]
        scm = cpk[:, C_SCM:C_SCM + 3]
        scm1 = cpk[:, C_SCM1:C_SCM1 + 3]
        hsel = cpk[:, C_HSEL:C_HSEL + 4]
        onesc = cpk[:, C_ONE:C_ONE + 1]
        nc.const_aps.register(F32, 1e-5, cpk[:, 91:92])
        nc.const_aps.register(F32, SCALE, cpk[:, 92:93])
        invw = smisc[:, 0:16]
        hsum = smisc[:, 16:32]
        psum_ = smisc[:, 32:48]
        carry = smisc[:, 48:64]
        ctmp1 = smisc[:, 64:80]
        ctmp2 = smisc[:, 80:96]
        ctmp3 = smisc[:, 96:112]

        resT = [resA[:, c, :] for c in range(DC)]
        xnT = [xnB[:, c, :] for c in range(DC)]

        t_x0 = b.dma("sync", resA[:], x0t.ap().rearrange("(c p) n -> p c n", p=128))
        t_cp = b.dma("gpsimd", cpk[:], cpack.ap())
        t_id = b.dma("gpsimd", identS[:], ident.ap())
        t_or = b.dma("gpsimd", onesrS[:], onesr.ap())
        t_wr = b.dma("gpsimd", wrS[:], wrout.ap())
        b.w("vector", "dg", t_wr)
        b.op("vector", "reciprocal", out=invw, in_=rmsw)
        for e in ENGS:
            b.w(e, "dw", t_x0)
            b.w(e, "dg", t_wr)

        def layernorm(src, dst):
            lastsq = None
            for c in range(DC):
                if lastsq is not None:
                    b.w("scalar", "tms", lastsq)
                sst = b.op("scalar", "activation", out=sq_t, in_=src[c], func=AF.Square)
                for hf in range(2):
                    b.op("tensor", "matmul", out=P[4 + hf][0:1, :], lhsT=onesc,
                         rhs=src[c][:, hf * 512:(hf + 1) * 512],
                         start=(c == 0), stop=(c == DC - 1), sig=None)
                b.w("tensor", "ss", sst)
                for hf in range(2):
                    lastsq = b.op("tensor", "matmul", out=P[6 + hf][0:1, :], lhsT=onesc,
                                  rhs=sq_t[:, hf * 512:(hf + 1) * 512],
                                  start=(c == 0), stop=(c == DC - 1))
            b.w("vector", "tms", lastsq)
            for hf in range(2):
                b.op("vector", "tensor_copy", out=rowA[0:1, hf * 512:(hf + 1) * 512],
                     in_=P[4 + hf][0:1, :])
                b.op("vector", "tensor_copy", out=rowB[0:1, hf * 512:(hf + 1) * 512],
                     in_=P[6 + hf][0:1, :])
            b.op("vector", "tensor_scalar_mul", out=rowA[0:1, :], in0=rowA[0:1, :],
                 scalar1=1.0 / D)
            b.op("vector", "tensor_scalar_mul", out=rowB[0:1, :], in0=rowB[0:1, :],
                 scalar1=1.0 / D)
            b.op("vector", "tensor_mul", out=sq_t[0:1, :], in0=rowA[0:1, :],
                 in1=rowA[0:1, :])
            v1 = b.op("vector", "tensor_sub", out=rowB[0:1, :], in0=rowB[0:1, :],
                      in1=sq_t[0:1, :])
            b.w("scalar", "vv", v1)
            s1 = b.op("scalar", "activation", out=rowB[0:1, :], in_=rowB[0:1, :],
                      func=AF.Sqrt, bias=1e-5)
            b.w("vector", "ss", s1)
            v2 = b.op("vector", "reciprocal", out=rowB[0:1, :], in_=rowB[0:1, :])
            b.w("tensor", "vv", v2)
            for hf in range(2):
                b.op("tensor", "matmul", out=P[4 + hf][:, :], lhsT=onesrS[:],
                     rhs=rowA[0:1, hf * 512:(hf + 1) * 512], start=True, stop=True,
                     sig=None)
                tb_ = b.op("tensor", "matmul", out=P[6 + hf][:, :], lhsT=onesrS[:],
                           rhs=rowB[0:1, hf * 512:(hf + 1) * 512], start=True,
                           stop=True)
            b.w("vector", "tms", tb_)
            for hf in range(2):
                b.op("vector", "tensor_copy", out=mbc[:, hf * 512:(hf + 1) * 512],
                     in_=P[4 + hf][:, :])
                vbc = b.op("vector", "tensor_copy", out=rbc[:, hf * 512:(hf + 1) * 512],
                           in_=P[6 + hf][:, :])
            b.w("gpsimd", "vv", vbc)
            for c in range(DC):
                eng = "vector" if c % 2 == 0 else "gpsimd"
                b.op(eng, "tensor_sub", out=dst[c], in0=src[c], in1=mbc)
                b.op(eng, "tensor_mul", out=dst[c], in0=dst[c], in1=rbc)
            return b.now("vv"), b.now("gg")

        vA, gA = layernorm(resT, resT)

        b.w("scalar", "vv", vA)
        b.w("scalar", "gg", gA)
        b.w("tensor", "vv", vA)
        b.w("tensor", "gg", gA)
        lastsq = None
        for c in range(DC):
            if lastsq is not None:
                b.w("scalar", "tms", lastsq)
            sst = b.op("scalar", "activation", out=sq_t, in_=resT[c], func=AF.Square)
            b.w("tensor", "ss", sst)
            for hf in range(2):
                lastsq = b.op("tensor", "matmul", out=P[6 + hf][0:1, :], lhsT=onesc,
                              rhs=sq_t[:, hf * 512:(hf + 1) * 512],
                              start=(c == 0), stop=(c == DC - 1))
        b.w("vector", "tms", lastsq)
        for hf in range(2):
            b.op("vector", "tensor_copy", out=rowB[0:1, hf * 512:(hf + 1) * 512],
                 in_=P[6 + hf][0:1, :])
        v1 = b.op("vector", "tensor_scalar_mul", out=rowB[0:1, :], in0=rowB[0:1, :],
                  scalar1=1.0 / D)
        b.w("scalar", "vv", v1)
        s1 = b.op("scalar", "activation", out=rowA[0:1, :], in_=rowB[0:1, :],
                  func=AF.Sqrt, bias=1e-5)
        b.w("vector", "ss", s1)
        v2 = b.op("vector", "reciprocal", out=rowB[0:1, :], in_=rowA[0:1, :])
        b.w("tensor", "vv", v2)
        for hf in range(2):
            b.op("tensor", "matmul", out=P[4 + hf][:, :], lhsT=onesrS[:],
                 rhs=rowA[0:1, hf * 512:(hf + 1) * 512], start=True, stop=True,
                 sig=None)
            tb_ = b.op("tensor", "matmul", out=P[6 + hf][:, :], lhsT=onesrS[:],
                       rhs=rowB[0:1, hf * 512:(hf + 1) * 512], start=True, stop=True)
        b.w("vector", "tms", tb_)
        for hf in range(2):
            b.op("vector", "tensor_copy", out=srmsbc[:, hf * 512:(hf + 1) * 512],
                 in_=P[4 + hf][:, :])
            vbc = b.op("vector", "tensor_copy", out=rbc[:, hf * 512:(hf + 1) * 512],
                       in_=P[6 + hf][:, :])
        for c in range(DC):
            b.op("vector", "scalar_tensor_tensor", out=resT[c], in0=resT[c],
                 scalar=rmsw[:, c:c + 1], in1=rbc, op0=AL.mult, op1=AL.mult)
        xnV, xnG = b.now("vv"), b.now("gg")

        NSL = SC * 5
        bank_free = {i: ("tms", 0) for i in range(6)}
        bank_free[4] = ("vv", vbc)
        bank_free[5] = ("vv", vbc)
        silu_ga = None
        sig_zg = None
        scan_last = 0
        b.w("tensor", "vv", xnV)
        b.w("tensor", "gg", xnG)
        for s in range(SC):
            for p in range(5):
                k = s * 5 + p
                if k == 0:
                    ws.load(wl0.ap()[0, 0, :, :])
                if k + 1 < NSL:
                    p2, s2 = (k + 1) % 5, (k + 1) // 5
                    ws.load(wl0.ap()[p2, s2, :, :])
                buf = ws.use(k)
                for hf in range(2):
                    b.w("tensor", *bank_free[(k % 3) * 2 + hf])
                    for c in range(DC):
                        b.op("tensor", "matmul", out=P[(k % 3) * 2 + hf][:, :],
                             lhsT=buf[:, c * 128:(c + 1) * 128],
                             rhs=resT[c][:, hf * 512:(hf + 1) * 512],
                             start=(c == 0), stop=(c == DC - 1),
                             sig=(None if c < DC - 1 else "auto"))
                tmm = b.now("tms")
                ws.release(k, "tms", tmm)
                bp = (k % 3) * 2
                if p == 0:
                    b.w("scalar", "tms", tmm)
                    b.w("scalar", "vv", b.now("vv"))
                    for hf in range(2):
                        sv = b.op("scalar", "activation",
                                  out=sq_t[:, hf * 512:(hf + 1) * 512],
                                  in_=P[bp + hf][:, :], func=AF.Silu)
                    silu_ga = sv
                    bank_free[bp] = ("ss", sv)
                    bank_free[bp + 1] = ("ss", sv)
                elif p == 1:
                    b.w("vector", "tms", tmm)
                    b.w("vector", "ss", silu_ga)
                    for hf in range(2):
                        vv1 = b.op("vector", "tensor_mul",
                                   out=ogT[:, s, hf * 512:(hf + 1) * 512],
                                   in0=sq_t[:, hf * 512:(hf + 1) * 512],
                                   in1=P[bp + hf][:, :])
                    bank_free[bp] = ("vv", vv1)
                    bank_free[bp + 1] = ("vv", vv1)
                elif p == 2:
                    b.w("scalar", "tms", tmm)
                    for hf in range(2):
                        sv = b.op("scalar", "activation",
                                  out=f16[:, s, hf * 512:(hf + 1) * 512],
                                  in_=P[bp + hf][:, :], func=AF.Sigmoid,
                                  bias=wfb[:, s:s + 1])
                    bank_free[bp] = ("ss", sv)
                    bank_free[bp + 1] = ("ss", sv)
                elif p == 3:
                    b.w("scalar", "tms", tmm)
                    b.w("scalar", "vv", b.now("vv"))
                    for hf in range(2):
                        sv = b.op("scalar", "activation",
                                  out=mbc[:, hf * 512:(hf + 1) * 512],
                                  in_=P[bp + hf][:, :], func=AF.Sigmoid)
                    sig_zg = sv
                    bank_free[bp] = ("ss", sv)
                    bank_free[bp + 1] = ("ss", sv)
                else:
                    b.w("vector", "tms", tmm)
                    b.w("vector", "ss", sig_zg)
                    for hf in range(2):
                        vv1 = b.op("vector", "tensor_mul",
                                   out=z16[:, s, hf * 512:(hf + 1) * 512],
                                   in0=mbc[:, hf * 512:(hf + 1) * 512],
                                   in1=P[bp + hf][:, :])
                    bank_free[bp] = ("vv", vv1)
                    bank_free[bp + 1] = ("vv", vv1)
                    b.op("vector", "tensor_tensor_scan", out=scanb[:, 0:TL],
                         data0=f16[:, s, :], data1=z16[:, s, :], initial=0.0,
                         op0=AL.mult, op1=AL.add)
                    b.op("vector", "tensor_copy", out=hsum[:, s:s + 1],
                         in_=scanb[:, TL - 1:TL])
                    b.op("vector", "tensor_tensor_scan", out=scanb[:, TL:2 * TL],
                         data0=f16[:, s, :], data1=f16[:, s, :], initial=1.0,
                         op0=AL.mult, op1=AL.bypass)
                    scan_last = b.op("vector", "tensor_copy", out=psum_[:, s:s + 1],
                                     in_=scanb[:, 2 * TL - 1:2 * TL])

        b.w("gpsimd", "vv", scan_last)
        tgi = b.dma("gpsimd", cc1i.ap(), smisc[:, 16:48])
        b.w("gpsimd", "dg", tgi)
        ccv = b.op("gpsimd", "collective_compute", "AllGather", AL.bypass,
                   replica_groups=[[0, 1, 2, 3], [4, 5, 6, 7]],
                   ins=[cc1i.ap().opt()], outs=[cc1o.ap().opt()],
                   sig="cc", serial=False)
        b.w("gpsimd", "cc", ccv)
        tg = None
        for r in range(3):
            tg = b.dma("gpsimd", gsmA[:, r * 32:(r + 1) * 32],
                       cc1o.ap().rearrange("(r p) c -> p r c", p=128)[:, r, :])
        b.w("vector", "dg", tg)
        b.op("vector", "tensor_scalar_mul", out=carry, in0=gsmA[:, 0:16],
             scalar1=scm[:, 0:1])
        cdone = None
        for j in (1, 2):
            b.op("vector", "tensor_scalar_mul", out=ctmp1, in0=gsmA[:, j * 32:j * 32 + 16],
                 scalar1=scm[:, j:j + 1])
            b.op("vector", "tensor_scalar", out=ctmp2, in0=gsmA[:, j * 32 + 16:j * 32 + 32],
                 scalar1=scm[:, j:j + 1], scalar2=scm1[:, j:j + 1],
                 op0=AL.mult, op1=AL.add)
            b.op("vector", "tensor_mul", out=carry, in0=carry, in1=ctmp2)
            cdone = b.op("vector", "tensor_add", out=carry, in0=carry, in1=ctmp1)

        gate_done = []
        for s in range(SC):
            b.op("vector", "tensor_tensor_scan", out=z16[:, s, :],
                 data0=f16[:, s, :], data1=z16[:, s, :],
                 initial=carry[:, s:s + 1], op0=AL.mult, op1=AL.add)
            gate_done.append(b.op("vector", "tensor_mul", out=ogT[:, s, :],
                                  in0=ogT[:, s, :], in1=z16[:, s, :]))

        x0v = x0t.ap().rearrange("(c p) n -> p c n", p=128)
        b.w("gpsimd", "vv", cdone)
        ep_v = 0
        st_free = [("tms", 0), ("tms", 0)]
        wk_i = len(ws.loaded)
        for dhalf in range(2):
            for tb in range(2):
                tok = slice(tb * 512, (tb + 1) * 512)
                lastd = None
                for sc in range(SC):
                    kk = wk_i + (dhalf * 2 + tb) * SC + sc
                    if kk == wk_i:
                        ws.load(woutt.ap()[0, :, :])
                    if sc + 1 < SC or dhalf * 2 + tb < 3:
                        nxt = (dhalf * 2 + tb) * SC + sc + 1
                        ws.load(woutt.ap()[nxt % SC, :, :])
                    buf = ws.use(kk)
                    sl = sc % 2
                    b.w("vector", "vv", gate_done[sc])
                    b.w("vector", *st_free[sl])
                    vstg = b.op("vector", "tensor_copy",
                                out=stg[:, sl * 512:(sl + 1) * 512],
                                in_=ogT[:, sc, tok])
                    b.w("tensor", "vv", vstg)
                    if sc == 0:
                        b.w("tensor", "vv", ep_v)
                    for di in range(8):
                        d = dhalf * 8 + di
                        lastd = b.op("tensor", "matmul", out=P[di][:, :],
                                     lhsT=buf[:, d * 128:(d + 1) * 128],
                                     rhs=stg[:, sl * 512:(sl + 1) * 512],
                                     start=(sc == 0), stop=(sc == SC - 1),
                                     sig=(None if di < 7 else "auto"))
                    st_free[sl] = ("tms", lastd)
                    ws.release(kk, "tms", lastd)
                for di in range(8):
                    d = dhalf * 8 + di
                    b.w("gpsimd", "vv", b.now("vv"))
                    tx0 = b.dma("gpsimd", x0r[:, (di % 2) * 512:(di % 2 + 1) * 512],
                                x0v[:, d, tok])
                    g1 = b.op("vector", "scalar_tensor_tensor", out=eptm[:, 0:512],
                              in0=resT[d][:, tok], scalar=invw[:, d:d + 1],
                              in1=srmsbc[:, tok], op0=AL.mult, op1=AL.mult)
                    b.w("vector", "dg", tx0)
                    g2 = b.op("vector", "tensor_add", out=eptm[:, 0:512],
                              in0=eptm[:, 0:512],
                              in1=x0r[:, (di % 2) * 512:(di % 2 + 1) * 512])
                    b.w("vector", "tms", lastd)
                    ep_v = b.op("vector", "scalar_tensor_tensor", out=resT[d][:, tok],
                                in0=P[di][:, :], scalar=woutb[:, d:d + 1],
                                in1=eptm[:, 0:512], op0=AL.add, op1=AL.add)
        b.barrier()
        if "x1" in taps:
            tv = None
            for c in range(DC):
                tv = b.dma("gpsimd", taps["x1"].ap().rearrange(
                    "(c p) n -> p c n", p=128)[:, c, :], resT[c])
            b.w("gpsimd", "dg", tv)
            b.barrier()

        layernorm(resT, xnT)
        b.barrier()

        vcwT = 0
        for tt in range(8):
            pb = P[6 + (tt % 2)]
            b.w("tensor", "vv", b.now("vv"))
            for c in range(DC):
                b.op("tensor", "matmul", out=pb[:, 0:16],
                     lhsT=xnT[c][:, tt * 128:(tt + 1) * 128],
                     rhs=wrS[:, c * 16:(c + 1) * 16],
                     start=(c == 0), stop=(c == DC - 1),
                     sig=(None if c < DC - 1 else "auto"))
            tmm = b.now("tms")
            b.w("vector", "tms", tmm)
            cwt = cwS[:, tt, :]
            mx = ctmp1[:, 0:1]
            b.op("vector", "reduce_max", out=mx, in_=pb[:, 0:16], axis=AX.X,
                 negate=True)
            vv1 = b.op("vector", "tensor_copy", out=cwt, in_=pb[:, 0:16])
            b.w("scalar", "vv", vv1)
            sx = b.op("scalar", "activation", out=cwt, in_=cwt, func=AF.Exp, bias=mx)
            b.w("vector", "ss", sx)
            sm = ctmp1[:, 1:2]
            b.op("vector", "reduce_sum", out=sm, in_=cwt, axis=AX.X)
            b.op("vector", "reciprocal", out=sm, in_=sm)
            b.op("vector", "tensor_scalar_mul", out=cwt, in0=cwt, scalar1=sm)
            cur = ctmp2
            b.op("vector", "tensor_copy", out=cur, in_=cwt)
            th = ctmp1[:, 2:3]
            for it in range(4):
                b.op("vector", "reduce_max", out=th, in_=cur, axis=AX.X)
                if it < 3:
                    b.op("vector", "tensor_single_scalar", out=ctmp3, in_=cur,
                         scalar=th, op=AL.is_ge)
                    b.op("vector", "tensor_scalar_mul", out=ctmp3, in0=ctmp3,
                         scalar1=1e30)
                    b.op("vector", "tensor_sub", out=cur, in0=cur, in1=ctmp3)
            b.op("vector", "tensor_single_scalar", out=cur, in_=cwt, scalar=th,
                 op=AL.is_ge)
            b.op("vector", "tensor_mul", out=cwt, in0=cwt, in1=cur)
            s4 = ctmp1[:, 1:2]
            b.op("vector", "reduce_sum", out=s4, in_=cwt, axis=AX.X)
            b.op("vector", "reciprocal", out=s4, in_=s4)
            vcw = b.op("vector", "tensor_scalar_mul", out=cwt, in0=cwt, scalar1=s4)
            b.w("tensor", "vv", vcw)
            ttr = b.op("tensor", "transpose", out=P[4 + (tt % 2)][0:16, 0:128],
                       in_=cwt, identity=identS[:])
            b.w("vector", "tms", ttr)
            vcwT = b.op("vector", "tensor_copy", out=cwT[:, tt * 128:(tt + 1) * 128],
                        in_=P[4 + (tt % 2)][0:16, 0:128])
        b.barrier()

        gbufF = aux(0, 8 * TL)
        ep_sh = ("vv", 0)
        for tq in range(4):
            tok = slice(tq * 256, (tq + 1) * 256)
            glu_done = [0] * HSC
            for hs in range(HSC):
                k1 = ws.load(ws1t.ap()[hs, :, :])
                buf1 = ws.use(k1)
                b.w("tensor", *ep_sh)
                for c in range(DC):
                    b.op("tensor", "matmul", out=P0[:, 0:256],
                         lhsT=buf1[:, c * 128:(c + 1) * 128], rhs=xnT[c][:, tok],
                         start=(c == 0), stop=(c == DC - 1),
                         sig=(None if c < DC - 1 else "auto"))
                t1 = b.now("tms")
                ws.release(k1, "tms", t1)
                k3 = ws.load(ws3t.ap()[hs, :, :])
                buf3 = ws.use(k3)
                for c in range(DC):
                    b.op("tensor", "matmul", out=P1[:, 0:256],
                         lhsT=buf3[:, c * 128:(c + 1) * 128], rhs=xnT[c][:, tok],
                         start=(c == 0), stop=(c == DC - 1),
                         sig=(None if c < DC - 1 else "auto"))
                t3 = b.now("tms")
                ws.release(k3, "tms", t3)
                b.w("scalar", "tms", t1)
                b.w("scalar", "vv", b.now("vv"))
                ssl = b.op("scalar", "activation", out=sq_t[:, 0:256],
                           in_=P0[:, 0:256], func=AF.Silu)
                b.w("vector", "ss", ssl)
                b.w("vector", "tms", t3)
                glu_done[hs] = b.op("vector", "tensor_mul",
                                    out=gbufF[:, hs * 256:(hs + 1) * 256],
                                    in0=sq_t[:, 0:256], in1=P1[:, 0:256])
                ep_sh = ("vv", glu_done[hs])
            for dh in range(2):
                lastd = None
                for hs in range(HSC):
                    ki = ws.load(ws2t.ap()[dh, hs, :, :])
                    buf = ws.use(ki)
                    b.w("tensor", "vv", glu_done[hs])
                    if hs == 0:
                        b.w("tensor", "vv", b.now("vv"))
                        b.w("tensor", "gg", b.now("gg"))
                    for di in range(8):
                        lastd = b.op("tensor", "matmul", out=P[di][:, 0:256],
                                     lhsT=buf[:, di * 128:(di + 1) * 128],
                                     rhs=gbufF[:, hs * 256:(hs + 1) * 256],
                                     start=(hs == 0), stop=(hs == HSC - 1),
                                     sig=(None if di < 7 else "auto"))
                    ws.release(ki, "tms", lastd)
                for di in range(8):
                    d = dh * 8 + di
                    b.w("vector", "tms", lastd)
                    epv = b.op("vector", "scalar_tensor_tensor", out=resT[d][:, tok],
                               in0=P[di][:, 0:256], scalar=0.2,
                               in1=resT[d][:, tok], op0=AL.mult, op1=AL.add)
                ep_sh = ("vv", b.now("vv"))
                b.w("tensor", "vv", b.now("vv"))
                b.w("tensor", "gg", b.now("gg"))
        b.barrier()

        for e in range(E):
            b.w("tensor", "vv", b.now("vv"))
            b.w("tensor", "gg", b.now("gg"))
            tsel = None
            for hf in range(2):
                tsel = b.op("tensor", "matmul", out=P[6 + hf][0:1, :],
                            lhsT=identS[0:16, e:e + 1],
                            rhs=cwT[:, hf * 512:(hf + 1) * 512],
                            start=True, stop=True)
            b.w("vector", "tms", tsel)
            b.w("vector", "gg", b.now("gg"))
            vsel = None
            for hf in range(2):
                vsel = b.op("vector", "tensor_copy",
                            out=rowA[0:1, hf * 512:(hf + 1) * 512],
                            in_=P[6 + hf][0:1, :])
            b.w("tensor", "vv", vsel)
            tbc = None
            for hf in range(2):
                tbc = b.op("tensor", "matmul", out=P[6 + hf][:, :], lhsT=onesrS[:],
                           rhs=rowA[0:1, hf * 512:(hf + 1) * 512],
                           start=True, stop=True)
            b.w("vector", "tms", tbc)
            vbc = None
            for hf in range(2):
                vbc = b.op("vector", "tensor_copy",
                           out=cwbc[:, hf * 512:(hf + 1) * 512], in_=P[6 + hf][:, :])
            gb_last = 0
            for ht in range(HMC):
                k1 = ws.load(we1t.ap()[e, ht, :, :])
                buf1 = ws.use(k1)
                b.w("tensor", "ss", b.now("ss"))
                b.w("tensor", "vv", b.now("vv"))
                for hf in range(2):
                    for c in range(DC):
                        b.op("tensor", "matmul", out=P[hf][:, :],
                             lhsT=buf1[:, c * 128:(c + 1) * 128],
                             rhs=xnT[c][:, hf * 512:(hf + 1) * 512],
                             start=(c == 0), stop=(c == DC - 1),
                             sig=(None if c < DC - 1 else "auto"))
                t1 = b.now("tms")
                ws.release(k1, "tms", t1)
                k3 = ws.load(we3t.ap()[e, ht, :, :])
                buf3 = ws.use(k3)
                for hf in range(2):
                    for c in range(DC):
                        b.op("tensor", "matmul", out=P[2 + hf][:, :],
                             lhsT=buf3[:, c * 128:(c + 1) * 128],
                             rhs=xnT[c][:, hf * 512:(hf + 1) * 512],
                             start=(c == 0), stop=(c == DC - 1),
                             sig=(None if c < DC - 1 else "auto"))
                t3 = b.now("tms")
                ws.release(k3, "tms", t3)
                b.w("scalar", "tms", t1)
                b.w("scalar", "vv", b.now("vv"))
                ssl = None
                for hf in range(2):
                    ssl = b.op("scalar", "activation",
                               out=sq_t[:, hf * 512:(hf + 1) * 512],
                               in_=P[hf][:, :], func=AF.Silu)
                b.w("vector", "ss", ssl)
                b.w("vector", "tms", t3)
                b.w("vector", "gg", b.now("gg"))
                vg1 = None
                for hf in range(2):
                    vg1 = b.op("vector", "tensor_mul",
                               out=mbc[:, hf * 512:(hf + 1) * 512],
                               in0=sq_t[:, hf * 512:(hf + 1) * 512],
                               in1=P[2 + hf][:, :])
                b.w("gpsimd", "vv", vg1)
                b.w("gpsimd", "tms", b.now("tms"))
                gb_last = b.op("gpsimd", "tensor_mul",
                               out=gbufF[:, ht * TL:(ht + 1) * TL],
                               in0=mbc[:, 0:TL], in1=cwbc[:])
            for dt in range(DC):
                ki = ws.load(we2t.ap()[e, dt, :, :])
                buf = ws.use(ki)
                pb = P[4 + (dt % 2)]
                b.w("tensor", "gg", gb_last)
                b.w("tensor", "vv", b.now("vv"))
                b.w("tensor", "gg", b.now("gg"))
                tmm2 = None
                for hf in range(2):
                    for hc in range(HMC):
                        b.op("tensor", "matmul", out=pb[:, :],
                             lhsT=buf[:, hc * 128:(hc + 1) * 128],
                             rhs=gbufF[:, hc * TL + hf * 512:hc * TL + (hf + 1) * 512],
                             start=(hc == 0), stop=(hc == HMC - 1),
                             sig=(None if hc < HMC - 1 else "auto"))
                    tmm2 = b.now("tms")
                    b.w("vector", "tms", tmm2)
                    epv = b.op("vector", "scalar_tensor_tensor",
                               out=resT[dt][:, hf * 512:(hf + 1) * 512],
                               in0=pb[:, :], scalar=0.8,
                               in1=resT[dt][:, hf * 512:(hf + 1) * 512],
                               op0=AL.mult, op1=AL.add)
                    b.w("tensor", "vv", epv)
                ws.release(ki, "tms", tmm2)
        b.barrier()
        if "x2" in taps:
            tv = None
            for c in range(DC):
                tv = b.dma("gpsimd", taps["x2"].ap().rearrange(
                    "(c p) n -> p c n", p=128)[:, c, :], resT[c])
            b.w("gpsimd", "dg", tv)
            b.barrier()

        layernorm(resT, xnT)
        b.barrier()

        t_rp = b.dma("gpsimd", ropeS[:], ropet.ap())
        t_mk = b.dma("gpsimd", maskC[:], maskt.ap())
        for e_ in ("tensor", "vector", "scalar"):
            b.w(e_, "dg", t_mk)
        kT = aux(0, 12 * 128).rearrange("p (a m) -> p a m", a=12)
        vtm = aux(12 * 128, 12 * 128).rearrange("p (a m) -> p a m", a=12)
        attb = aux(24 * 128, 16 * 128).rearrange("p (a m) -> p a m", a=16)
        exps = aux(40 * 128, 5 * 128).rearrange("p (a m) -> p a m", a=5)
        qrope = aux(45 * 128, 128)
        rbc2 = aux(46 * 128, 128)
        rsb = rowA[0:1, 0:128]
        cosT = ropeS[0:32, :]
        sinT = ropeS[32:64, :]

        kwi = ws.load(wkt.ap())
        vwi = ws.load(wvt.ap())
        kbuf = ws.use(kwi)
        vbuf = ws.use(vwi)

        def kv_block(blk):
            t0 = (blk - 4) * 128
            tokb = slice(t0, t0 + 128)
            pb = P[blk % 2]
            b.w("tensor", "vv", b.now("vv"))
            for c in range(DC):
                b.op("tensor", "matmul", out=pb[:, 0:128],
                     lhsT=kbuf[:, c * 128:(c + 1) * 128], rhs=xnT[c][:, tokb],
                     start=(c == 0), stop=(c == DC - 1),
                     sig=(None if c < DC - 1 else "auto"))
            tk = b.now("tms")
            b.w("vector", "tms", tk)
            cs = cosT[:, tokb]
            sn = sinT[:, tokb]
            kb_ = kT[:, blk, :]
            b.op("vector", "tensor_mul", out=kb_[0:32, :], in0=pb[0:32, 0:128], in1=cs)
            b.op("vector", "tensor_mul", out=rtmp[0:32, :], in0=pb[32:64, 0:128], in1=sn)
            b.op("vector", "tensor_sub", out=kb_[0:32, :], in0=kb_[0:32, :],
                 in1=rtmp[0:32, :])
            b.op("vector", "tensor_mul", out=kb_[32:64, :], in0=pb[0:32, 0:128], in1=sn)
            b.op("vector", "tensor_mul", out=rtmp[32:64, :], in0=pb[32:64, 0:128], in1=cs)
            b.op("vector", "tensor_add", out=kb_[32:64, :], in0=kb_[32:64, :],
                 in1=rtmp[32:64, :])
            vk = b.op("vector", "tensor_copy", out=kb_[64:128, :],
                      in_=pb[64:128, 0:128])
            pv = P[2 + blk % 2]
            b.w("tensor", "vv", vk)
            for c in range(DC):
                b.op("tensor", "matmul", out=pv[:, 0:128],
                     lhsT=xnT[c][:, tokb], rhs=vbuf[:, c * 128:(c + 1) * 128],
                     start=(c == 0), stop=(c == DC - 1),
                     sig=(None if c < DC - 1 else "auto"))
            tv_ = b.now("tms")
            b.w("vector", "tms", tv_)
            return b.op("vector", "tensor_copy", out=vtm[:, blk, :], in_=pv[:, 0:128])

        for blk in (8, 9, 10, 11):
            kv_tail = kv_block(blk)
        b.w("gpsimd", "vv", kv_tail)
        tg1 = b.dma("gpsimd", cc2i.ap()[0:4, :, :].rearrange("a p m -> p a m"),
                    kT[:, 8:12, :])
        tg2 = b.dma("gpsimd", cc2i.ap()[4:8, :, :].rearrange("a p m -> p a m"),
                    vtm[:, 8:12, :])
        b.w("gpsimd", "dg", tg2)
        ccv2 = b.op("gpsimd", "collective_compute", "AllGather", AL.bypass,
                    replica_groups=[[0, 1, 2, 3], [4, 5, 6, 7]],
                    ins=[cc2i.ap().opt()], outs=[cc2o.ap().opt()],
                    sig="cc", serial=False)
        for blk in (4, 5, 6, 7):
            kv_own = kv_block(blk)
        tmm_kv = b.now("tms")
        ws.release(kwi, "tms", tmm_kv)
        ws.release(vwi, "tms", tmm_kv)
        b.w("gpsimd", "cc", ccv2)
        vprev = kv_own
        for part in range(2):
            base = kT if part == 0 else vtm
            dst = aux(part * 12 * 128, 4 * 128)
            for r in range(4):
                b.w("gpsimd", "vv", vprev)
                tgr = b.dma("gpsimd", gsm[:, :],
                            cc2o.ap()[r * 8 + part * 4:r * 8 + part * 4 + 4, :, :]
                            .rearrange("a p m -> p a m"))
                b.w("vector", "dg", tgr)
                if r == 0:
                    vprev = b.op("vector", "tensor_scalar_mul", out=dst,
                                 in0=gsm[:].rearrange("p (a m) -> p a m", a=4)
                                 .rearrange("p a m -> p (a m)"),
                                 scalar1=hsel[:, 0:1])
                else:
                    vprev = b.op("vector", "scalar_tensor_tensor", out=dst,
                                 in0=gsm[:], scalar=hsel[:, r:r + 1], in1=dst,
                                 op0=AL.mult, op1=AL.add)
        b.barrier()

        for qt in range(8):
            tokq = slice(qt * 128, (qt + 1) * 128)
            att_last = 0
            tse = 0
            for h in range(NH):
                ki = ws.load(wqt.ap()[h, :, :])
                buf = ws.use(ki)
                b.w("tensor", "vv", b.now("vv"))
                b.w("tensor", "ss", b.now("ss"))
                for c in range(DC):
                    b.op("tensor", "matmul", out=P0[:, 0:128],
                         lhsT=buf[:, c * 128:(c + 1) * 128], rhs=xnT[c][:, tokq],
                         start=(c == 0), stop=(c == DC - 1),
                         sig=(None if c < DC - 1 else "auto"))
                tq_ = b.now("tms")
                ws.release(ki, "tms", tq_)
                b.w("vector", "tms", tq_)
                cs = cosT[:, tokq]
                sn = sinT[:, tokq]
                b.op("vector", "tensor_mul", out=qrope[0:32, :], in0=P0[0:32, 0:128],
                     in1=cs)
                b.op("vector", "tensor_mul", out=rtmp[0:32, :], in0=P0[32:64, 0:128],
                     in1=sn)
                b.op("vector", "tensor_sub", out=qrope[0:32, :], in0=qrope[0:32, :],
                     in1=rtmp[0:32, :])
                b.op("vector", "tensor_mul", out=qrope[32:64, :], in0=P0[0:32, 0:128],
                     in1=sn)
                b.op("vector", "tensor_mul", out=rtmp[32:64, :], in0=P0[32:64, 0:128],
                     in1=cs)
                b.op("vector", "tensor_add", out=qrope[32:64, :], in0=qrope[32:64, :],
                     in1=rtmp[32:64, :])
                vq = b.op("vector", "tensor_copy", out=qrope[64:128, :],
                          in_=P0[64:128, 0:128])
                b.w("tensor", "vv", vq)
                for i in range(5):
                    kb = qt + i
                    pb = P[1 + (i % 2)]
                    b.w("tensor", "ss", b.now("ss"))
                    b.w("tensor", "vv", b.now("vv"))
                    tsc = b.op("tensor", "matmul", out=pb[:, 0:128],
                               lhsT=kT[:, kb, :], rhs=qrope, start=True, stop=True)
                    if i == 0:
                        mi = 2 if qt < 4 else 1
                    elif i == 4:
                        mi = 0
                    elif qt + i < 4:
                        mi = 3
                    else:
                        mi = None
                    if mi is not None:
                        b.w("vector", "tms", tsc)
                        vmk = b.op("vector", "tensor_add", out=pb[:, 0:128],
                                   in0=pb[:, 0:128], in1=maskC[:, mi, :])
                        b.w("scalar", "vv", vmk)
                    else:
                        b.w("scalar", "tms", tsc)
                    b.w("scalar", "tms", tse)
                    sxp = b.op("scalar", "activation", out=exps[:, i, :],
                               in_=pb[:, 0:128], func=AF.Exp, scale=SCALE)
                    b.w("tensor", "ss", sxp)
                    b.op("tensor", "matmul", out=P3[0:1, 0:128], lhsT=onesc,
                         rhs=exps[:, i, :], start=(i == 0), stop=(i == 4), sig=None)
                    b.op("tensor", "matmul", out=P5[:, 0:128], lhsT=vtm[:, kb, :],
                         rhs=exps[:, i, :], start=(i == 0), stop=(i == 4),
                         sig=(None if i < 4 else "auto"))
                tse = b.now("tms")
                b.w("vector", "tms", tse)
                vrc = b.op("vector", "reciprocal", out=rsb, in_=P3[0:1, 0:128])
                b.w("tensor", "vv", vrc)
                tbc = b.op("tensor", "matmul", out=P4[:, 0:128], lhsT=onesrS[:],
                           rhs=rsb, start=True, stop=True)
                b.w("vector", "tms", tbc)
                b.op("vector", "tensor_copy", out=rbc2, in_=P4[:, 0:128])
                att_last = b.op("vector", "tensor_mul", out=attb[:, h, :],
                                in0=P5[:, 0:128], in1=rbc2)
            for dh in range(2):
                lastd = None
                for di in range(8):
                    d = dh * 8 + di
                    ki = ws.load(wot.ap()[d, :, :])
                    buf = ws.use(ki)
                    if di == 0:
                        b.w("tensor", "vv", att_last)
                        b.w("tensor", "gg", b.now("gg"))
                        b.w("tensor", "vv", b.now("vv"))
                    for hc in range(NH):
                        lastd = b.op("tensor", "matmul", out=P[di][:, 0:128],
                                     lhsT=buf[:, hc * 128:(hc + 1) * 128],
                                     rhs=attb[:, hc, :],
                                     start=(hc == 0), stop=(hc == NH - 1),
                                     sig=(None if hc < NH - 1 else "auto"))
                    ws.release(ki, "tms", lastd)
                for di in range(8):
                    d = dh * 8 + di
                    b.w("vector", "tms", lastd)
                    b.op("vector", "scalar_tensor_tensor", out=resT[d][:, tokq],
                         in0=P[di][:, 0:128], scalar=bo[:, d:d + 1],
                         in1=resT[d][:, tokq], op0=AL.add, op1=AL.add)
        b.barrier()
        if "x3" in taps:
            tv = None
            for c in range(DC):
                tv = b.dma("gpsimd", taps["x3"].ap().rearrange(
                    "(c p) n -> p c n", p=128)[:, c, :], resT[c])
            b.w("gpsimd", "dg", tv)
            b.barrier()

        layernorm(resT, xnT)
        b.barrier()

        abuf = aux(0, D)
        thb = aux(D, 128)
        qsl = aux(D + 128, D)
        store_t = None
        for dt in range(DC):
            ki = ws.load(wltt.ap()[dt, :, :])
            buf = ws.use(ki, eng="scalar")
            b.w("vector", "dw", ws.loaded[ki])
            b.w("gpsimd", "dw", ws.loaded[ki])
            b.w("scalar", "tms", b.now("tms"))
            b.op("scalar", "activation", out=bigA[:, DC, :], in_=buf[:, 0:1024],
                 func=AF.Abs)
            sab = b.op("scalar", "activation", out=bigA[:, DC + 1, :],
                       in_=buf[:, 1024:2048], func=AF.Abs)
            b.w("tensor", "ss", sab)
            pb = 6 if dt % 2 == 0 else 4
            b.w("tensor", "vv", b.now("vv"))
            b.w("tensor", "gg", b.now("gg"))
            for c in range(DC):
                b.op("tensor", "matmul", out=P[pb][0:1, 0:128], lhsT=onesc,
                     rhs=abuf[:, c * 128:(c + 1) * 128],
                     start=(c == 0), stop=(c == DC - 1),
                     sig=(None if c < DC - 1 else "auto"))
            tsa = b.now("tms")
            if dt == 0 and "adbg" in taps:
                b.w("gpsimd", "ss", sab)
                b.w("gpsimd", "dw", ws.loaded[ki])
                ta1 = b.dma("gpsimd", taps["adbg"].ap(), abuf)
                ta2 = b.dma("gpsimd", taps["bdbg"].ap(), buf[:])
                b.w("gpsimd", "dg", ta2)
                b.w("vector", "dg", ta2)
            b.w("vector", "tms", tsa)
            vth = b.op("vector", "tensor_scalar_mul", out=rsb,
                       in0=P[pb][0:1, 0:128], scalar1=0.7 / D)
            b.w("tensor", "vv", vth)
            tbc = b.op("tensor", "matmul", out=P[pb + 1][:, 0:128], lhsT=onesrS[:],
                       rhs=rsb, start=True, stop=True)
            b.w("vector", "tms", tbc)
            vthb = b.op("vector", "tensor_copy", out=thb, in_=P[pb + 1][:, 0:128])
            thbb = thb.rearrange("p (a m) -> p a m", a=1).broadcast_to([128, DC, 128])
            bufv = buf[:].rearrange("p (a m) -> p a m", a=DC)
            qslv = qsl.rearrange("p (a m) -> p a m", a=DC)
            abv = abuf.rearrange("p (a m) -> p a m", a=DC)
            b.w("vector", "ss", sab)
            g1 = b.op("vector", "tensor_tensor", out=abv, in0=bufv, in1=thbb,
                      op=AL.add)
            b.op("vector", "tensor_tensor", out=qslv, in0=bufv, in1=thbb, op=AL.is_gt)
            b.op("vector", "tensor_single_scalar", out=abuf, in_=abuf, scalar=0.0,
                 op=AL.is_lt)
            vq = b.op("vector", "tensor_sub", out=qsl, in0=qsl, in1=abuf)
            ws.release(ki, "vv", vq)
            if dt == 0 and "qdbg" in taps:
                b.w("gpsimd", "vv", vq)
                tq1 = b.dma("gpsimd", taps["qdbg"].ap(), qsl)
                tq2 = b.dma("gpsimd", taps["thdbg"].ap(), thb)
                b.w("gpsimd", "dg", tq2)
            b.w("tensor", "vv", vq)
            for hf in range(2):
                for c in range(DC):
                    b.op("tensor", "matmul", out=P[dt % 2][:, :],
                         lhsT=qsl[:, c * 128:(c + 1) * 128],
                         rhs=xnT[c][:, hf * 512:(hf + 1) * 512],
                         start=(c == 0), stop=(c == DC - 1),
                         sig=(None if c < DC - 1 else "auto"))
                lastq = b.now("tms")
                b.w("vector", "tms", lastq)
                epd = b.op("vector", "scalar_tensor_tensor",
                           out=resT[dt][:, hf * 512:(hf + 1) * 512],
                           in0=P[dt % 2][:, :], scalar=lb[:, dt:dt + 1],
                           in1=resT[dt][:, hf * 512:(hf + 1) * 512],
                           op0=AL.add, op1=AL.add)
                b.w("tensor", "vv", epd)
            b.w("gpsimd", "vv", b.now("vv"))
            b.w("gpsimd", "gg", b.now("gg"))
            store_t = b.dma("gpsimd",
                            y.ap().rearrange("(c p) n -> p c n", p=128)[:, dt, :],
                            resT[dt])
        b.w("gpsimd", "dg", store_t)

        @block.sync
        def _(sync):
            for fn in b.st["sync"]:
                fn(sync)

        @block.tensor
        def _(tensor):
            for fn in b.st["tensor"]:
                fn(tensor)

        @block.vector
        def _(vector):
            for fn in b.st["vector"]:
                fn(vector)

        @block.scalar
        def _(scalar):
            for fn in b.st["scalar"]:
                fn(scalar)

        @block.gpsimd
        def _(gpsimd):
            for fn in b.st["gpsimd"]:
                fn(gpsimd)

    return nc



def _tile_w_sD(W):
    S_, D_ = W.shape
    return np.ascontiguousarray(
        W.reshape(S_ // 128, 128, D_ // 128, 128).transpose(0, 3, 2, 1)
    ).reshape(S_ // 128, 128, D_)


def _tile_lhsT(Wt):
    K, M = Wt.shape
    return np.ascontiguousarray(
        Wt.reshape(K // 128, 128, M // 128, 128).transpose(2, 1, 0, 3)
    ).reshape(M // 128, 128, (K // 128) * 128)


def _col16(v):
    return np.ascontiguousarray(np.asarray(v, dtype=np.float32).reshape(16, 128).T)


def make_inputs(I):
    x = np.asarray(I["L0_attn_x"], dtype=np.float32)

    wl0 = np.ascontiguousarray(np.stack([
        _tile_w_sD(np.asarray(I["L0_attn_Wga_w"], dtype=np.float32)),
        _tile_w_sD(np.asarray(I["L0_attn_Wgb_w"], dtype=np.float32)),
        _tile_w_sD(np.asarray(I["L0_attn_Wf_w"], dtype=np.float32)),
        _tile_w_sD(np.asarray(I["L0_attn_Wzg_w"], dtype=np.float32)),
        _tile_w_sD(np.asarray(I["L0_attn_Wz_w"], dtype=np.float32)),
    ]))
    woutt = np.ascontiguousarray(
        np.asarray(I["L0_attn_Wout_w"], dtype=np.float32)
        .reshape(16, 128, 16, 128).transpose(2, 3, 0, 1)).reshape(16, 128, 2048)
    wrout = np.ascontiguousarray(
        np.asarray(I["L0_ffn_w_router"], dtype=np.float32)
        .reshape(16, 128, 16).transpose(1, 0, 2)).reshape(128, 256)
    we1t = np.ascontiguousarray(
        np.asarray(I["L0_ffn_we1"], dtype=np.float32)
        .reshape(16, 16, 128, 8, 128).transpose(0, 3, 2, 1, 4)).reshape(16, 8, 128, 2048)
    we3t = np.ascontiguousarray(
        np.asarray(I["L0_ffn_we3"], dtype=np.float32)
        .reshape(16, 16, 128, 8, 128).transpose(0, 3, 2, 1, 4)).reshape(16, 8, 128, 2048)
    we2t = np.ascontiguousarray(
        np.asarray(I["L0_ffn_we2"], dtype=np.float32)
        .reshape(16, 8, 128, 16, 128).transpose(0, 3, 2, 1, 4)).reshape(16, 16, 128, 1024)
    ws1t = _tile_lhsT(np.asarray(I["L0_ffn_ws1"], dtype=np.float32))
    ws3t = _tile_lhsT(np.asarray(I["L0_ffn_ws3"], dtype=np.float32))
    ws2t = np.ascontiguousarray(
        np.asarray(I["L0_ffn_ws2"], dtype=np.float32)
        .reshape(32, 128, 2, 8, 128).transpose(2, 0, 1, 3, 4)).reshape(2, 32, 128, 1024)
    wqt = _tile_lhsT(np.asarray(I["L1_attn_Wq"], dtype=np.float32))
    wkt = np.ascontiguousarray(
        np.asarray(I["L1_attn_Wk"], dtype=np.float32)
        .reshape(16, 128, 128).transpose(1, 0, 2)).reshape(128, 2048)
    wvt = np.ascontiguousarray(
        np.asarray(I["L1_attn_Wv"], dtype=np.float32)
        .reshape(16, 128, 128).transpose(1, 0, 2)).reshape(128, 2048)
    wot = _tile_lhsT(np.asarray(I["L1_attn_Wo"], dtype=np.float32))
    wltt = np.ascontiguousarray(
        np.asarray(I["L1_ffn_weight"], dtype=np.float32)
        .reshape(16, 128, 16, 128).transpose(0, 3, 2, 1)).reshape(16, 128, 2048)

    ident = np.eye(128, dtype=np.float32)
    onesr = np.ones((1, 128), dtype=np.float32)

    a = np.arange(128)
    mdiag = np.where(a[None, :] >= a[:, None], 0.0, NEG).astype(np.float32)
    mfar = np.where(a[None, :] < a[:, None], 0.0, NEG).astype(np.float32)
    mneg = np.full((128, 128), NEG, dtype=np.float32)
    mzero = np.zeros((128, 128), dtype=np.float32)

    shared = dict(wl0=wl0, woutt=woutt, wrout=wrout, we1t=we1t, we3t=we3t,
                  we2t=we2t, ws1t=ws1t, ws3t=ws3t, ws2t=ws2t, wqt=wqt, wkt=wkt,
                  wvt=wvt, wot=wot, wltt=wltt, identm=ident, onesr=onesr)

    cbase = np.zeros((128, 96), dtype=np.float32)
    cbase[:, 0:16] = _col16(I["L0_attn_rms_w"])
    cbase[:, 16:32] = _col16(I["L0_attn_Wf_b"])
    cbase[:, 32:48] = _col16(I["L0_attn_Wout_b"])
    cbase[:, 48:64] = _col16(I["L1_attn_bo"])
    cbase[:, 64:80] = _col16(I["L1_ffn_bias"])
    cbase[:, 90] = 1.0
    cbase[:, 91] = 1e-5
    cbase[:, 92] = SCALE

    in_maps = []
    for c in range(8):
        bq, q = c // 4, c % 4
        x0tc = np.ascontiguousarray(x[bq, q * 1024:(q + 1) * 1024, :].T)
        cp = cbase.copy()
        for j in range(3):
            cp[:, 80 + j] = 1.0 if q > j else 0.0
            cp[:, 83 + j] = 0.0 if q > j else 1.0
        for r in range(4):
            cp[:, 86 + r] = 1.0 if (q > 0 and r == q - 1) else 0.0
        pos = np.arange(1024, dtype=np.float32) + np.float32(q * 1024)
        invf = np.exp(-np.log(np.float32(10000.0)) *
                      (np.arange(0, 64, 2, dtype=np.float32) /
                       np.float32(64.0))).astype(np.float32)
        theta = (pos[None, :] * invf[:, None]).astype(np.float32)
        ropetc = np.ascontiguousarray(
            np.concatenate([np.cos(theta), np.sin(theta)], axis=0).astype(np.float32))
        masktc = np.ascontiguousarray(np.stack(
            [mdiag, mfar, (mfar if q > 0 else mneg), (mzero if q > 0 else mneg)],
            axis=1))
        m = dict(shared)
        m.update(x0t=x0tc, cpack=cp, ropet=ropetc, maskt=masktc)
        in_maps.append(m)
    return in_maps


def kernel(**inputs):
    in_maps = make_inputs(inputs)
    nc = build()
    res = run_bass_kernel_spmd(nc, in_maps, core_ids=list(range(8)))
    out = np.empty((2, 4096, 2048), dtype=np.float32)
    for c in range(8):
        bq, q = c // 4, c % 4
        out[bq, q * 1024:(q + 1) * 1024, :] = res.results[c]["y"].T
    return out


# version 02
# speedup vs baseline: 1.1038x; 1.1038x over previous
import numpy as np
import concourse.bass as bass
import concourse.mybir as mybir
from concourse.bass_utils import run_bass_kernel_spmd

F32 = mybir.dt.float32
BF16 = mybir.dt.bfloat16
AF = mybir.ActivationFunctionType
AL = mybir.AluOpType
AX = mybir.AxisListType

D = 2048
DC = 16
TL = 1024
SC = 16
HMC = 8
E = 16
HSC = 32
HD = 128
NH = 16
NEG = -1.0e9
SCALE = float(HD) ** -0.5

ENGS = ("sync", "tensor", "vector", "scalar", "gpsimd")
ESEM = {"tensor": "tms", "vector": "vv", "scalar": "ss", "gpsimd": "gg"}
SERIAL = ("vector", "scalar", "gpsimd")


class B:
    def __init__(self, nc, sems):
        self.nc = nc
        self.st = {e: [] for e in ENGS}
        self.sems = sems
        self.cnt = {k: 0 for k in sems}

    def w(self, eng, sem, val):
        if val is None or val <= 0:
            return
        s = self.sems[sem]
        self.st[eng].append(lambda EE, s=s, v=val: EE.wait_ge(s, v))

    def op(self, eng, meth, *a, sig="auto", sigamt=1, serial=True, **kw):
        if eng in SERIAL and serial:
            self.w(eng, ESEM[eng], self.cnt[ESEM[eng]])
        if sig == "auto":
            sig = ESEM.get(eng)
        tgt, s = None, None
        if sig is not None:
            self.cnt[sig] += sigamt
            tgt = self.cnt[sig]
            s = self.sems[sig]

        import traceback
        tb = traceback.extract_stack()[-3]

        def f(EE, meth=meth, a=a, kw=kw, s=s, sigamt=sigamt, tb=tb):
            try:
                inst = getattr(EE, meth)(*a, **kw)
            except Exception:
                print(f"EMIT-FAIL {meth} from {tb.filename}:{tb.lineno}")
                raise
            if s is not None:
                inst.then_inc(s, sigamt)

        self.st[eng].append(f)
        return tgt

    def dma(self, eng, out, in_):
        sem = "dw" if eng == "sync" else "dg"
        self.cnt[sem] += 16
        tgt = self.cnt[sem]
        s = self.sems[sem]

        def f(EE, out=out, in_=in_, s=s):
            EE.dma_start(out=out, in_=in_).then_inc(s, 16)

        self.st[eng].append(f)
        return tgt

    def barrier(self):
        snap = dict(self.cnt)
        for e in ENGS:
            for sem, val in snap.items():
                if sem != "cc":
                    self.w(e, sem, val)

    def now(self, sem):
        return self.cnt[sem]


class WS:

    def __init__(self, b, bufs):
        self.b = b
        self.bufs = bufs
        self.n = len(bufs)
        self.loaded = []
        self.released = []

    def load(self, src_ap):
        i = len(self.loaded)
        if i >= self.n:
            j = i - self.n
            assert j < len(self.released), f"WS: load {i} before release {j}"
            self.b.w("sync", *self.released[j])
        ncols = 1
        for d in src_ap.shape[1:]:
            ncols *= d
        self.loaded.append(
            self.b.dma("sync", self.bufs[i % self.n][:, 0:ncols], src_ap))
        return i

    def use(self, i, eng="tensor"):
        self.b.w(eng, "dw", self.loaded[i])
        return self.bufs[i % self.n]

    def release(self, i, sem, val):
        assert i == len(self.released), "WS: release out of order"
        self.released.append((sem, val))


def build(tap=()):
    nc = bass.Bass("TRN2", target_bir_lowering=False)

    def dram_in(name, shape):
        return nc.dram_tensor(name, list(shape), F32, kind="ExternalInput")

    x0t = dram_in("x0t", (D, TL))
    wl0 = dram_in("wl0", (5, SC, 128, D))
    woutt = dram_in("woutt", (SC, 128, D))
    wrout = dram_in("wrout", (128, DC * 16))
    we1t = dram_in("we1t", (E, HMC, 128, D))
    we3t = dram_in("we3t", (E, HMC, 128, D))
    we2t = dram_in("we2t", (E, DC, 128, HMC * 128))
    ws1t = dram_in("ws1t", (HSC, 128, D))
    ws3t = dram_in("ws3t", (HSC, 128, D))
    ws2t = dram_in("ws2t", (2, HSC, 128, 8 * 128))
    wqt = dram_in("wqt", (NH, 128, D))
    wkt = dram_in("wkt", (128, D))
    wvt = dram_in("wvt", (128, D))
    wot = dram_in("wot", (DC, 128, D))
    wltt = dram_in("wltt", (DC, 128, D))
    cpack = dram_in("cpack", (128, 96))
    ident = dram_in("identm", (128, 128))
    onesr = dram_in("onesr", (1, 128))
    ropet = dram_in("ropet", (64, TL))
    maskt = dram_in("maskt", (128, 4, 128))

    y = nc.dram_tensor("y", [D, TL], F32, kind="ExternalOutput")
    taps = {t: nc.dram_tensor(t, [D, TL], F32, kind="ExternalOutput")
            for t in tap if not t.startswith("q")}
    if "qdbg" in tap:
        taps["qdbg"] = nc.dram_tensor("qdbg", [128, 2048], F32, kind="ExternalOutput")
        taps["thdbg"] = nc.dram_tensor("thdbg", [128, 128], F32, kind="ExternalOutput")
        taps["adbg"] = nc.dram_tensor("adbg", [128, 2048], F32, kind="ExternalOutput")
        taps["bdbg"] = nc.dram_tensor("bdbg", [128, 2048], F32, kind="ExternalOutput")

    cc1i = nc.dram_tensor("cc1i", [128, 32], F32)
    cc1o = nc.dram_tensor("cc1o", [512, 32], F32)
    cc2i = nc.dram_tensor("cc2i", [8, 128, 128], F32)
    cc2o = nc.dram_tensor("cc2o", [32, 128, 128], F32)

    C_RMSW, C_WFB, C_WOUTB, C_BO, C_LB = 0, 16, 32, 48, 64
    C_SCM, C_SCM1, C_HSEL, C_ONE = 80, 83, 86, 90

    import contextlib
    es = contextlib.ExitStack()
    with es:
        resA = es.enter_context(nc.sbuf_tensor("resA", [128, DC, TL], F32))
        bigA = es.enter_context(nc.sbuf_tensor("bigA", [128, 24, TL], F32))
        pool = es.enter_context(nc.sbuf_tensor("pool", [128, 5, TL], F32))
        wsb0 = es.enter_context(nc.sbuf_tensor("wsb0", [128, D], F32))
        wsb1 = es.enter_context(nc.sbuf_tensor("wsb1", [128, D], F32))
        rowA = es.enter_context(nc.sbuf_tensor("rowA", [1, TL], F32))
        rowB = es.enter_context(nc.sbuf_tensor("rowB", [1, TL], F32))
        smisc = es.enter_context(nc.sbuf_tensor("smisc", [128, 112], F32))
        cpk = es.enter_context(nc.sbuf_tensor("cpk", [128, 96], F32))
        identS = es.enter_context(nc.sbuf_tensor("identS", [128, 128], F32))
        onesrS = es.enter_context(nc.sbuf_tensor("onesrS", [1, 128], F32))
        rtmp = es.enter_context(nc.sbuf_tensor("rtmp", [64, 128], F32))
        wrS = es.enter_context(nc.sbuf_tensor("wrS", [128, DC * 16], F32))
        cwS = es.enter_context(nc.sbuf_tensor("cwS", [128, 8, 16], F32))
        P0 = es.enter_context(nc.psum_tensor("P0", [128, 512], F32))
        P1 = es.enter_context(nc.psum_tensor("P1", [128, 512], F32))
        P2 = es.enter_context(nc.psum_tensor("P2", [128, 512], F32))
        P3 = es.enter_context(nc.psum_tensor("P3", [128, 512], F32))
        P4 = es.enter_context(nc.psum_tensor("P4", [128, 512], F32))
        P5 = es.enter_context(nc.psum_tensor("P5", [128, 512], F32))
        P6 = es.enter_context(nc.psum_tensor("P6", [128, 512], F32))
        P7 = es.enter_context(nc.psum_tensor("P7", [128, 512], F32))
        s_dw = es.enter_context(nc.semaphore("dw"))
        s_dg = es.enter_context(nc.semaphore("dg"))
        s_tms = es.enter_context(nc.semaphore("tms"))
        s_vv = es.enter_context(nc.semaphore("vv"))
        s_ss = es.enter_context(nc.semaphore("ss"))
        s_gg = es.enter_context(nc.semaphore("gg"))
        s_cc = es.enter_context(nc.semaphore("cc"))
        block = es.enter_context(nc.Block())

        P = [P0, P1, P2, P3, P4, P5, P6, P7]
        sems = dict(dw=s_dw, dg=s_dg, tms=s_tms, vv=s_vv, ss=s_ss, gg=s_gg, cc=s_cc)
        b = B(nc, sems)
        ws = WS(b, [wsb0, wsb1])

        bf3 = bigA.ap().rearrange("p a t -> p (a t)").bitcast(BF16) \
            .rearrange("p (a t) -> p a t", a=48)
        ogT = bf3[:, 0:SC, :]
        f16 = bf3[:, SC:2 * SC, :]
        z16 = bf3[:, 2 * SC:3 * SC, :]
        xnB = bigA[:, 0:DC, :]
        auxB = bigA[:, DC:24, :].rearrange("p a t -> p (a t)")

        def aux(off, n):
            return auxB[:, off:off + n]

        sq_t = pool[:, 0, :]
        mbc = pool[:, 1, :]
        rbc = pool[:, 2, :]
        srmsbc = pool[:, 3, :]
        x0r = pool[:, 4, :]
        stg = pool[:, 0, :]
        eptm = pool[:, 1, :]
        scanb = pool[:, 2, :].rearrange("p t -> p t").bitcast(BF16)
        gsmA = pool[:, 4, :]
        cwT = pool[0:16, 3, :]
        cwbc = pool[:, 4, :]
        ropeS = pool[0:64, 3, :]
        maskC = pool[:, 4, 0:512].rearrange("p (a m) -> p a m", a=4)
        gsm = pool[:, 4, 512:1024]

        rmsw = cpk[:, C_RMSW:C_RMSW + 16]
        wfb = cpk[:, C_WFB:C_WFB + 16]
        woutb = cpk[:, C_WOUTB:C_WOUTB + 16]
        bo = cpk[:, C_BO:C_BO + 16]
        lb = cpk[:, C_LB:C_LB + 16]
        scm = cpk[:, C_SCM:C_SCM + 3]
        scm1 = cpk[:, C_SCM1:C_SCM1 + 3]
        hsel = cpk[:, C_HSEL:C_HSEL + 4]
        onesc = cpk[:, C_ONE:C_ONE + 1]
        nc.const_aps.register(F32, 1e-5, cpk[:, 91:92])
        nc.const_aps.register(F32, SCALE, cpk[:, 92:93])
        invw = smisc[:, 0:16]
        hsum = smisc[:, 16:32]
        psum_ = smisc[:, 32:48]
        carry = smisc[:, 48:64]
        ctmp1 = smisc[:, 64:80]
        ctmp2 = smisc[:, 80:96]
        ctmp3 = smisc[:, 96:112]

        resT = [resA[:, c, :] for c in range(DC)]
        xnT = [xnB[:, c, :] for c in range(DC)]

        t_x0 = b.dma("sync", resA[:], x0t.ap().rearrange("(c p) n -> p c n", p=128))
        t_cp = b.dma("gpsimd", cpk[:], cpack.ap())
        t_id = b.dma("gpsimd", identS[:], ident.ap())
        t_or = b.dma("gpsimd", onesrS[:], onesr.ap())
        t_wr = b.dma("gpsimd", wrS[:], wrout.ap())
        b.w("vector", "dg", t_wr)
        b.op("vector", "reciprocal", out=invw, in_=rmsw)
        for e in ENGS:
            b.w(e, "dw", t_x0)
            b.w(e, "dg", t_wr)

        def layernorm(src, dst):
            lastsq = None
            for c in range(DC):
                if lastsq is not None:
                    b.w("scalar", "tms", lastsq)
                sst = b.op("scalar", "activation", out=sq_t, in_=src[c], func=AF.Square)
                for hf in range(2):
                    b.op("tensor", "matmul", out=P[4 + hf][0:1, :], lhsT=onesc,
                         rhs=src[c][:, hf * 512:(hf + 1) * 512],
                         start=(c == 0), stop=(c == DC - 1), sig=None)
                b.w("tensor", "ss", sst)
                for hf in range(2):
                    lastsq = b.op("tensor", "matmul", out=P[6 + hf][0:1, :], lhsT=onesc,
                                  rhs=sq_t[:, hf * 512:(hf + 1) * 512],
                                  start=(c == 0), stop=(c == DC - 1))
            b.w("vector", "tms", lastsq)
            for hf in range(2):
                b.op("vector", "tensor_copy", out=rowA[0:1, hf * 512:(hf + 1) * 512],
                     in_=P[4 + hf][0:1, :])
                b.op("vector", "tensor_copy", out=rowB[0:1, hf * 512:(hf + 1) * 512],
                     in_=P[6 + hf][0:1, :])
            b.op("vector", "tensor_scalar_mul", out=rowA[0:1, :], in0=rowA[0:1, :],
                 scalar1=1.0 / D)
            b.op("vector", "tensor_scalar_mul", out=rowB[0:1, :], in0=rowB[0:1, :],
                 scalar1=1.0 / D)
            b.op("vector", "tensor_mul", out=sq_t[0:1, :], in0=rowA[0:1, :],
                 in1=rowA[0:1, :])
            v1 = b.op("vector", "tensor_sub", out=rowB[0:1, :], in0=rowB[0:1, :],
                      in1=sq_t[0:1, :])
            b.w("scalar", "vv", v1)
            s1 = b.op("scalar", "activation", out=rowB[0:1, :], in_=rowB[0:1, :],
                      func=AF.Sqrt, bias=1e-5)
            b.w("vector", "ss", s1)
            v2 = b.op("vector", "reciprocal", out=rowB[0:1, :], in_=rowB[0:1, :])
            b.w("tensor", "vv", v2)
            for hf in range(2):
                b.op("tensor", "matmul", out=P[4 + hf][:, :], lhsT=onesrS[:],
                     rhs=rowA[0:1, hf * 512:(hf + 1) * 512], start=True, stop=True,
                     sig=None)
                tb_ = b.op("tensor", "matmul", out=P[6 + hf][:, :], lhsT=onesrS[:],
                           rhs=rowB[0:1, hf * 512:(hf + 1) * 512], start=True,
                           stop=True)
            b.w("vector", "tms", tb_)
            for hf in range(2):
                b.op("vector", "tensor_copy", out=mbc[:, hf * 512:(hf + 1) * 512],
                     in_=P[4 + hf][:, :])
                vbc = b.op("vector", "tensor_copy", out=rbc[:, hf * 512:(hf + 1) * 512],
                           in_=P[6 + hf][:, :])
            b.w("gpsimd", "vv", vbc)
            for c in range(DC):
                eng = "vector" if c % 2 == 0 else "gpsimd"
                b.op(eng, "tensor_sub", out=dst[c], in0=src[c], in1=mbc)
                b.op(eng, "tensor_mul", out=dst[c], in0=dst[c], in1=rbc)
            return b.now("vv"), b.now("gg")

        vA, gA = layernorm(resT, resT)

        b.w("scalar", "vv", vA)
        b.w("scalar", "gg", gA)
        b.w("tensor", "vv", vA)
        b.w("tensor", "gg", gA)
        lastsq = None
        for c in range(DC):
            if lastsq is not None:
                b.w("scalar", "tms", lastsq)
            sst = b.op("scalar", "activation", out=sq_t, in_=resT[c], func=AF.Square)
            b.w("tensor", "ss", sst)
            for hf in range(2):
                lastsq = b.op("tensor", "matmul", out=P[6 + hf][0:1, :], lhsT=onesc,
                              rhs=sq_t[:, hf * 512:(hf + 1) * 512],
                              start=(c == 0), stop=(c == DC - 1))
        b.w("vector", "tms", lastsq)
        for hf in range(2):
            b.op("vector", "tensor_copy", out=rowB[0:1, hf * 512:(hf + 1) * 512],
                 in_=P[6 + hf][0:1, :])
        v1 = b.op("vector", "tensor_scalar_mul", out=rowB[0:1, :], in0=rowB[0:1, :],
                  scalar1=1.0 / D)
        b.w("scalar", "vv", v1)
        s1 = b.op("scalar", "activation", out=rowA[0:1, :], in_=rowB[0:1, :],
                  func=AF.Sqrt, bias=1e-5)
        b.w("vector", "ss", s1)
        v2 = b.op("vector", "reciprocal", out=rowB[0:1, :], in_=rowA[0:1, :])
        b.w("tensor", "vv", v2)
        for hf in range(2):
            b.op("tensor", "matmul", out=P[4 + hf][:, :], lhsT=onesrS[:],
                 rhs=rowA[0:1, hf * 512:(hf + 1) * 512], start=True, stop=True,
                 sig=None)
            tb_ = b.op("tensor", "matmul", out=P[6 + hf][:, :], lhsT=onesrS[:],
                       rhs=rowB[0:1, hf * 512:(hf + 1) * 512], start=True, stop=True)
        b.w("vector", "tms", tb_)
        for hf in range(2):
            b.op("vector", "tensor_copy", out=srmsbc[:, hf * 512:(hf + 1) * 512],
                 in_=P[4 + hf][:, :])
            vbc = b.op("vector", "tensor_copy", out=rbc[:, hf * 512:(hf + 1) * 512],
                       in_=P[6 + hf][:, :])
        for c in range(DC):
            b.op("vector", "scalar_tensor_tensor", out=resT[c], in0=resT[c],
                 scalar=rmsw[:, c:c + 1], in1=rbc, op0=AL.mult, op1=AL.mult)
        xnV, xnG = b.now("vv"), b.now("gg")

        NSL = SC * 5
        bank_free = {i: ("tms", 0) for i in range(6)}
        bank_free[4] = ("vv", vbc)
        bank_free[5] = ("vv", vbc)
        silu_ga = None
        sig_zg = None
        scan_last = 0
        b.w("tensor", "vv", xnV)
        b.w("tensor", "gg", xnG)
        for s in range(SC):
            for p in range(5):
                k = s * 5 + p
                if k == 0:
                    ws.load(wl0.ap()[0, 0, :, :])
                if k + 1 < NSL:
                    p2, s2 = (k + 1) % 5, (k + 1) // 5
                    ws.load(wl0.ap()[p2, s2, :, :])
                buf = ws.use(k)
                for hf in range(2):
                    b.w("tensor", *bank_free[(k % 3) * 2 + hf])
                    for c in range(DC):
                        b.op("tensor", "matmul", out=P[(k % 3) * 2 + hf][:, :],
                             lhsT=buf[:, c * 128:(c + 1) * 128],
                             rhs=resT[c][:, hf * 512:(hf + 1) * 512],
                             start=(c == 0), stop=(c == DC - 1),
                             sig=(None if c < DC - 1 else "auto"))
                tmm = b.now("tms")
                ws.release(k, "tms", tmm)
                bp = (k % 3) * 2
                if p == 0:
                    b.w("scalar", "tms", tmm)
                    b.w("scalar", "vv", b.now("vv"))
                    for hf in range(2):
                        sv = b.op("scalar", "activation",
                                  out=sq_t[:, hf * 512:(hf + 1) * 512],
                                  in_=P[bp + hf][:, :], func=AF.Silu)
                    silu_ga = sv
                    bank_free[bp] = ("ss", sv)
                    bank_free[bp + 1] = ("ss", sv)
                elif p == 1:
                    b.w("vector", "tms", tmm)
                    b.w("vector", "ss", silu_ga)
                    for hf in range(2):
                        vv1 = b.op("vector", "tensor_mul",
                                   out=ogT[:, s, hf * 512:(hf + 1) * 512],
                                   in0=sq_t[:, hf * 512:(hf + 1) * 512],
                                   in1=P[bp + hf][:, :])
                    bank_free[bp] = ("vv", vv1)
                    bank_free[bp + 1] = ("vv", vv1)
                elif p == 2:
                    b.w("scalar", "tms", tmm)
                    for hf in range(2):
                        sv = b.op("scalar", "activation",
                                  out=f16[:, s, hf * 512:(hf + 1) * 512],
                                  in_=P[bp + hf][:, :], func=AF.Sigmoid,
                                  bias=wfb[:, s:s + 1])
                    bank_free[bp] = ("ss", sv)
                    bank_free[bp + 1] = ("ss", sv)
                elif p == 3:
                    b.w("scalar", "tms", tmm)
                    b.w("scalar", "vv", b.now("vv"))
                    for hf in range(2):
                        sv = b.op("scalar", "activation",
                                  out=mbc[:, hf * 512:(hf + 1) * 512],
                                  in_=P[bp + hf][:, :], func=AF.Sigmoid)
                    sig_zg = sv
                    bank_free[bp] = ("ss", sv)
                    bank_free[bp + 1] = ("ss", sv)
                else:
                    b.w("vector", "tms", tmm)
                    b.w("vector", "ss", sig_zg)
                    for hf in range(2):
                        vv1 = b.op("vector", "tensor_mul",
                                   out=z16[:, s, hf * 512:(hf + 1) * 512],
                                   in0=mbc[:, hf * 512:(hf + 1) * 512],
                                   in1=P[bp + hf][:, :])
                    bank_free[bp] = ("vv", vv1)
                    bank_free[bp + 1] = ("vv", vv1)
                    b.op("vector", "tensor_tensor_scan", out=scanb[:, 0:TL],
                         data0=f16[:, s, :], data1=z16[:, s, :], initial=0.0,
                         op0=AL.mult, op1=AL.add)
                    b.op("vector", "tensor_copy", out=hsum[:, s:s + 1],
                         in_=scanb[:, TL - 1:TL])
                    b.op("vector", "tensor_tensor_scan", out=scanb[:, TL:2 * TL],
                         data0=f16[:, s, :], data1=f16[:, s, :], initial=1.0,
                         op0=AL.mult, op1=AL.bypass)
                    scan_last = b.op("vector", "tensor_copy", out=psum_[:, s:s + 1],
                                     in_=scanb[:, 2 * TL - 1:2 * TL])

        b.w("gpsimd", "vv", scan_last)
        tgi = b.dma("gpsimd", cc1i.ap(), smisc[:, 16:48])
        b.w("gpsimd", "dg", tgi)
        ccv = b.op("gpsimd", "collective_compute", "AllGather", AL.bypass,
                   replica_groups=[[0, 1, 2, 3], [4, 5, 6, 7]],
                   ins=[cc1i.ap().opt()], outs=[cc1o.ap().opt()],
                   sig="cc", serial=False)
        b.w("gpsimd", "cc", ccv)
        tg = None
        for r in range(3):
            tg = b.dma("gpsimd", gsmA[:, r * 32:(r + 1) * 32],
                       cc1o.ap().rearrange("(r p) c -> p r c", p=128)[:, r, :])
        b.w("vector", "dg", tg)
        b.op("vector", "tensor_scalar_mul", out=carry, in0=gsmA[:, 0:16],
             scalar1=scm[:, 0:1])
        cdone = None
        for j in (1, 2):
            b.op("vector", "tensor_scalar_mul", out=ctmp1, in0=gsmA[:, j * 32:j * 32 + 16],
                 scalar1=scm[:, j:j + 1])
            b.op("vector", "tensor_scalar", out=ctmp2, in0=gsmA[:, j * 32 + 16:j * 32 + 32],
                 scalar1=scm[:, j:j + 1], scalar2=scm1[:, j:j + 1],
                 op0=AL.mult, op1=AL.add)
            b.op("vector", "tensor_mul", out=carry, in0=carry, in1=ctmp2)
            cdone = b.op("vector", "tensor_add", out=carry, in0=carry, in1=ctmp1)

        gate_done = []
        for s in range(SC):
            b.op("vector", "tensor_tensor_scan", out=z16[:, s, :],
                 data0=f16[:, s, :], data1=z16[:, s, :],
                 initial=carry[:, s:s + 1], op0=AL.mult, op1=AL.add)
            gate_done.append(b.op("vector", "tensor_mul", out=ogT[:, s, :],
                                  in0=ogT[:, s, :], in1=z16[:, s, :]))

        x0v = x0t.ap().rearrange("(c p) n -> p c n", p=128)
        b.w("gpsimd", "vv", cdone)
        ep_v = 0
        st_free = [("tms", 0), ("tms", 0)]
        wk_i = len(ws.loaded)
        for dhalf in range(2):
            for tb in range(2):
                tok = slice(tb * 512, (tb + 1) * 512)
                lastd = None
                for sc in range(SC):
                    kk = wk_i + (dhalf * 2 + tb) * SC + sc
                    if kk == wk_i:
                        ws.load(woutt.ap()[0, :, :])
                    if sc + 1 < SC or dhalf * 2 + tb < 3:
                        nxt = (dhalf * 2 + tb) * SC + sc + 1
                        ws.load(woutt.ap()[nxt % SC, :, :])
                    buf = ws.use(kk)
                    sl = sc % 2
                    b.w("vector", "vv", gate_done[sc])
                    b.w("vector", *st_free[sl])
                    vstg = b.op("vector", "tensor_copy",
                                out=stg[:, sl * 512:(sl + 1) * 512],
                                in_=ogT[:, sc, tok])
                    b.w("tensor", "vv", vstg)
                    if sc == 0:
                        b.w("tensor", "vv", ep_v)
                    for di in range(8):
                        d = dhalf * 8 + di
                        lastd = b.op("tensor", "matmul", out=P[di][:, :],
                                     lhsT=buf[:, d * 128:(d + 1) * 128],
                                     rhs=stg[:, sl * 512:(sl + 1) * 512],
                                     start=(sc == 0), stop=(sc == SC - 1),
                                     sig=(None if di < 7 else "auto"))
                    st_free[sl] = ("tms", lastd)
                    ws.release(kk, "tms", lastd)
                for di in range(8):
                    d = dhalf * 8 + di
                    b.w("gpsimd", "vv", b.now("vv"))
                    tx0 = b.dma("gpsimd", x0r[:, (di % 2) * 512:(di % 2 + 1) * 512],
                                x0v[:, d, tok])
                    g1 = b.op("vector", "scalar_tensor_tensor", out=eptm[:, 0:512],
                              in0=resT[d][:, tok], scalar=invw[:, d:d + 1],
                              in1=srmsbc[:, tok], op0=AL.mult, op1=AL.mult)
                    b.w("vector", "dg", tx0)
                    g2 = b.op("vector", "tensor_add", out=eptm[:, 0:512],
                              in0=eptm[:, 0:512],
                              in1=x0r[:, (di % 2) * 512:(di % 2 + 1) * 512])
                    b.w("vector", "tms", lastd)
                    ep_v = b.op("vector", "scalar_tensor_tensor", out=resT[d][:, tok],
                                in0=P[di][:, :], scalar=woutb[:, d:d + 1],
                                in1=eptm[:, 0:512], op0=AL.add, op1=AL.add)
        b.barrier()
        if "x1" in taps:
            tv = None
            for c in range(DC):
                tv = b.dma("gpsimd", taps["x1"].ap().rearrange(
                    "(c p) n -> p c n", p=128)[:, c, :], resT[c])
            b.w("gpsimd", "dg", tv)
            b.barrier()

        layernorm(resT, xnT)
        b.barrier()

        vcwT = 0
        for tt in range(8):
            pb = P[6 + (tt % 2)]
            b.w("tensor", "vv", b.now("vv"))
            for c in range(DC):
                b.op("tensor", "matmul", out=pb[:, 0:16],
                     lhsT=xnT[c][:, tt * 128:(tt + 1) * 128],
                     rhs=wrS[:, c * 16:(c + 1) * 16],
                     start=(c == 0), stop=(c == DC - 1),
                     sig=(None if c < DC - 1 else "auto"))
            tmm = b.now("tms")
            b.w("vector", "tms", tmm)
            cwt = cwS[:, tt, :]
            mx = ctmp1[:, 0:1]
            b.op("vector", "reduce_max", out=mx, in_=pb[:, 0:16], axis=AX.X,
                 negate=True)
            vv1 = b.op("vector", "tensor_copy", out=cwt, in_=pb[:, 0:16])
            b.w("scalar", "vv", vv1)
            sx = b.op("scalar", "activation", out=cwt, in_=cwt, func=AF.Exp, bias=mx)
            b.w("vector", "ss", sx)
            sm = ctmp1[:, 1:2]
            b.op("vector", "reduce_sum", out=sm, in_=cwt, axis=AX.X)
            b.op("vector", "reciprocal", out=sm, in_=sm)
            b.op("vector", "tensor_scalar_mul", out=cwt, in0=cwt, scalar1=sm)
            cur = ctmp2
            b.op("vector", "tensor_copy", out=cur, in_=cwt)
            th = ctmp1[:, 2:3]
            for it in range(4):
                b.op("vector", "reduce_max", out=th, in_=cur, axis=AX.X)
                if it < 3:
                    b.op("vector", "tensor_single_scalar", out=ctmp3, in_=cur,
                         scalar=th, op=AL.is_ge)
                    b.op("vector", "tensor_scalar_mul", out=ctmp3, in0=ctmp3,
                         scalar1=1e30)
                    b.op("vector", "tensor_sub", out=cur, in0=cur, in1=ctmp3)
            b.op("vector", "tensor_single_scalar", out=cur, in_=cwt, scalar=th,
                 op=AL.is_ge)
            b.op("vector", "tensor_mul", out=cwt, in0=cwt, in1=cur)
            s4 = ctmp1[:, 1:2]
            b.op("vector", "reduce_sum", out=s4, in_=cwt, axis=AX.X)
            b.op("vector", "reciprocal", out=s4, in_=s4)
            vcw = b.op("vector", "tensor_scalar_mul", out=cwt, in0=cwt, scalar1=s4)
            b.w("tensor", "vv", vcw)
            ttr = b.op("tensor", "transpose", out=P[4 + (tt % 2)][0:16, 0:128],
                       in_=cwt, identity=identS[:])
            b.w("vector", "tms", ttr)
            vcwT = b.op("vector", "tensor_copy", out=cwT[:, tt * 128:(tt + 1) * 128],
                        in_=P[4 + (tt % 2)][0:16, 0:128])
        b.barrier()

        gbufF = aux(0, 8 * TL)
        ep_sh = ("vv", 0)
        for tq in range(4):
            tok = slice(tq * 256, (tq + 1) * 256)
            glu_done = [0] * HSC
            for hs in range(HSC):
                k1 = ws.load(ws1t.ap()[hs, :, :])
                buf1 = ws.use(k1)
                b.w("tensor", *ep_sh)
                for c in range(DC):
                    b.op("tensor", "matmul", out=P0[:, 0:256],
                         lhsT=buf1[:, c * 128:(c + 1) * 128], rhs=xnT[c][:, tok],
                         start=(c == 0), stop=(c == DC - 1),
                         sig=(None if c < DC - 1 else "auto"))
                t1 = b.now("tms")
                ws.release(k1, "tms", t1)
                k3 = ws.load(ws3t.ap()[hs, :, :])
                buf3 = ws.use(k3)
                for c in range(DC):
                    b.op("tensor", "matmul", out=P1[:, 0:256],
                         lhsT=buf3[:, c * 128:(c + 1) * 128], rhs=xnT[c][:, tok],
                         start=(c == 0), stop=(c == DC - 1),
                         sig=(None if c < DC - 1 else "auto"))
                t3 = b.now("tms")
                ws.release(k3, "tms", t3)
                b.w("scalar", "tms", t1)
                b.w("scalar", "vv", b.now("vv"))
                ssl = b.op("scalar", "activation", out=sq_t[:, 0:256],
                           in_=P0[:, 0:256], func=AF.Silu)
                b.w("vector", "ss", ssl)
                b.w("vector", "tms", t3)
                glu_done[hs] = b.op("vector", "tensor_mul",
                                    out=gbufF[:, hs * 256:(hs + 1) * 256],
                                    in0=sq_t[:, 0:256], in1=P1[:, 0:256])
                ep_sh = ("vv", glu_done[hs])
            for dh in range(2):
                lastd = None
                for hs in range(HSC):
                    ki = ws.load(ws2t.ap()[dh, hs, :, :])
                    buf = ws.use(ki)
                    b.w("tensor", "vv", glu_done[hs])
                    if hs == 0:
                        b.w("tensor", "vv", b.now("vv"))
                        b.w("tensor", "gg", b.now("gg"))
                    for di in range(8):
                        lastd = b.op("tensor", "matmul", out=P[di][:, 0:256],
                                     lhsT=buf[:, di * 128:(di + 1) * 128],
                                     rhs=gbufF[:, hs * 256:(hs + 1) * 256],
                                     start=(hs == 0), stop=(hs == HSC - 1),
                                     sig=(None if di < 7 else "auto"))
                    ws.release(ki, "tms", lastd)
                for di in range(8):
                    d = dh * 8 + di
                    b.w("vector", "tms", lastd)
                    epv = b.op("vector", "scalar_tensor_tensor", out=resT[d][:, tok],
                               in0=P[di][:, 0:256], scalar=0.2,
                               in1=resT[d][:, tok], op0=AL.mult, op1=AL.add)
                ep_sh = ("vv", b.now("vv"))
                b.w("tensor", "vv", b.now("vv"))
                b.w("tensor", "gg", b.now("gg"))
        b.barrier()

        for e in range(E):
            b.w("tensor", "vv", b.now("vv"))
            b.w("tensor", "gg", b.now("gg"))
            tsel = None
            for hf in range(2):
                tsel = b.op("tensor", "matmul", out=P[6 + hf][0:1, :],
                            lhsT=identS[0:16, e:e + 1],
                            rhs=cwT[:, hf * 512:(hf + 1) * 512],
                            start=True, stop=True)
            b.w("vector", "tms", tsel)
            b.w("vector", "gg", b.now("gg"))
            vsel = None
            for hf in range(2):
                vsel = b.op("vector", "tensor_copy",
                            out=rowA[0:1, hf * 512:(hf + 1) * 512],
                            in_=P[6 + hf][0:1, :])
            b.w("tensor", "vv", vsel)
            tbc = None
            for hf in range(2):
                tbc = b.op("tensor", "matmul", out=P[6 + hf][:, :], lhsT=onesrS[:],
                           rhs=rowA[0:1, hf * 512:(hf + 1) * 512],
                           start=True, stop=True)
            b.w("vector", "tms", tbc)
            vbc = None
            for hf in range(2):
                vbc = b.op("vector", "tensor_copy",
                           out=cwbc[:, hf * 512:(hf + 1) * 512], in_=P[6 + hf][:, :])
            gb_last = 0
            ht_done = {}
            for ht in range(HMC):
                ab_base = 0 if ht % 2 == 0 else 4
                k1 = ws.load(we1t.ap()[e, ht, :, :])
                buf1 = ws.use(k1)
                if ht - 2 in ht_done:
                    b.w("tensor", "ss", ht_done[ht - 2][0])
                    b.w("tensor", "vv", ht_done[ht - 2][1])
                elif ht < 2:
                    b.w("tensor", "ss", b.now("ss"))
                    b.w("tensor", "vv", b.now("vv"))
                for hf in range(2):
                    for c in range(DC):
                        b.op("tensor", "matmul", out=P[ab_base + hf][:, :],
                             lhsT=buf1[:, c * 128:(c + 1) * 128],
                             rhs=xnT[c][:, hf * 512:(hf + 1) * 512],
                             start=(c == 0), stop=(c == DC - 1),
                             sig=(None if c < DC - 1 else "auto"))
                t1 = b.now("tms")
                ws.release(k1, "tms", t1)
                k3 = ws.load(we3t.ap()[e, ht, :, :])
                buf3 = ws.use(k3)
                for hf in range(2):
                    for c in range(DC):
                        b.op("tensor", "matmul", out=P[ab_base + 2 + hf][:, :],
                             lhsT=buf3[:, c * 128:(c + 1) * 128],
                             rhs=xnT[c][:, hf * 512:(hf + 1) * 512],
                             start=(c == 0), stop=(c == DC - 1),
                             sig=(None if c < DC - 1 else "auto"))
                t3 = b.now("tms")
                ws.release(k3, "tms", t3)
                b.w("scalar", "tms", t1)
                b.w("scalar", "vv", b.now("vv"))
                ssl = None
                for hf in range(2):
                    ssl = b.op("scalar", "activation",
                               out=sq_t[:, hf * 512:(hf + 1) * 512],
                               in_=P[ab_base + hf][:, :], func=AF.Silu)
                b.w("vector", "ss", ssl)
                b.w("vector", "tms", t3)
                b.w("vector", "gg", b.now("gg"))
                vg1 = None
                for hf in range(2):
                    vg1 = b.op("vector", "tensor_mul",
                               out=mbc[:, hf * 512:(hf + 1) * 512],
                               in0=sq_t[:, hf * 512:(hf + 1) * 512],
                               in1=P[ab_base + 2 + hf][:, :])
                b.w("gpsimd", "vv", vg1)
                b.w("gpsimd", "tms", b.now("tms"))
                gb_last = b.op("gpsimd", "tensor_mul",
                               out=gbufF[:, ht * TL:(ht + 1) * TL],
                               in0=mbc[:, 0:TL], in1=cwbc[:])
                ht_done[ht] = (ssl, vg1)
            ep_rec = {}
            for dt in range(DC):
                ki = ws.load(we2t.ap()[e, dt, :, :])
                buf = ws.use(ki)
                b.w("tensor", "gg", gb_last)
                if dt - 2 in ep_rec:
                    b.w("tensor", "vv", ep_rec[dt - 2])
                elif dt < 2:
                    b.w("tensor", "vv", b.now("vv"))
                    b.w("tensor", "gg", b.now("gg"))
                    b.w("tensor", "ss", b.now("ss"))
                tmm2 = {}
                for hf in range(2):
                    pb = P[4 + 2 * (dt % 2) + hf - 4 * ((dt % 2) and False)] if False else P[(dt % 2) * 2 + hf + 4]
                    for hc in range(HMC):
                        b.op("tensor", "matmul", out=pb[:, :],
                             lhsT=buf[:, hc * 128:(hc + 1) * 128],
                             rhs=gbufF[:, hc * TL + hf * 512:hc * TL + (hf + 1) * 512],
                             start=(hc == 0), stop=(hc == HMC - 1),
                             sig=(None if hc < HMC - 1 else "auto"))
                    tmm2[hf] = b.now("tms")
                epv = None
                for hf in range(2):
                    pb = P[(dt % 2) * 2 + hf + 4]
                    b.w("vector", "tms", tmm2[hf])
                    epv = b.op("vector", "scalar_tensor_tensor",
                               out=resT[dt][:, hf * 512:(hf + 1) * 512],
                               in0=pb[:, :], scalar=0.8,
                               in1=resT[dt][:, hf * 512:(hf + 1) * 512],
                               op0=AL.mult, op1=AL.add)
                ep_rec[dt] = epv
                ws.release(ki, "tms", tmm2[1])
        b.barrier()
        if "x2" in taps:
            tv = None
            for c in range(DC):
                tv = b.dma("gpsimd", taps["x2"].ap().rearrange(
                    "(c p) n -> p c n", p=128)[:, c, :], resT[c])
            b.w("gpsimd", "dg", tv)
            b.barrier()

        layernorm(resT, xnT)
        b.barrier()

        t_rp = b.dma("gpsimd", ropeS[:], ropet.ap())
        t_mk = b.dma("gpsimd", maskC[:], maskt.ap())
        for e_ in ("tensor", "vector", "scalar"):
            b.w(e_, "dg", t_mk)
        kT = aux(0, 12 * 128).rearrange("p (a m) -> p a m", a=12)
        vtm = aux(12 * 128, 12 * 128).rearrange("p (a m) -> p a m", a=12)
        attb = aux(24 * 128, 16 * 128).rearrange("p (a m) -> p a m", a=16)
        exps = aux(40 * 128, 5 * 128).rearrange("p (a m) -> p a m", a=5)
        qrope = aux(45 * 128, 128)
        rbc2 = aux(46 * 128, 128)
        rsb = rowA[0:1, 0:128]
        cosT = ropeS[0:32, :]
        sinT = ropeS[32:64, :]

        kwi = ws.load(wkt.ap())
        vwi = ws.load(wvt.ap())
        kbuf = ws.use(kwi)
        vbuf = ws.use(vwi)

        def kv_block(blk):
            t0 = (blk - 4) * 128
            tokb = slice(t0, t0 + 128)
            pb = P[blk % 2]
            b.w("tensor", "vv", b.now("vv"))
            for c in range(DC):
                b.op("tensor", "matmul", out=pb[:, 0:128],
                     lhsT=kbuf[:, c * 128:(c + 1) * 128], rhs=xnT[c][:, tokb],
                     start=(c == 0), stop=(c == DC - 1),
                     sig=(None if c < DC - 1 else "auto"))
            tk = b.now("tms")
            b.w("vector", "tms", tk)
            cs = cosT[:, tokb]
            sn = sinT[:, tokb]
            kb_ = kT[:, blk, :]
            b.op("vector", "tensor_mul", out=kb_[0:32, :], in0=pb[0:32, 0:128], in1=cs)
            b.op("vector", "tensor_mul", out=rtmp[0:32, :], in0=pb[32:64, 0:128], in1=sn)
            b.op("vector", "tensor_sub", out=kb_[0:32, :], in0=kb_[0:32, :],
                 in1=rtmp[0:32, :])
            b.op("vector", "tensor_mul", out=kb_[32:64, :], in0=pb[0:32, 0:128], in1=sn)
            b.op("vector", "tensor_mul", out=rtmp[32:64, :], in0=pb[32:64, 0:128], in1=cs)
            b.op("vector", "tensor_add", out=kb_[32:64, :], in0=kb_[32:64, :],
                 in1=rtmp[32:64, :])
            vk = b.op("vector", "tensor_copy", out=kb_[64:128, :],
                      in_=pb[64:128, 0:128])
            pv = P[2 + blk % 2]
            b.w("tensor", "vv", vk)
            for c in range(DC):
                b.op("tensor", "matmul", out=pv[:, 0:128],
                     lhsT=xnT[c][:, tokb], rhs=vbuf[:, c * 128:(c + 1) * 128],
                     start=(c == 0), stop=(c == DC - 1),
                     sig=(None if c < DC - 1 else "auto"))
            tv_ = b.now("tms")
            b.w("vector", "tms", tv_)
            return b.op("vector", "tensor_copy", out=vtm[:, blk, :], in_=pv[:, 0:128])

        for blk in (8, 9, 10, 11):
            kv_tail = kv_block(blk)
        b.w("gpsimd", "vv", kv_tail)
        tg1 = b.dma("gpsimd", cc2i.ap()[0:4, :, :].rearrange("a p m -> p a m"),
                    kT[:, 8:12, :])
        tg2 = b.dma("gpsimd", cc2i.ap()[4:8, :, :].rearrange("a p m -> p a m"),
                    vtm[:, 8:12, :])
        b.w("gpsimd", "dg", tg2)
        ccv2 = b.op("gpsimd", "collective_compute", "AllGather", AL.bypass,
                    replica_groups=[[0, 1, 2, 3], [4, 5, 6, 7]],
                    ins=[cc2i.ap().opt()], outs=[cc2o.ap().opt()],
                    sig="cc", serial=False)
        for blk in (4, 5, 6, 7):
            kv_own = kv_block(blk)
        tmm_kv = b.now("tms")
        ws.release(kwi, "tms", tmm_kv)
        ws.release(vwi, "tms", tmm_kv)
        b.w("gpsimd", "cc", ccv2)
        vprev = kv_own
        for part in range(2):
            base = kT if part == 0 else vtm
            dst = aux(part * 12 * 128, 4 * 128)
            for r in range(4):
                b.w("gpsimd", "vv", vprev)
                tgr = b.dma("gpsimd", gsm[:, :],
                            cc2o.ap()[r * 8 + part * 4:r * 8 + part * 4 + 4, :, :]
                            .rearrange("a p m -> p a m"))
                b.w("vector", "dg", tgr)
                if r == 0:
                    vprev = b.op("vector", "tensor_scalar_mul", out=dst,
                                 in0=gsm[:].rearrange("p (a m) -> p a m", a=4)
                                 .rearrange("p a m -> p (a m)"),
                                 scalar1=hsel[:, 0:1])
                else:
                    vprev = b.op("vector", "scalar_tensor_tensor", out=dst,
                                 in0=gsm[:], scalar=hsel[:, r:r + 1], in1=dst,
                                 op0=AL.mult, op1=AL.add)
        b.barrier()

        qr2 = aux(47 * 128, 128)
        for qt in range(8):
            tokq = slice(qt * 128, (qt + 1) * 128)
            att_last = 0
            tse = 0
            qrb = [qrope, qr2]

            def qproj(h):
                ki = ws.load(wqt.ap()[h, :, :])
                buf = ws.use(ki)
                for c in range(DC):
                    b.op("tensor", "matmul", out=P0[:, 0:128],
                         lhsT=buf[:, c * 128:(c + 1) * 128], rhs=xnT[c][:, tokq],
                         start=(c == 0), stop=(c == DC - 1),
                         sig=(None if c < DC - 1 else "auto"))
                tq_ = b.now("tms")
                ws.release(ki, "tms", tq_)
                qr = qrb[h % 2]
                b.w("vector", "tms", tq_)
                cs = cosT[:, tokq]
                sn = sinT[:, tokq]
                b.op("vector", "tensor_mul", out=qr[0:32, :], in0=P0[0:32, 0:128],
                     in1=cs)
                b.op("vector", "tensor_mul", out=rtmp[0:32, :], in0=P0[32:64, 0:128],
                     in1=sn)
                b.op("vector", "tensor_sub", out=qr[0:32, :], in0=qr[0:32, :],
                     in1=rtmp[0:32, :])
                b.op("vector", "tensor_mul", out=qr[32:64, :], in0=P0[0:32, 0:128],
                     in1=sn)
                b.op("vector", "tensor_mul", out=rtmp[32:64, :], in0=P0[32:64, 0:128],
                     in1=cs)
                b.op("vector", "tensor_add", out=qr[32:64, :], in0=qr[32:64, :],
                     in1=rtmp[32:64, :])
                return b.op("vector", "tensor_copy", out=qr[64:128, :],
                            in_=P0[64:128, 0:128])

            b.w("tensor", "vv", b.now("vv"))
            b.w("tensor", "ss", b.now("ss"))
            vq = qproj(0)
            vq_next = None
            sb = [P1, P2, P6]
            exp_rec = [0] * 5
            for h in range(NH):
                qr = qrb[h % 2]
                b.w("tensor", "vv", vq)
                for i in range(5):
                    kb = qt + i
                    pb = sb[i % 3]
                    b.w("tensor", "ss", exp_rec[[3, 4, 2, 0, 1][i]])
                    tsc = b.op("tensor", "matmul", out=pb[:, 0:128],
                               lhsT=kT[:, kb, :], rhs=qr, start=True, stop=True)
                    if i == 0:
                        mi = 2 if qt < 4 else 1
                    elif i == 4:
                        mi = 0
                    elif qt + i < 4:
                        mi = 3
                    else:
                        mi = None
                    if mi is not None:
                        b.w("vector", "tms", tsc)
                        vmk = b.op("vector", "tensor_add", out=pb[:, 0:128],
                                   in0=pb[:, 0:128], in1=maskC[:, mi, :])
                        b.w("scalar", "vv", vmk)
                    else:
                        b.w("scalar", "tms", tsc)
                    b.w("scalar", "tms", tse)
                    exp_rec[i] = b.op("scalar", "activation", out=exps[:, i, :],
                                      in_=pb[:, 0:128], func=AF.Exp, scale=SCALE)
                if h + 1 < NH:
                    vq_next = qproj(h + 1)
                b.w("tensor", "vv", att_last)
                for i in range(5):
                    kb = qt + i
                    b.w("tensor", "ss", exp_rec[i])
                    b.op("tensor", "matmul", out=P3[0:1, 0:128], lhsT=onesc,
                         rhs=exps[:, i, :], start=(i == 0), stop=(i == 4), sig=None)
                    b.op("tensor", "matmul", out=P5[:, 0:128], lhsT=vtm[:, kb, :],
                         rhs=exps[:, i, :], start=(i == 0), stop=(i == 4),
                         sig=(None if i < 4 else "auto"))
                tse = b.now("tms")
                b.w("vector", "tms", tse)
                vrc = b.op("vector", "reciprocal", out=rsb, in_=P3[0:1, 0:128])
                b.w("tensor", "vv", vrc)
                tbc = b.op("tensor", "matmul", out=P4[:, 0:128], lhsT=onesrS[:],
                           rhs=rsb, start=True, stop=True)
                b.w("vector", "tms", tbc)
                b.op("vector", "tensor_copy", out=rbc2, in_=P4[:, 0:128])
                att_last = b.op("vector", "tensor_mul", out=attb[:, h, :],
                                in0=P5[:, 0:128], in1=rbc2)
                vq = vq_next
            for dh in range(2):
                lastd = None
                for di in range(8):
                    d = dh * 8 + di
                    ki = ws.load(wot.ap()[d, :, :])
                    buf = ws.use(ki)
                    if di == 0:
                        b.w("tensor", "vv", att_last)
                        b.w("tensor", "gg", b.now("gg"))
                        b.w("tensor", "vv", b.now("vv"))
                    for hc in range(NH):
                        lastd = b.op("tensor", "matmul", out=P[di][:, 0:128],
                                     lhsT=buf[:, hc * 128:(hc + 1) * 128],
                                     rhs=attb[:, hc, :],
                                     start=(hc == 0), stop=(hc == NH - 1),
                                     sig=(None if hc < NH - 1 else "auto"))
                    ws.release(ki, "tms", lastd)
                for di in range(8):
                    d = dh * 8 + di
                    b.w("vector", "tms", lastd)
                    b.op("vector", "scalar_tensor_tensor", out=resT[d][:, tokq],
                         in0=P[di][:, 0:128], scalar=bo[:, d:d + 1],
                         in1=resT[d][:, tokq], op0=AL.add, op1=AL.add)
        b.barrier()
        if "x3" in taps:
            tv = None
            for c in range(DC):
                tv = b.dma("gpsimd", taps["x3"].ap().rearrange(
                    "(c p) n -> p c n", p=128)[:, c, :], resT[c])
            b.w("gpsimd", "dg", tv)
            b.barrier()

        layernorm(resT, xnT)
        b.barrier()

        abuf = aux(0, D)
        thb = aux(D, 128)
        qsl2 = [aux(D + 128, D), aux(D + 128 + D, D)]
        store_t = None
        prep_rec = {}

        def prep(dt):
            ki = ws.load(wltt.ap()[dt, :, :])
            buf = ws.use(ki, eng="scalar")
            b.w("vector", "dw", ws.loaded[ki])
            qsl = qsl2[dt % 2]
            if dt - 1 in prep_rec:
                b.w("scalar", "tms", prep_rec[dt - 1][1])
                b.w("scalar", "vv", prep_rec[dt - 1][0])
            else:
                b.w("scalar", "tms", b.now("tms"))
            b.op("scalar", "activation", out=bigA[:, DC, :], in_=buf[:, 0:1024],
                 func=AF.Abs)
            sab = b.op("scalar", "activation", out=bigA[:, DC + 1, :],
                       in_=buf[:, 1024:2048], func=AF.Abs)
            b.w("tensor", "ss", sab)
            pb = 6 if dt % 2 == 0 else 4
            b.w("tensor", "vv", b.now("vv"))
            for c in range(DC):
                b.op("tensor", "matmul", out=P[pb][0:1, 0:128], lhsT=onesc,
                     rhs=abuf[:, c * 128:(c + 1) * 128],
                     start=(c == 0), stop=(c == DC - 1),
                     sig=(None if c < DC - 1 else "auto"))
            tsa = b.now("tms")
            b.w("vector", "tms", tsa)
            vth = b.op("vector", "tensor_scalar_mul", out=rsb,
                       in0=P[pb][0:1, 0:128], scalar1=0.7 / D)
            b.w("tensor", "vv", vth)
            tbc = b.op("tensor", "matmul", out=P[pb + 1][:, 0:128], lhsT=onesrS[:],
                       rhs=rsb, start=True, stop=True)
            b.w("vector", "tms", tbc)
            vthb = b.op("vector", "tensor_copy", out=thb, in_=P[pb + 1][:, 0:128])
            thbb = thb.rearrange("p (a m) -> p a m", a=1).broadcast_to([128, DC, 128])
            bufv = buf[:].rearrange("p (a m) -> p a m", a=DC)
            qslv = qsl.rearrange("p (a m) -> p a m", a=DC)
            abv = abuf.rearrange("p (a m) -> p a m", a=DC)
            b.w("vector", "ss", sab)
            b.op("vector", "tensor_tensor", out=abv, in0=bufv, in1=thbb, op=AL.add)
            b.op("vector", "tensor_tensor", out=qslv, in0=bufv, in1=thbb, op=AL.is_gt)
            b.op("vector", "tensor_single_scalar", out=abuf, in_=abuf, scalar=0.0,
                 op=AL.is_lt)
            vq = b.op("vector", "tensor_sub", out=qsl, in0=qsl, in1=abuf)
            ws.release(ki, "vv", vq)
            prep_rec[dt] = (vq, tsa)
            return vq

        prep(0)
        for dt in range(DC):
            if dt + 1 < DC:
                prep(dt + 1)
            qsl = qsl2[dt % 2]
            b.w("tensor", "vv", prep_rec[dt][0])
            for hf in range(2):
                for c in range(DC):
                    b.op("tensor", "matmul", out=P[dt % 2][:, :],
                         lhsT=qsl[:, c * 128:(c + 1) * 128],
                         rhs=xnT[c][:, hf * 512:(hf + 1) * 512],
                         start=(c == 0), stop=(c == DC - 1),
                         sig=(None if c < DC - 1 else "auto"))
                lastq = b.now("tms")
                b.w("vector", "tms", lastq)
                epd = b.op("vector", "scalar_tensor_tensor",
                           out=resT[dt][:, hf * 512:(hf + 1) * 512],
                           in0=P[dt % 2][:, :], scalar=lb[:, dt:dt + 1],
                           in1=resT[dt][:, hf * 512:(hf + 1) * 512],
                           op0=AL.add, op1=AL.add)
                b.w("tensor", "vv", epd)
            b.w("gpsimd", "vv", b.now("vv"))
            store_t = b.dma("gpsimd",
                            y.ap().rearrange("(c p) n -> p c n", p=128)[:, dt, :],
                            resT[dt])
        b.w("gpsimd", "dg", store_t)

        @block.sync
        def _(sync):
            for fn in b.st["sync"]:
                fn(sync)

        @block.tensor
        def _(tensor):
            for fn in b.st["tensor"]:
                fn(tensor)

        @block.vector
        def _(vector):
            for fn in b.st["vector"]:
                fn(vector)

        @block.scalar
        def _(scalar):
            for fn in b.st["scalar"]:
                fn(scalar)

        @block.gpsimd
        def _(gpsimd):
            for fn in b.st["gpsimd"]:
                fn(gpsimd)

    return nc



def _tile_w_sD(W):
    S_, D_ = W.shape
    return np.ascontiguousarray(
        W.reshape(S_ // 128, 128, D_ // 128, 128).transpose(0, 3, 2, 1)
    ).reshape(S_ // 128, 128, D_)


def _tile_lhsT(Wt):
    K, M = Wt.shape
    return np.ascontiguousarray(
        Wt.reshape(K // 128, 128, M // 128, 128).transpose(2, 1, 0, 3)
    ).reshape(M // 128, 128, (K // 128) * 128)


def _col16(v):
    return np.ascontiguousarray(np.asarray(v, dtype=np.float32).reshape(16, 128).T)


def make_inputs(I):
    x = np.asarray(I["L0_attn_x"], dtype=np.float32)

    wl0 = np.ascontiguousarray(np.stack([
        _tile_w_sD(np.asarray(I["L0_attn_Wga_w"], dtype=np.float32)),
        _tile_w_sD(np.asarray(I["L0_attn_Wgb_w"], dtype=np.float32)),
        _tile_w_sD(np.asarray(I["L0_attn_Wf_w"], dtype=np.float32)),
        _tile_w_sD(np.asarray(I["L0_attn_Wzg_w"], dtype=np.float32)),
        _tile_w_sD(np.asarray(I["L0_attn_Wz_w"], dtype=np.float32)),
    ]))
    woutt = np.ascontiguousarray(
        np.asarray(I["L0_attn_Wout_w"], dtype=np.float32)
        .reshape(16, 128, 16, 128).transpose(2, 3, 0, 1)).reshape(16, 128, 2048)
    wrout = np.ascontiguousarray(
        np.asarray(I["L0_ffn_w_router"], dtype=np.float32)
        .reshape(16, 128, 16).transpose(1, 0, 2)).reshape(128, 256)
    we1t = np.ascontiguousarray(
        np.asarray(I["L0_ffn_we1"], dtype=np.float32)
        .reshape(16, 16, 128, 8, 128).transpose(0, 3, 2, 1, 4)).reshape(16, 8, 128, 2048)
    we3t = np.ascontiguousarray(
        np.asarray(I["L0_ffn_we3"], dtype=np.float32)
        .reshape(16, 16, 128, 8, 128).transpose(0, 3, 2, 1, 4)).reshape(16, 8, 128, 2048)
    we2t = np.ascontiguousarray(
        np.asarray(I["L0_ffn_we2"], dtype=np.float32)
        .reshape(16, 8, 128, 16, 128).transpose(0, 3, 2, 1, 4)).reshape(16, 16, 128, 1024)
    ws1t = _tile_lhsT(np.asarray(I["L0_ffn_ws1"], dtype=np.float32))
    ws3t = _tile_lhsT(np.asarray(I["L0_ffn_ws3"], dtype=np.float32))
    ws2t = np.ascontiguousarray(
        np.asarray(I["L0_ffn_ws2"], dtype=np.float32)
        .reshape(32, 128, 2, 8, 128).transpose(2, 0, 1, 3, 4)).reshape(2, 32, 128, 1024)
    wqt = _tile_lhsT(np.asarray(I["L1_attn_Wq"], dtype=np.float32))
    wkt = np.ascontiguousarray(
        np.asarray(I["L1_attn_Wk"], dtype=np.float32)
        .reshape(16, 128, 128).transpose(1, 0, 2)).reshape(128, 2048)
    wvt = np.ascontiguousarray(
        np.asarray(I["L1_attn_Wv"], dtype=np.float32)
        .reshape(16, 128, 128).transpose(1, 0, 2)).reshape(128, 2048)
    wot = _tile_lhsT(np.asarray(I["L1_attn_Wo"], dtype=np.float32))
    wltt = np.ascontiguousarray(
        np.asarray(I["L1_ffn_weight"], dtype=np.float32)
        .reshape(16, 128, 16, 128).transpose(0, 3, 2, 1)).reshape(16, 128, 2048)

    ident = np.eye(128, dtype=np.float32)
    onesr = np.ones((1, 128), dtype=np.float32)

    a = np.arange(128)
    mdiag = np.where(a[None, :] >= a[:, None], 0.0, NEG).astype(np.float32)
    mfar = np.where(a[None, :] < a[:, None], 0.0, NEG).astype(np.float32)
    mneg = np.full((128, 128), NEG, dtype=np.float32)
    mzero = np.zeros((128, 128), dtype=np.float32)

    shared = dict(wl0=wl0, woutt=woutt, wrout=wrout, we1t=we1t, we3t=we3t,
                  we2t=we2t, ws1t=ws1t, ws3t=ws3t, ws2t=ws2t, wqt=wqt, wkt=wkt,
                  wvt=wvt, wot=wot, wltt=wltt, identm=ident, onesr=onesr)

    cbase = np.zeros((128, 96), dtype=np.float32)
    cbase[:, 0:16] = _col16(I["L0_attn_rms_w"])
    cbase[:, 16:32] = _col16(I["L0_attn_Wf_b"])
    cbase[:, 32:48] = _col16(I["L0_attn_Wout_b"])
    cbase[:, 48:64] = _col16(I["L1_attn_bo"])
    cbase[:, 64:80] = _col16(I["L1_ffn_bias"])
    cbase[:, 90] = 1.0
    cbase[:, 91] = 1e-5
    cbase[:, 92] = SCALE

    in_maps = []
    for c in range(8):
        bq, q = c // 4, c % 4
        x0tc = np.ascontiguousarray(x[bq, q * 1024:(q + 1) * 1024, :].T)
        cp = cbase.copy()
        for j in range(3):
            cp[:, 80 + j] = 1.0 if q > j else 0.0
            cp[:, 83 + j] = 0.0 if q > j else 1.0
        for r in range(4):
            cp[:, 86 + r] = 1.0 if (q > 0 and r == q - 1) else 0.0
        pos = np.arange(1024, dtype=np.float32) + np.float32(q * 1024)
        invf = np.exp(-np.log(np.float32(10000.0)) *
                      (np.arange(0, 64, 2, dtype=np.float32) /
                       np.float32(64.0))).astype(np.float32)
        theta = (pos[None, :] * invf[:, None]).astype(np.float32)
        ropetc = np.ascontiguousarray(
            np.concatenate([np.cos(theta), np.sin(theta)], axis=0).astype(np.float32))
        masktc = np.ascontiguousarray(np.stack(
            [mdiag, mfar, (mfar if q > 0 else mneg), (mzero if q > 0 else mneg)],
            axis=1))
        m = dict(shared)
        m.update(x0t=x0tc, cpack=cp, ropet=ropetc, maskt=masktc)
        in_maps.append(m)
    return in_maps


def kernel(**inputs):
    in_maps = make_inputs(inputs)
    nc = build()
    res = run_bass_kernel_spmd(nc, in_maps, core_ids=list(range(8)))
    out = np.empty((2, 4096, 2048), dtype=np.float32)
    for c in range(8):
        bq, q = c // 4, c % 4
        out[bq, q * 1024:(q + 1) * 1024, :] = res.results[c]["y"].T
    return out


# version 03
# speedup vs baseline: 1.1055x; 1.0015x over previous
import numpy as np
import concourse.bass as bass
import concourse.mybir as mybir
from concourse.bass_utils import run_bass_kernel_spmd

F32 = mybir.dt.float32
BF16 = mybir.dt.bfloat16
AF = mybir.ActivationFunctionType
AL = mybir.AluOpType
AX = mybir.AxisListType

D = 2048
DC = 16
TL = 1024
SC = 16
HMC = 8
E = 16
HSC = 32
HD = 128
NH = 16
NEG = -1.0e9
SCALE = float(HD) ** -0.5

ENGS = ("sync", "tensor", "vector", "scalar", "gpsimd")
ESEM = {"tensor": "tms", "vector": "vv", "scalar": "ss", "gpsimd": "gg"}
SERIAL = ("vector", "scalar", "gpsimd")


class B:
    def __init__(self, nc, sems):
        self.nc = nc
        self.st = {e: [] for e in ENGS}
        self.sems = sems
        self.cnt = {k: 0 for k in sems}

    def w(self, eng, sem, val):
        if val is None or val <= 0:
            return
        s = self.sems[sem]
        self.st[eng].append(lambda EE, s=s, v=val: EE.wait_ge(s, v))

    def op(self, eng, meth, *a, sig="auto", sigamt=1, serial=True, **kw):
        if eng in SERIAL and serial:
            self.w(eng, ESEM[eng], self.cnt[ESEM[eng]])
        if sig == "auto":
            sig = ESEM.get(eng)
        tgt, s = None, None
        if sig is not None:
            self.cnt[sig] += sigamt
            tgt = self.cnt[sig]
            s = self.sems[sig]

        import traceback
        tb = traceback.extract_stack()[-3]

        def f(EE, meth=meth, a=a, kw=kw, s=s, sigamt=sigamt, tb=tb):
            try:
                inst = getattr(EE, meth)(*a, **kw)
            except Exception:
                print(f"EMIT-FAIL {meth} from {tb.filename}:{tb.lineno}")
                raise
            if s is not None:
                inst.then_inc(s, sigamt)

        self.st[eng].append(f)
        return tgt

    def dma(self, eng, out, in_, sem=None):
        if sem is None:
            sem = "dw" if eng == "sync" else "dg"
        self.cnt[sem] += 16
        tgt = self.cnt[sem]
        s = self.sems[sem]

        def f(EE, out=out, in_=in_, s=s):
            EE.dma_start(out=out, in_=in_).then_inc(s, 16)

        self.st[eng].append(f)
        return tgt

    def barrier(self):
        snap = dict(self.cnt)
        for e in ENGS:
            for sem, val in snap.items():
                if sem != "cc":
                    self.w(e, sem, val)

    def now(self, sem):
        return self.cnt[sem]


class WS:

    def __init__(self, b, bufs, sems):
        self.b = b
        self.bufs = bufs
        self.n = len(bufs)
        self.sems = sems
        self.loaded = []
        self.released = []

    def load(self, src_ap):
        i = len(self.loaded)
        if i >= self.n:
            j = i - self.n
            assert j < len(self.released), f"WS: load {i} before release {j}"
            self.b.w("sync", *self.released[j])
        ncols = 1
        for d in src_ap.shape[1:]:
            ncols *= d
        t = self.b.dma("sync", self.bufs[i % self.n][:, 0:ncols], src_ap,
                       sem=self.sems[i % self.n])
        self.loaded.append((self.sems[i % self.n], t))
        return i

    def use(self, i, eng="tensor"):
        self.b.w(eng, *self.loaded[i])
        return self.bufs[i % self.n]

    def release(self, i, sem, val):
        assert i == len(self.released), "WS: release out of order"
        self.released.append((sem, val))


def build(tap=()):
    nc = bass.Bass("TRN2", target_bir_lowering=False)

    def dram_in(name, shape):
        return nc.dram_tensor(name, list(shape), F32, kind="ExternalInput")

    x0t = dram_in("x0t", (D, TL))
    wl0 = dram_in("wl0", (5, SC, 128, D))
    woutt = dram_in("woutt", (SC, 128, D))
    wrout = dram_in("wrout", (128, DC * 16))
    we1t = dram_in("we1t", (E, HMC, 128, D))
    we3t = dram_in("we3t", (E, HMC, 128, D))
    we2t = dram_in("we2t", (E, DC, 128, HMC * 128))
    ws1t = dram_in("ws1t", (HSC, 128, D))
    ws3t = dram_in("ws3t", (HSC, 128, D))
    ws2t = dram_in("ws2t", (2, HSC, 128, 8 * 128))
    wqt = dram_in("wqt", (NH, 128, D))
    wkt = dram_in("wkt", (128, D))
    wvt = dram_in("wvt", (128, D))
    wot = dram_in("wot", (DC, 128, D))
    wltt = dram_in("wltt", (DC, 128, D))
    cpack = dram_in("cpack", (128, 96))
    ident = dram_in("identm", (128, 128))
    onesr = dram_in("onesr", (1, 128))
    ropet = dram_in("ropet", (64, TL))
    maskt = dram_in("maskt", (128, 4, 128))

    y = nc.dram_tensor("y", [D, TL], F32, kind="ExternalOutput")
    taps = {t: nc.dram_tensor(t, [D, TL], F32, kind="ExternalOutput")
            for t in tap if not t.startswith("q")}
    if "qdbg" in tap:
        taps["qdbg"] = nc.dram_tensor("qdbg", [128, 2048], F32, kind="ExternalOutput")
        taps["thdbg"] = nc.dram_tensor("thdbg", [128, 128], F32, kind="ExternalOutput")
        taps["adbg"] = nc.dram_tensor("adbg", [128, 2048], F32, kind="ExternalOutput")
        taps["bdbg"] = nc.dram_tensor("bdbg", [128, 2048], F32, kind="ExternalOutput")

    cc1i = nc.dram_tensor("cc1i", [128, 32], F32)
    cc1o = nc.dram_tensor("cc1o", [512, 32], F32)
    cc2i = nc.dram_tensor("cc2i", [8, 128, 128], F32)
    cc2o = nc.dram_tensor("cc2o", [32, 128, 128], F32)

    C_RMSW, C_WFB, C_WOUTB, C_BO, C_LB = 0, 16, 32, 48, 64
    C_SCM, C_SCM1, C_HSEL, C_ONE = 80, 83, 86, 90

    import contextlib
    es = contextlib.ExitStack()
    with es:
        resA = es.enter_context(nc.sbuf_tensor("resA", [128, DC, TL], F32))
        bigA = es.enter_context(nc.sbuf_tensor("bigA", [128, 24, TL], F32))
        pool = es.enter_context(nc.sbuf_tensor("pool", [128, 5, TL], F32))
        wsb0 = es.enter_context(nc.sbuf_tensor("wsb0", [128, D], F32))
        wsb1 = es.enter_context(nc.sbuf_tensor("wsb1", [128, D], F32))
        rowA = es.enter_context(nc.sbuf_tensor("rowA", [1, TL], F32))
        rowB = es.enter_context(nc.sbuf_tensor("rowB", [1, TL], F32))
        smisc = es.enter_context(nc.sbuf_tensor("smisc", [128, 112], F32))
        cpk = es.enter_context(nc.sbuf_tensor("cpk", [128, 96], F32))
        identS = es.enter_context(nc.sbuf_tensor("identS", [128, 128], F32))
        onesrS = es.enter_context(nc.sbuf_tensor("onesrS", [1, 128], F32))
        rtmp = es.enter_context(nc.sbuf_tensor("rtmp", [64, 128], F32))
        wrS = es.enter_context(nc.sbuf_tensor("wrS", [128, DC * 16], F32))
        cwS = es.enter_context(nc.sbuf_tensor("cwS", [128, 8, 16], F32))
        P0 = es.enter_context(nc.psum_tensor("P0", [128, 512], F32))
        P1 = es.enter_context(nc.psum_tensor("P1", [128, 512], F32))
        P2 = es.enter_context(nc.psum_tensor("P2", [128, 512], F32))
        P3 = es.enter_context(nc.psum_tensor("P3", [128, 512], F32))
        P4 = es.enter_context(nc.psum_tensor("P4", [128, 512], F32))
        P5 = es.enter_context(nc.psum_tensor("P5", [128, 512], F32))
        P6 = es.enter_context(nc.psum_tensor("P6", [128, 512], F32))
        P7 = es.enter_context(nc.psum_tensor("P7", [128, 512], F32))
        s_dw = es.enter_context(nc.semaphore("dw"))
        s_dwa = es.enter_context(nc.semaphore("dwa"))
        s_dwb = es.enter_context(nc.semaphore("dwb"))
        s_dg = es.enter_context(nc.semaphore("dg"))
        s_tms = es.enter_context(nc.semaphore("tms"))
        s_vv = es.enter_context(nc.semaphore("vv"))
        s_ss = es.enter_context(nc.semaphore("ss"))
        s_gg = es.enter_context(nc.semaphore("gg"))
        s_cc = es.enter_context(nc.semaphore("cc"))
        block = es.enter_context(nc.Block())

        P = [P0, P1, P2, P3, P4, P5, P6, P7]
        sems = dict(dw=s_dw, dwa=s_dwa, dwb=s_dwb, dg=s_dg, tms=s_tms, vv=s_vv, ss=s_ss, gg=s_gg, cc=s_cc)
        b = B(nc, sems)
        ws = WS(b, [wsb0, wsb1], ["dwa", "dwb"])

        bf3 = bigA.ap().rearrange("p a t -> p (a t)").bitcast(BF16) \
            .rearrange("p (a t) -> p a t", a=48)
        ogT = bf3[:, 0:SC, :]
        f16 = bf3[:, SC:2 * SC, :]
        z16 = bf3[:, 2 * SC:3 * SC, :]
        xnB = bigA[:, 0:DC, :]
        auxB = bigA[:, DC:24, :].rearrange("p a t -> p (a t)")

        def aux(off, n):
            return auxB[:, off:off + n]

        sq_t = pool[:, 0, :]
        mbc = pool[:, 1, :]
        rbc = pool[:, 2, :]
        srmsbc = pool[:, 3, :]
        x0r = pool[:, 4, :]
        stg = pool[:, 0, :]
        eptm = pool[:, 1, :]
        scanb = pool[:, 2, :].rearrange("p t -> p t").bitcast(BF16)
        gsmA = pool[:, 4, :]
        cwT = pool[0:16, 3, :]
        cwbc = pool[:, 4, :]
        ropeS = pool[0:64, 3, :]
        maskC = pool[:, 4, 0:512].rearrange("p (a m) -> p a m", a=4)
        gsm = pool[:, 4, 512:1024]

        rmsw = cpk[:, C_RMSW:C_RMSW + 16]
        wfb = cpk[:, C_WFB:C_WFB + 16]
        woutb = cpk[:, C_WOUTB:C_WOUTB + 16]
        bo = cpk[:, C_BO:C_BO + 16]
        lb = cpk[:, C_LB:C_LB + 16]
        scm = cpk[:, C_SCM:C_SCM + 3]
        scm1 = cpk[:, C_SCM1:C_SCM1 + 3]
        hsel = cpk[:, C_HSEL:C_HSEL + 4]
        onesc = cpk[:, C_ONE:C_ONE + 1]
        nc.const_aps.register(F32, 1e-5, cpk[:, 91:92])
        nc.const_aps.register(F32, SCALE, cpk[:, 92:93])
        invw = smisc[:, 0:16]
        hsum = smisc[:, 16:32]
        psum_ = smisc[:, 32:48]
        carry = smisc[:, 48:64]
        ctmp1 = smisc[:, 64:80]
        ctmp2 = smisc[:, 80:96]
        ctmp3 = smisc[:, 96:112]

        resT = [resA[:, c, :] for c in range(DC)]
        xnT = [xnB[:, c, :] for c in range(DC)]

        t_x0 = b.dma("sync", resA[:], x0t.ap().rearrange("(c p) n -> p c n", p=128))
        t_cp = b.dma("gpsimd", cpk[:], cpack.ap())
        t_id = b.dma("gpsimd", identS[:], ident.ap())
        t_or = b.dma("gpsimd", onesrS[:], onesr.ap())
        t_wr = b.dma("gpsimd", wrS[:], wrout.ap())
        b.w("vector", "dg", t_wr)
        b.op("vector", "reciprocal", out=invw, in_=rmsw)
        for e in ENGS:
            b.w(e, "dw", t_x0)
            b.w(e, "dg", t_wr)

        def layernorm(src, dst):
            lastsq = None
            for c in range(DC):
                if lastsq is not None:
                    b.w("scalar", "tms", lastsq)
                sst = b.op("scalar", "activation", out=sq_t, in_=src[c], func=AF.Square)
                for hf in range(2):
                    b.op("tensor", "matmul", out=P[4 + hf][0:1, :], lhsT=onesc,
                         rhs=src[c][:, hf * 512:(hf + 1) * 512],
                         start=(c == 0), stop=(c == DC - 1), sig=None)
                b.w("tensor", "ss", sst)
                for hf in range(2):
                    lastsq = b.op("tensor", "matmul", out=P[6 + hf][0:1, :], lhsT=onesc,
                                  rhs=sq_t[:, hf * 512:(hf + 1) * 512],
                                  start=(c == 0), stop=(c == DC - 1))
            b.w("vector", "tms", lastsq)
            for hf in range(2):
                b.op("vector", "tensor_copy", out=rowA[0:1, hf * 512:(hf + 1) * 512],
                     in_=P[4 + hf][0:1, :])
                b.op("vector", "tensor_copy", out=rowB[0:1, hf * 512:(hf + 1) * 512],
                     in_=P[6 + hf][0:1, :])
            b.op("vector", "tensor_scalar_mul", out=rowA[0:1, :], in0=rowA[0:1, :],
                 scalar1=1.0 / D)
            b.op("vector", "tensor_scalar_mul", out=rowB[0:1, :], in0=rowB[0:1, :],
                 scalar1=1.0 / D)
            b.op("vector", "tensor_mul", out=sq_t[0:1, :], in0=rowA[0:1, :],
                 in1=rowA[0:1, :])
            v1 = b.op("vector", "tensor_sub", out=rowB[0:1, :], in0=rowB[0:1, :],
                      in1=sq_t[0:1, :])
            b.w("scalar", "vv", v1)
            s1 = b.op("scalar", "activation", out=rowB[0:1, :], in_=rowB[0:1, :],
                      func=AF.Sqrt, bias=1e-5)
            b.w("vector", "ss", s1)
            v2 = b.op("vector", "reciprocal", out=rowB[0:1, :], in_=rowB[0:1, :])
            b.w("tensor", "vv", v2)
            for hf in range(2):
                b.op("tensor", "matmul", out=P[4 + hf][:, :], lhsT=onesrS[:],
                     rhs=rowA[0:1, hf * 512:(hf + 1) * 512], start=True, stop=True,
                     sig=None)
                tb_ = b.op("tensor", "matmul", out=P[6 + hf][:, :], lhsT=onesrS[:],
                           rhs=rowB[0:1, hf * 512:(hf + 1) * 512], start=True,
                           stop=True)
            b.w("vector", "tms", tb_)
            for hf in range(2):
                b.op("vector", "tensor_copy", out=mbc[:, hf * 512:(hf + 1) * 512],
                     in_=P[4 + hf][:, :])
                vbc = b.op("vector", "tensor_copy", out=rbc[:, hf * 512:(hf + 1) * 512],
                           in_=P[6 + hf][:, :])
            b.w("gpsimd", "vv", vbc)
            for c in range(DC):
                eng = "vector" if c % 2 == 0 else "gpsimd"
                b.op(eng, "tensor_sub", out=dst[c], in0=src[c], in1=mbc)
                b.op(eng, "tensor_mul", out=dst[c], in0=dst[c], in1=rbc)
            return b.now("vv"), b.now("gg")

        vA, gA = layernorm(resT, resT)

        b.w("scalar", "vv", vA)
        b.w("scalar", "gg", gA)
        b.w("tensor", "vv", vA)
        b.w("tensor", "gg", gA)
        lastsq = None
        for c in range(DC):
            if lastsq is not None:
                b.w("scalar", "tms", lastsq)
            sst = b.op("scalar", "activation", out=sq_t, in_=resT[c], func=AF.Square)
            b.w("tensor", "ss", sst)
            for hf in range(2):
                lastsq = b.op("tensor", "matmul", out=P[6 + hf][0:1, :], lhsT=onesc,
                              rhs=sq_t[:, hf * 512:(hf + 1) * 512],
                              start=(c == 0), stop=(c == DC - 1))
        b.w("vector", "tms", lastsq)
        for hf in range(2):
            b.op("vector", "tensor_copy", out=rowB[0:1, hf * 512:(hf + 1) * 512],
                 in_=P[6 + hf][0:1, :])
        v1 = b.op("vector", "tensor_scalar_mul", out=rowB[0:1, :], in0=rowB[0:1, :],
                  scalar1=1.0 / D)
        b.w("scalar", "vv", v1)
        s1 = b.op("scalar", "activation", out=rowA[0:1, :], in_=rowB[0:1, :],
                  func=AF.Sqrt, bias=1e-5)
        b.w("vector", "ss", s1)
        v2 = b.op("vector", "reciprocal", out=rowB[0:1, :], in_=rowA[0:1, :])
        b.w("tensor", "vv", v2)
        for hf in range(2):
            b.op("tensor", "matmul", out=P[4 + hf][:, :], lhsT=onesrS[:],
                 rhs=rowA[0:1, hf * 512:(hf + 1) * 512], start=True, stop=True,
                 sig=None)
            tb_ = b.op("tensor", "matmul", out=P[6 + hf][:, :], lhsT=onesrS[:],
                       rhs=rowB[0:1, hf * 512:(hf + 1) * 512], start=True, stop=True)
        b.w("vector", "tms", tb_)
        for hf in range(2):
            b.op("vector", "tensor_copy", out=srmsbc[:, hf * 512:(hf + 1) * 512],
                 in_=P[4 + hf][:, :])
            vbc = b.op("vector", "tensor_copy", out=rbc[:, hf * 512:(hf + 1) * 512],
                       in_=P[6 + hf][:, :])
        for c in range(DC):
            b.op("vector", "scalar_tensor_tensor", out=resT[c], in0=resT[c],
                 scalar=rmsw[:, c:c + 1], in1=rbc, op0=AL.mult, op1=AL.mult)
        xnV, xnG = b.now("vv"), b.now("gg")

        NSL = SC * 5
        bank_free = {i: ("tms", 0) for i in range(6)}
        bank_free[4] = ("vv", vbc)
        bank_free[5] = ("vv", vbc)
        silu_ga = None
        sig_zg = None
        scan_last = 0
        b.w("tensor", "vv", xnV)
        b.w("tensor", "gg", xnG)
        for s in range(SC):
            for p in range(5):
                k = s * 5 + p
                if k == 0:
                    ws.load(wl0.ap()[0, 0, :, :])
                if k + 1 < NSL:
                    p2, s2 = (k + 1) % 5, (k + 1) // 5
                    ws.load(wl0.ap()[p2, s2, :, :])
                buf = ws.use(k)
                for hf in range(2):
                    b.w("tensor", *bank_free[(k % 3) * 2 + hf])
                    for c in range(DC):
                        b.op("tensor", "matmul", out=P[(k % 3) * 2 + hf][:, :],
                             lhsT=buf[:, c * 128:(c + 1) * 128],
                             rhs=resT[c][:, hf * 512:(hf + 1) * 512],
                             start=(c == 0), stop=(c == DC - 1),
                             sig=(None if c < DC - 1 else "auto"))
                tmm = b.now("tms")
                ws.release(k, "tms", tmm)
                bp = (k % 3) * 2
                if p == 0:
                    b.w("scalar", "tms", tmm)
                    b.w("scalar", "vv", b.now("vv"))
                    for hf in range(2):
                        sv = b.op("scalar", "activation",
                                  out=sq_t[:, hf * 512:(hf + 1) * 512],
                                  in_=P[bp + hf][:, :], func=AF.Silu)
                    silu_ga = sv
                    bank_free[bp] = ("ss", sv)
                    bank_free[bp + 1] = ("ss", sv)
                elif p == 1:
                    b.w("vector", "tms", tmm)
                    b.w("vector", "ss", silu_ga)
                    for hf in range(2):
                        vv1 = b.op("vector", "tensor_mul",
                                   out=ogT[:, s, hf * 512:(hf + 1) * 512],
                                   in0=sq_t[:, hf * 512:(hf + 1) * 512],
                                   in1=P[bp + hf][:, :])
                    bank_free[bp] = ("vv", vv1)
                    bank_free[bp + 1] = ("vv", vv1)
                elif p == 2:
                    b.w("scalar", "tms", tmm)
                    for hf in range(2):
                        sv = b.op("scalar", "activation",
                                  out=f16[:, s, hf * 512:(hf + 1) * 512],
                                  in_=P[bp + hf][:, :], func=AF.Sigmoid,
                                  bias=wfb[:, s:s + 1])
                    bank_free[bp] = ("ss", sv)
                    bank_free[bp + 1] = ("ss", sv)
                elif p == 3:
                    b.w("scalar", "tms", tmm)
                    b.w("scalar", "vv", b.now("vv"))
                    for hf in range(2):
                        sv = b.op("scalar", "activation",
                                  out=mbc[:, hf * 512:(hf + 1) * 512],
                                  in_=P[bp + hf][:, :], func=AF.Sigmoid)
                    sig_zg = sv
                    bank_free[bp] = ("ss", sv)
                    bank_free[bp + 1] = ("ss", sv)
                else:
                    b.w("vector", "tms", tmm)
                    b.w("vector", "ss", sig_zg)
                    for hf in range(2):
                        vv1 = b.op("vector", "tensor_mul",
                                   out=z16[:, s, hf * 512:(hf + 1) * 512],
                                   in0=mbc[:, hf * 512:(hf + 1) * 512],
                                   in1=P[bp + hf][:, :])
                    bank_free[bp] = ("vv", vv1)
                    bank_free[bp + 1] = ("vv", vv1)
                    b.op("vector", "tensor_tensor_scan", out=scanb[:, 0:TL],
                         data0=f16[:, s, :], data1=z16[:, s, :], initial=0.0,
                         op0=AL.mult, op1=AL.add)
                    b.op("vector", "tensor_copy", out=hsum[:, s:s + 1],
                         in_=scanb[:, TL - 1:TL])
                    b.op("vector", "tensor_tensor_scan", out=scanb[:, TL:2 * TL],
                         data0=f16[:, s, :], data1=f16[:, s, :], initial=1.0,
                         op0=AL.mult, op1=AL.bypass)
                    scan_last = b.op("vector", "tensor_copy", out=psum_[:, s:s + 1],
                                     in_=scanb[:, 2 * TL - 1:2 * TL])

        b.w("gpsimd", "vv", scan_last)
        tgi = b.dma("gpsimd", cc1i.ap(), smisc[:, 16:48])
        b.w("gpsimd", "dg", tgi)
        ccv = b.op("gpsimd", "collective_compute", "AllGather", AL.bypass,
                   replica_groups=[[0, 1, 2, 3], [4, 5, 6, 7]],
                   ins=[cc1i.ap().opt()], outs=[cc1o.ap().opt()],
                   sig="cc", serial=False)
        b.w("gpsimd", "cc", ccv)
        tg = None
        for r in range(3):
            tg = b.dma("gpsimd", gsmA[:, r * 32:(r + 1) * 32],
                       cc1o.ap().rearrange("(r p) c -> p r c", p=128)[:, r, :])
        b.w("vector", "dg", tg)
        b.op("vector", "tensor_scalar_mul", out=carry, in0=gsmA[:, 0:16],
             scalar1=scm[:, 0:1])
        cdone = None
        for j in (1, 2):
            b.op("vector", "tensor_scalar_mul", out=ctmp1, in0=gsmA[:, j * 32:j * 32 + 16],
                 scalar1=scm[:, j:j + 1])
            b.op("vector", "tensor_scalar", out=ctmp2, in0=gsmA[:, j * 32 + 16:j * 32 + 32],
                 scalar1=scm[:, j:j + 1], scalar2=scm1[:, j:j + 1],
                 op0=AL.mult, op1=AL.add)
            b.op("vector", "tensor_mul", out=carry, in0=carry, in1=ctmp2)
            cdone = b.op("vector", "tensor_add", out=carry, in0=carry, in1=ctmp1)

        gate_done = []
        for s in range(SC):
            b.op("vector", "tensor_tensor_scan", out=z16[:, s, :],
                 data0=f16[:, s, :], data1=z16[:, s, :],
                 initial=carry[:, s:s + 1], op0=AL.mult, op1=AL.add)
            gate_done.append(b.op("vector", "tensor_mul", out=ogT[:, s, :],
                                  in0=ogT[:, s, :], in1=z16[:, s, :]))

        x0v = x0t.ap().rearrange("(c p) n -> p c n", p=128)
        b.w("gpsimd", "vv", cdone)
        ep_v = 0
        st_free = [("tms", 0), ("tms", 0)]
        wk_i = len(ws.loaded)
        for dhalf in range(2):
            for tb in range(2):
                tok = slice(tb * 512, (tb + 1) * 512)
                lastd = None
                for sc in range(SC):
                    kk = wk_i + (dhalf * 2 + tb) * SC + sc
                    if kk == wk_i:
                        ws.load(woutt.ap()[0, :, :])
                    if sc + 1 < SC or dhalf * 2 + tb < 3:
                        nxt = (dhalf * 2 + tb) * SC + sc + 1
                        ws.load(woutt.ap()[nxt % SC, :, :])
                    buf = ws.use(kk)
                    sl = sc % 2
                    b.w("vector", "vv", gate_done[sc])
                    b.w("vector", *st_free[sl])
                    vstg = b.op("vector", "tensor_copy",
                                out=stg[:, sl * 512:(sl + 1) * 512],
                                in_=ogT[:, sc, tok])
                    b.w("tensor", "vv", vstg)
                    if sc == 0:
                        b.w("tensor", "vv", ep_v)
                    for di in range(8):
                        d = dhalf * 8 + di
                        lastd = b.op("tensor", "matmul", out=P[di][:, :],
                                     lhsT=buf[:, d * 128:(d + 1) * 128],
                                     rhs=stg[:, sl * 512:(sl + 1) * 512],
                                     start=(sc == 0), stop=(sc == SC - 1),
                                     sig=(None if di < 7 else "auto"))
                    st_free[sl] = ("tms", lastd)
                    ws.release(kk, "tms", lastd)
                for di in range(8):
                    d = dhalf * 8 + di
                    b.w("gpsimd", "vv", b.now("vv"))
                    tx0 = b.dma("gpsimd", x0r[:, (di % 2) * 512:(di % 2 + 1) * 512],
                                x0v[:, d, tok])
                    g1 = b.op("vector", "scalar_tensor_tensor", out=eptm[:, 0:512],
                              in0=resT[d][:, tok], scalar=invw[:, d:d + 1],
                              in1=srmsbc[:, tok], op0=AL.mult, op1=AL.mult)
                    b.w("vector", "dg", tx0)
                    g2 = b.op("vector", "tensor_add", out=eptm[:, 0:512],
                              in0=eptm[:, 0:512],
                              in1=x0r[:, (di % 2) * 512:(di % 2 + 1) * 512])
                    b.w("vector", "tms", lastd)
                    ep_v = b.op("vector", "scalar_tensor_tensor", out=resT[d][:, tok],
                                in0=P[di][:, :], scalar=woutb[:, d:d + 1],
                                in1=eptm[:, 0:512], op0=AL.add, op1=AL.add)
        b.barrier()
        if "x1" in taps:
            tv = None
            for c in range(DC):
                tv = b.dma("gpsimd", taps["x1"].ap().rearrange(
                    "(c p) n -> p c n", p=128)[:, c, :], resT[c])
            b.w("gpsimd", "dg", tv)
            b.barrier()

        layernorm(resT, xnT)
        b.barrier()

        vcwT = 0
        for tt in range(8):
            pb = P[6 + (tt % 2)]
            b.w("tensor", "vv", b.now("vv"))
            for c in range(DC):
                b.op("tensor", "matmul", out=pb[:, 0:16],
                     lhsT=xnT[c][:, tt * 128:(tt + 1) * 128],
                     rhs=wrS[:, c * 16:(c + 1) * 16],
                     start=(c == 0), stop=(c == DC - 1),
                     sig=(None if c < DC - 1 else "auto"))
            tmm = b.now("tms")
            b.w("vector", "tms", tmm)
            cwt = cwS[:, tt, :]
            mx = ctmp1[:, 0:1]
            b.op("vector", "reduce_max", out=mx, in_=pb[:, 0:16], axis=AX.X,
                 negate=True)
            vv1 = b.op("vector", "tensor_copy", out=cwt, in_=pb[:, 0:16])
            b.w("scalar", "vv", vv1)
            sx = b.op("scalar", "activation", out=cwt, in_=cwt, func=AF.Exp, bias=mx)
            b.w("vector", "ss", sx)
            sm = ctmp1[:, 1:2]
            b.op("vector", "reduce_sum", out=sm, in_=cwt, axis=AX.X)
            b.op("vector", "reciprocal", out=sm, in_=sm)
            b.op("vector", "tensor_scalar_mul", out=cwt, in0=cwt, scalar1=sm)
            cur = ctmp2
            b.op("vector", "tensor_copy", out=cur, in_=cwt)
            th = ctmp1[:, 2:3]
            for it in range(4):
                b.op("vector", "reduce_max", out=th, in_=cur, axis=AX.X)
                if it < 3:
                    b.op("vector", "tensor_single_scalar", out=ctmp3, in_=cur,
                         scalar=th, op=AL.is_ge)
                    b.op("vector", "tensor_scalar_mul", out=ctmp3, in0=ctmp3,
                         scalar1=1e30)
                    b.op("vector", "tensor_sub", out=cur, in0=cur, in1=ctmp3)
            b.op("vector", "tensor_single_scalar", out=cur, in_=cwt, scalar=th,
                 op=AL.is_ge)
            b.op("vector", "tensor_mul", out=cwt, in0=cwt, in1=cur)
            s4 = ctmp1[:, 1:2]
            b.op("vector", "reduce_sum", out=s4, in_=cwt, axis=AX.X)
            b.op("vector", "reciprocal", out=s4, in_=s4)
            vcw = b.op("vector", "tensor_scalar_mul", out=cwt, in0=cwt, scalar1=s4)
            b.w("tensor", "vv", vcw)
            ttr = b.op("tensor", "transpose", out=P[4 + (tt % 2)][0:16, 0:128],
                       in_=cwt, identity=identS[:])
            b.w("vector", "tms", ttr)
            vcwT = b.op("vector", "tensor_copy", out=cwT[:, tt * 128:(tt + 1) * 128],
                        in_=P[4 + (tt % 2)][0:16, 0:128])
        b.barrier()

        gbufF = aux(0, 8 * TL)
        ep_sh = ("vv", 0)
        for tq in range(4):
            tok = slice(tq * 256, (tq + 1) * 256)
            glu_done = [0] * HSC
            for hs in range(HSC):
                k1 = ws.load(ws1t.ap()[hs, :, :])
                buf1 = ws.use(k1)
                b.w("tensor", *ep_sh)
                for c in range(DC):
                    b.op("tensor", "matmul", out=P0[:, 0:256],
                         lhsT=buf1[:, c * 128:(c + 1) * 128], rhs=xnT[c][:, tok],
                         start=(c == 0), stop=(c == DC - 1),
                         sig=(None if c < DC - 1 else "auto"))
                t1 = b.now("tms")
                ws.release(k1, "tms", t1)
                k3 = ws.load(ws3t.ap()[hs, :, :])
                buf3 = ws.use(k3)
                for c in range(DC):
                    b.op("tensor", "matmul", out=P1[:, 0:256],
                         lhsT=buf3[:, c * 128:(c + 1) * 128], rhs=xnT[c][:, tok],
                         start=(c == 0), stop=(c == DC - 1),
                         sig=(None if c < DC - 1 else "auto"))
                t3 = b.now("tms")
                ws.release(k3, "tms", t3)
                b.w("scalar", "tms", t1)
                b.w("scalar", "vv", b.now("vv"))
                ssl = b.op("scalar", "activation", out=sq_t[:, 0:256],
                           in_=P0[:, 0:256], func=AF.Silu)
                b.w("vector", "ss", ssl)
                b.w("vector", "tms", t3)
                glu_done[hs] = b.op("vector", "tensor_mul",
                                    out=gbufF[:, hs * 256:(hs + 1) * 256],
                                    in0=sq_t[:, 0:256], in1=P1[:, 0:256])
                ep_sh = ("vv", glu_done[hs])
            for dh in range(2):
                lastd = None
                for hs in range(HSC):
                    ki = ws.load(ws2t.ap()[dh, hs, :, :])
                    buf = ws.use(ki)
                    b.w("tensor", "vv", glu_done[hs])
                    if hs == 0:
                        b.w("tensor", "vv", b.now("vv"))
                        b.w("tensor", "gg", b.now("gg"))
                    for di in range(8):
                        lastd = b.op("tensor", "matmul", out=P[di][:, 0:256],
                                     lhsT=buf[:, di * 128:(di + 1) * 128],
                                     rhs=gbufF[:, hs * 256:(hs + 1) * 256],
                                     start=(hs == 0), stop=(hs == HSC - 1),
                                     sig=(None if di < 7 else "auto"))
                    ws.release(ki, "tms", lastd)
                for di in range(8):
                    d = dh * 8 + di
                    b.w("vector", "tms", lastd)
                    epv = b.op("vector", "scalar_tensor_tensor", out=resT[d][:, tok],
                               in0=P[di][:, 0:256], scalar=0.2,
                               in1=resT[d][:, tok], op0=AL.mult, op1=AL.add)
                ep_sh = ("vv", b.now("vv"))
                b.w("tensor", "vv", b.now("vv"))
                b.w("tensor", "gg", b.now("gg"))
        b.barrier()

        for e in range(E):
            b.w("tensor", "vv", b.now("vv"))
            b.w("tensor", "gg", b.now("gg"))
            tsel = None
            for hf in range(2):
                tsel = b.op("tensor", "matmul", out=P[6 + hf][0:1, :],
                            lhsT=identS[0:16, e:e + 1],
                            rhs=cwT[:, hf * 512:(hf + 1) * 512],
                            start=True, stop=True)
            b.w("vector", "tms", tsel)
            b.w("vector", "gg", b.now("gg"))
            vsel = None
            for hf in range(2):
                vsel = b.op("vector", "tensor_copy",
                            out=rowA[0:1, hf * 512:(hf + 1) * 512],
                            in_=P[6 + hf][0:1, :])
            b.w("tensor", "vv", vsel)
            tbc = None
            for hf in range(2):
                tbc = b.op("tensor", "matmul", out=P[6 + hf][:, :], lhsT=onesrS[:],
                           rhs=rowA[0:1, hf * 512:(hf + 1) * 512],
                           start=True, stop=True)
            b.w("vector", "tms", tbc)
            vbc = None
            for hf in range(2):
                vbc = b.op("vector", "tensor_copy",
                           out=cwbc[:, hf * 512:(hf + 1) * 512], in_=P[6 + hf][:, :])
            gb_last = 0
            ht_done = {}
            for ht in range(HMC):
                ab_base = 0 if ht % 2 == 0 else 4
                k1 = ws.load(we1t.ap()[e, ht, :, :])
                buf1 = ws.use(k1)
                if ht - 2 in ht_done:
                    b.w("tensor", "ss", ht_done[ht - 2][0])
                    b.w("tensor", "vv", ht_done[ht - 2][1])
                elif ht < 2:
                    b.w("tensor", "ss", b.now("ss"))
                    b.w("tensor", "vv", b.now("vv"))
                for hf in range(2):
                    for c in range(DC):
                        b.op("tensor", "matmul", out=P[ab_base + hf][:, :],
                             lhsT=buf1[:, c * 128:(c + 1) * 128],
                             rhs=xnT[c][:, hf * 512:(hf + 1) * 512],
                             start=(c == 0), stop=(c == DC - 1),
                             sig=(None if c < DC - 1 else "auto"))
                t1 = b.now("tms")
                ws.release(k1, "tms", t1)
                k3 = ws.load(we3t.ap()[e, ht, :, :])
                buf3 = ws.use(k3)
                for hf in range(2):
                    for c in range(DC):
                        b.op("tensor", "matmul", out=P[ab_base + 2 + hf][:, :],
                             lhsT=buf3[:, c * 128:(c + 1) * 128],
                             rhs=xnT[c][:, hf * 512:(hf + 1) * 512],
                             start=(c == 0), stop=(c == DC - 1),
                             sig=(None if c < DC - 1 else "auto"))
                t3 = b.now("tms")
                ws.release(k3, "tms", t3)
                b.w("scalar", "tms", t1)
                b.w("scalar", "vv", b.now("vv"))
                ssl = None
                for hf in range(2):
                    ssl = b.op("scalar", "activation",
                               out=sq_t[:, hf * 512:(hf + 1) * 512],
                               in_=P[ab_base + hf][:, :], func=AF.Silu)
                b.w("vector", "ss", ssl)
                b.w("vector", "tms", t3)
                b.w("vector", "gg", b.now("gg"))
                vg1 = None
                for hf in range(2):
                    vg1 = b.op("vector", "tensor_mul",
                               out=mbc[:, hf * 512:(hf + 1) * 512],
                               in0=sq_t[:, hf * 512:(hf + 1) * 512],
                               in1=P[ab_base + 2 + hf][:, :])
                b.w("gpsimd", "vv", vg1)
                b.w("gpsimd", "tms", b.now("tms"))
                gb_last = b.op("gpsimd", "tensor_mul",
                               out=gbufF[:, ht * TL:(ht + 1) * TL],
                               in0=mbc[:, 0:TL], in1=cwbc[:])
                ht_done[ht] = (ssl, vg1)
            ep_rec = {}
            for dt in range(DC):
                ki = ws.load(we2t.ap()[e, dt, :, :])
                buf = ws.use(ki)
                b.w("tensor", "gg", gb_last)
                if dt - 2 in ep_rec:
                    b.w("tensor", "vv", ep_rec[dt - 2])
                elif dt < 2:
                    b.w("tensor", "vv", b.now("vv"))
                    b.w("tensor", "gg", b.now("gg"))
                    b.w("tensor", "ss", b.now("ss"))
                tmm2 = {}
                for hf in range(2):
                    pb = P[4 + 2 * (dt % 2) + hf - 4 * ((dt % 2) and False)] if False else P[(dt % 2) * 2 + hf + 4]
                    for hc in range(HMC):
                        b.op("tensor", "matmul", out=pb[:, :],
                             lhsT=buf[:, hc * 128:(hc + 1) * 128],
                             rhs=gbufF[:, hc * TL + hf * 512:hc * TL + (hf + 1) * 512],
                             start=(hc == 0), stop=(hc == HMC - 1),
                             sig=(None if hc < HMC - 1 else "auto"))
                    tmm2[hf] = b.now("tms")
                epv = None
                for hf in range(2):
                    pb = P[(dt % 2) * 2 + hf + 4]
                    b.w("vector", "tms", tmm2[hf])
                    epv = b.op("vector", "scalar_tensor_tensor",
                               out=resT[dt][:, hf * 512:(hf + 1) * 512],
                               in0=pb[:, :], scalar=0.8,
                               in1=resT[dt][:, hf * 512:(hf + 1) * 512],
                               op0=AL.mult, op1=AL.add)
                ep_rec[dt] = epv
                ws.release(ki, "tms", tmm2[1])
        b.barrier()
        if "x2" in taps:
            tv = None
            for c in range(DC):
                tv = b.dma("gpsimd", taps["x2"].ap().rearrange(
                    "(c p) n -> p c n", p=128)[:, c, :], resT[c])
            b.w("gpsimd", "dg", tv)
            b.barrier()

        layernorm(resT, xnT)
        b.barrier()

        t_rp = b.dma("gpsimd", ropeS[:], ropet.ap())
        t_mk = b.dma("gpsimd", maskC[:], maskt.ap())
        for e_ in ("tensor", "vector", "scalar"):
            b.w(e_, "dg", t_mk)
        kT = aux(0, 12 * 128).rearrange("p (a m) -> p a m", a=12)
        vtm = aux(12 * 128, 12 * 128).rearrange("p (a m) -> p a m", a=12)
        attb = aux(24 * 128, 16 * 128).rearrange("p (a m) -> p a m", a=16)
        exps = aux(40 * 128, 5 * 128).rearrange("p (a m) -> p a m", a=5)
        qrope = aux(45 * 128, 128)
        rbc2 = aux(46 * 128, 128)
        rsb = rowA[0:1, 0:128]
        cosT = ropeS[0:32, :]
        sinT = ropeS[32:64, :]

        kwi = ws.load(wkt.ap())
        vwi = ws.load(wvt.ap())
        kbuf = ws.use(kwi)
        vbuf = ws.use(vwi)

        def kv_block(blk):
            t0 = (blk - 4) * 128
            tokb = slice(t0, t0 + 128)
            pb = P[blk % 2]
            b.w("tensor", "vv", b.now("vv"))
            for c in range(DC):
                b.op("tensor", "matmul", out=pb[:, 0:128],
                     lhsT=kbuf[:, c * 128:(c + 1) * 128], rhs=xnT[c][:, tokb],
                     start=(c == 0), stop=(c == DC - 1),
                     sig=(None if c < DC - 1 else "auto"))
            tk = b.now("tms")
            b.w("vector", "tms", tk)
            cs = cosT[:, tokb]
            sn = sinT[:, tokb]
            kb_ = kT[:, blk, :]
            b.op("vector", "tensor_mul", out=kb_[0:32, :], in0=pb[0:32, 0:128], in1=cs)
            b.op("vector", "tensor_mul", out=rtmp[0:32, :], in0=pb[32:64, 0:128], in1=sn)
            b.op("vector", "tensor_sub", out=kb_[0:32, :], in0=kb_[0:32, :],
                 in1=rtmp[0:32, :])
            b.op("vector", "tensor_mul", out=kb_[32:64, :], in0=pb[0:32, 0:128], in1=sn)
            b.op("vector", "tensor_mul", out=rtmp[32:64, :], in0=pb[32:64, 0:128], in1=cs)
            b.op("vector", "tensor_add", out=kb_[32:64, :], in0=kb_[32:64, :],
                 in1=rtmp[32:64, :])
            vk = b.op("vector", "tensor_copy", out=kb_[64:128, :],
                      in_=pb[64:128, 0:128])
            pv = P[2 + blk % 2]
            b.w("tensor", "vv", vk)
            for c in range(DC):
                b.op("tensor", "matmul", out=pv[:, 0:128],
                     lhsT=xnT[c][:, tokb], rhs=vbuf[:, c * 128:(c + 1) * 128],
                     start=(c == 0), stop=(c == DC - 1),
                     sig=(None if c < DC - 1 else "auto"))
            tv_ = b.now("tms")
            b.w("vector", "tms", tv_)
            return b.op("vector", "tensor_copy", out=vtm[:, blk, :], in_=pv[:, 0:128])

        for blk in (8, 9, 10, 11):
            kv_tail = kv_block(blk)
        b.w("gpsimd", "vv", kv_tail)
        tg1 = b.dma("gpsimd", cc2i.ap()[0:4, :, :].rearrange("a p m -> p a m"),
                    kT[:, 8:12, :])
        tg2 = b.dma("gpsimd", cc2i.ap()[4:8, :, :].rearrange("a p m -> p a m"),
                    vtm[:, 8:12, :])
        b.w("gpsimd", "dg", tg2)
        ccv2 = b.op("gpsimd", "collective_compute", "AllGather", AL.bypass,
                    replica_groups=[[0, 1, 2, 3], [4, 5, 6, 7]],
                    ins=[cc2i.ap().opt()], outs=[cc2o.ap().opt()],
                    sig="cc", serial=False)
        for blk in (4, 5, 6, 7):
            kv_own = kv_block(blk)
        tmm_kv = b.now("tms")
        ws.release(kwi, "tms", tmm_kv)
        ws.release(vwi, "tms", tmm_kv)
        b.w("gpsimd", "cc", ccv2)
        vprev = kv_own
        for part in range(2):
            base = kT if part == 0 else vtm
            dst = aux(part * 12 * 128, 4 * 128)
            for r in range(4):
                b.w("gpsimd", "vv", vprev)
                tgr = b.dma("gpsimd", gsm[:, :],
                            cc2o.ap()[r * 8 + part * 4:r * 8 + part * 4 + 4, :, :]
                            .rearrange("a p m -> p a m"))
                b.w("vector", "dg", tgr)
                if r == 0:
                    vprev = b.op("vector", "tensor_scalar_mul", out=dst,
                                 in0=gsm[:].rearrange("p (a m) -> p a m", a=4)
                                 .rearrange("p a m -> p (a m)"),
                                 scalar1=hsel[:, 0:1])
                else:
                    vprev = b.op("vector", "scalar_tensor_tensor", out=dst,
                                 in0=gsm[:], scalar=hsel[:, r:r + 1], in1=dst,
                                 op0=AL.mult, op1=AL.add)
        b.barrier()

        qr2 = aux(47 * 128, 128)
        for qt in range(8):
            tokq = slice(qt * 128, (qt + 1) * 128)
            att_last = 0
            tse = 0
            qrb = [qrope, qr2]

            def qproj(h):
                ki = ws.load(wqt.ap()[h, :, :])
                buf = ws.use(ki)
                for c in range(DC):
                    b.op("tensor", "matmul", out=P0[:, 0:128],
                         lhsT=buf[:, c * 128:(c + 1) * 128], rhs=xnT[c][:, tokq],
                         start=(c == 0), stop=(c == DC - 1),
                         sig=(None if c < DC - 1 else "auto"))
                tq_ = b.now("tms")
                ws.release(ki, "tms", tq_)
                qr = qrb[h % 2]
                b.w("vector", "tms", tq_)
                cs = cosT[:, tokq]
                sn = sinT[:, tokq]
                b.op("vector", "tensor_mul", out=qr[0:32, :], in0=P0[0:32, 0:128],
                     in1=cs)
                b.op("vector", "tensor_mul", out=rtmp[0:32, :], in0=P0[32:64, 0:128],
                     in1=sn)
                b.op("vector", "tensor_sub", out=qr[0:32, :], in0=qr[0:32, :],
                     in1=rtmp[0:32, :])
                b.op("vector", "tensor_mul", out=qr[32:64, :], in0=P0[0:32, 0:128],
                     in1=sn)
                b.op("vector", "tensor_mul", out=rtmp[32:64, :], in0=P0[32:64, 0:128],
                     in1=cs)
                b.op("vector", "tensor_add", out=qr[32:64, :], in0=qr[32:64, :],
                     in1=rtmp[32:64, :])
                return b.op("vector", "tensor_copy", out=qr[64:128, :],
                            in_=P0[64:128, 0:128])

            b.w("tensor", "vv", b.now("vv"))
            b.w("tensor", "ss", b.now("ss"))
            vq = qproj(0)
            vq_next = None
            sb = [P1, P2, P6]
            exp_rec = [0] * 5
            for h in range(NH):
                qr = qrb[h % 2]
                b.w("tensor", "vv", vq)
                for i in range(5):
                    kb = qt + i
                    pb = sb[i % 3]
                    b.w("tensor", "ss", exp_rec[[3, 4, 2, 0, 1][i]])
                    tsc = b.op("tensor", "matmul", out=pb[:, 0:128],
                               lhsT=kT[:, kb, :], rhs=qr, start=True, stop=True)
                    if i == 0:
                        mi = 2 if qt < 4 else 1
                    elif i == 4:
                        mi = 0
                    elif qt + i < 4:
                        mi = 3
                    else:
                        mi = None
                    if mi is not None:
                        b.w("vector", "tms", tsc)
                        vmk = b.op("vector", "tensor_add", out=pb[:, 0:128],
                                   in0=pb[:, 0:128], in1=maskC[:, mi, :])
                        b.w("scalar", "vv", vmk)
                    else:
                        b.w("scalar", "tms", tsc)
                    b.w("scalar", "tms", tse)
                    exp_rec[i] = b.op("scalar", "activation", out=exps[:, i, :],
                                      in_=pb[:, 0:128], func=AF.Exp, scale=SCALE)
                if h + 1 < NH:
                    vq_next = qproj(h + 1)
                b.w("tensor", "vv", att_last)
                for i in range(5):
                    kb = qt + i
                    b.w("tensor", "ss", exp_rec[i])
                    b.op("tensor", "matmul", out=P3[0:1, 0:128], lhsT=onesc,
                         rhs=exps[:, i, :], start=(i == 0), stop=(i == 4), sig=None)
                    b.op("tensor", "matmul", out=P5[:, 0:128], lhsT=vtm[:, kb, :],
                         rhs=exps[:, i, :], start=(i == 0), stop=(i == 4),
                         sig=(None if i < 4 else "auto"))
                tse = b.now("tms")
                b.w("vector", "tms", tse)
                vrc = b.op("vector", "reciprocal", out=rsb, in_=P3[0:1, 0:128])
                b.w("tensor", "vv", vrc)
                tbc = b.op("tensor", "matmul", out=P4[:, 0:128], lhsT=onesrS[:],
                           rhs=rsb, start=True, stop=True)
                b.w("vector", "tms", tbc)
                b.op("vector", "tensor_copy", out=rbc2, in_=P4[:, 0:128])
                att_last = b.op("vector", "tensor_mul", out=attb[:, h, :],
                                in0=P5[:, 0:128], in1=rbc2)
                vq = vq_next
            for dh in range(2):
                lastd = None
                for di in range(8):
                    d = dh * 8 + di
                    ki = ws.load(wot.ap()[d, :, :])
                    buf = ws.use(ki)
                    if di == 0:
                        b.w("tensor", "vv", att_last)
                        b.w("tensor", "gg", b.now("gg"))
                        b.w("tensor", "vv", b.now("vv"))
                    for hc in range(NH):
                        lastd = b.op("tensor", "matmul", out=P[di][:, 0:128],
                                     lhsT=buf[:, hc * 128:(hc + 1) * 128],
                                     rhs=attb[:, hc, :],
                                     start=(hc == 0), stop=(hc == NH - 1),
                                     sig=(None if hc < NH - 1 else "auto"))
                    ws.release(ki, "tms", lastd)
                for di in range(8):
                    d = dh * 8 + di
                    b.w("vector", "tms", lastd)
                    b.op("vector", "scalar_tensor_tensor", out=resT[d][:, tokq],
                         in0=P[di][:, 0:128], scalar=bo[:, d:d + 1],
                         in1=resT[d][:, tokq], op0=AL.add, op1=AL.add)
        b.barrier()
        if "x3" in taps:
            tv = None
            for c in range(DC):
                tv = b.dma("gpsimd", taps["x3"].ap().rearrange(
                    "(c p) n -> p c n", p=128)[:, c, :], resT[c])
            b.w("gpsimd", "dg", tv)
            b.barrier()

        layernorm(resT, xnT)
        b.barrier()

        abuf = aux(0, D)
        thb = aux(D, 128)
        qsl2 = [aux(D + 128, D), aux(D + 128 + D, D)]
        store_t = None
        prep_rec = {}

        def prep(dt):
            ki = ws.load(wltt.ap()[dt, :, :])
            buf = ws.use(ki, eng="scalar")
            b.w("vector", *ws.loaded[ki])
            qsl = qsl2[dt % 2]
            if dt - 1 in prep_rec:
                b.w("scalar", "tms", prep_rec[dt - 1][1])
                b.w("scalar", "vv", prep_rec[dt - 1][0])
            else:
                b.w("scalar", "tms", b.now("tms"))
            b.op("scalar", "activation", out=bigA[:, DC, :], in_=buf[:, 0:1024],
                 func=AF.Abs)
            sab = b.op("scalar", "activation", out=bigA[:, DC + 1, :],
                       in_=buf[:, 1024:2048], func=AF.Abs)
            b.w("tensor", "ss", sab)
            pb = 6 if dt % 2 == 0 else 4
            b.w("tensor", "vv", b.now("vv"))
            for c in range(DC):
                b.op("tensor", "matmul", out=P[pb][0:1, 0:128], lhsT=onesc,
                     rhs=abuf[:, c * 128:(c + 1) * 128],
                     start=(c == 0), stop=(c == DC - 1),
                     sig=(None if c < DC - 1 else "auto"))
            tsa = b.now("tms")
            b.w("vector", "tms", tsa)
            vth = b.op("vector", "tensor_scalar_mul", out=rsb,
                       in0=P[pb][0:1, 0:128], scalar1=0.7 / D)
            b.w("tensor", "vv", vth)
            tbc = b.op("tensor", "matmul", out=P[pb + 1][:, 0:128], lhsT=onesrS[:],
                       rhs=rsb, start=True, stop=True)
            b.w("vector", "tms", tbc)
            vthb = b.op("vector", "tensor_copy", out=thb, in_=P[pb + 1][:, 0:128])
            thbb = thb.rearrange("p (a m) -> p a m", a=1).broadcast_to([128, DC, 128])
            bufv = buf[:].rearrange("p (a m) -> p a m", a=DC)
            qslv = qsl.rearrange("p (a m) -> p a m", a=DC)
            abv = abuf.rearrange("p (a m) -> p a m", a=DC)
            b.w("vector", "ss", sab)
            b.op("vector", "tensor_tensor", out=abv, in0=bufv, in1=thbb, op=AL.add)
            b.op("vector", "tensor_tensor", out=qslv, in0=bufv, in1=thbb, op=AL.is_gt)
            b.op("vector", "tensor_single_scalar", out=abuf, in_=abuf, scalar=0.0,
                 op=AL.is_lt)
            vq = b.op("vector", "tensor_sub", out=qsl, in0=qsl, in1=abuf)
            ws.release(ki, "vv", vq)
            prep_rec[dt] = (vq, tsa)
            return vq

        prep(0)
        for dt in range(DC):
            if dt + 1 < DC:
                prep(dt + 1)
            qsl = qsl2[dt % 2]
            b.w("tensor", "vv", prep_rec[dt][0])
            for hf in range(2):
                for c in range(DC):
                    b.op("tensor", "matmul", out=P[dt % 2][:, :],
                         lhsT=qsl[:, c * 128:(c + 1) * 128],
                         rhs=xnT[c][:, hf * 512:(hf + 1) * 512],
                         start=(c == 0), stop=(c == DC - 1),
                         sig=(None if c < DC - 1 else "auto"))
                lastq = b.now("tms")
                b.w("vector", "tms", lastq)
                epd = b.op("vector", "scalar_tensor_tensor",
                           out=resT[dt][:, hf * 512:(hf + 1) * 512],
                           in0=P[dt % 2][:, :], scalar=lb[:, dt:dt + 1],
                           in1=resT[dt][:, hf * 512:(hf + 1) * 512],
                           op0=AL.add, op1=AL.add)
                b.w("tensor", "vv", epd)
            b.w("gpsimd", "vv", b.now("vv"))
            store_t = b.dma("gpsimd",
                            y.ap().rearrange("(c p) n -> p c n", p=128)[:, dt, :],
                            resT[dt])
        b.w("gpsimd", "dg", store_t)

        @block.sync
        def _(sync):
            for fn in b.st["sync"]:
                fn(sync)

        @block.tensor
        def _(tensor):
            for fn in b.st["tensor"]:
                fn(tensor)

        @block.vector
        def _(vector):
            for fn in b.st["vector"]:
                fn(vector)

        @block.scalar
        def _(scalar):
            for fn in b.st["scalar"]:
                fn(scalar)

        @block.gpsimd
        def _(gpsimd):
            for fn in b.st["gpsimd"]:
                fn(gpsimd)

    return nc



def _tile_w_sD(W):
    S_, D_ = W.shape
    return np.ascontiguousarray(
        W.reshape(S_ // 128, 128, D_ // 128, 128).transpose(0, 3, 2, 1)
    ).reshape(S_ // 128, 128, D_)


def _tile_lhsT(Wt):
    K, M = Wt.shape
    return np.ascontiguousarray(
        Wt.reshape(K // 128, 128, M // 128, 128).transpose(2, 1, 0, 3)
    ).reshape(M // 128, 128, (K // 128) * 128)


def _col16(v):
    return np.ascontiguousarray(np.asarray(v, dtype=np.float32).reshape(16, 128).T)


def make_inputs(I):
    x = np.asarray(I["L0_attn_x"], dtype=np.float32)

    wl0 = np.ascontiguousarray(np.stack([
        _tile_w_sD(np.asarray(I["L0_attn_Wga_w"], dtype=np.float32)),
        _tile_w_sD(np.asarray(I["L0_attn_Wgb_w"], dtype=np.float32)),
        _tile_w_sD(np.asarray(I["L0_attn_Wf_w"], dtype=np.float32)),
        _tile_w_sD(np.asarray(I["L0_attn_Wzg_w"], dtype=np.float32)),
        _tile_w_sD(np.asarray(I["L0_attn_Wz_w"], dtype=np.float32)),
    ]))
    woutt = np.ascontiguousarray(
        np.asarray(I["L0_attn_Wout_w"], dtype=np.float32)
        .reshape(16, 128, 16, 128).transpose(2, 3, 0, 1)).reshape(16, 128, 2048)
    wrout = np.ascontiguousarray(
        np.asarray(I["L0_ffn_w_router"], dtype=np.float32)
        .reshape(16, 128, 16).transpose(1, 0, 2)).reshape(128, 256)
    we1t = np.ascontiguousarray(
        np.asarray(I["L0_ffn_we1"], dtype=np.float32)
        .reshape(16, 16, 128, 8, 128).transpose(0, 3, 2, 1, 4)).reshape(16, 8, 128, 2048)
    we3t = np.ascontiguousarray(
        np.asarray(I["L0_ffn_we3"], dtype=np.float32)
        .reshape(16, 16, 128, 8, 128).transpose(0, 3, 2, 1, 4)).reshape(16, 8, 128, 2048)
    we2t = np.ascontiguousarray(
        np.asarray(I["L0_ffn_we2"], dtype=np.float32)
        .reshape(16, 8, 128, 16, 128).transpose(0, 3, 2, 1, 4)).reshape(16, 16, 128, 1024)
    ws1t = _tile_lhsT(np.asarray(I["L0_ffn_ws1"], dtype=np.float32))
    ws3t = _tile_lhsT(np.asarray(I["L0_ffn_ws3"], dtype=np.float32))
    ws2t = np.ascontiguousarray(
        np.asarray(I["L0_ffn_ws2"], dtype=np.float32)
        .reshape(32, 128, 2, 8, 128).transpose(2, 0, 1, 3, 4)).reshape(2, 32, 128, 1024)
    wqt = _tile_lhsT(np.asarray(I["L1_attn_Wq"], dtype=np.float32))
    wkt = np.ascontiguousarray(
        np.asarray(I["L1_attn_Wk"], dtype=np.float32)
        .reshape(16, 128, 128).transpose(1, 0, 2)).reshape(128, 2048)
    wvt = np.ascontiguousarray(
        np.asarray(I["L1_attn_Wv"], dtype=np.float32)
        .reshape(16, 128, 128).transpose(1, 0, 2)).reshape(128, 2048)
    wot = _tile_lhsT(np.asarray(I["L1_attn_Wo"], dtype=np.float32))
    wltt = np.ascontiguousarray(
        np.asarray(I["L1_ffn_weight"], dtype=np.float32)
        .reshape(16, 128, 16, 128).transpose(0, 3, 2, 1)).reshape(16, 128, 2048)

    ident = np.eye(128, dtype=np.float32)
    onesr = np.ones((1, 128), dtype=np.float32)

    a = np.arange(128)
    mdiag = np.where(a[None, :] >= a[:, None], 0.0, NEG).astype(np.float32)
    mfar = np.where(a[None, :] < a[:, None], 0.0, NEG).astype(np.float32)
    mneg = np.full((128, 128), NEG, dtype=np.float32)
    mzero = np.zeros((128, 128), dtype=np.float32)

    shared = dict(wl0=wl0, woutt=woutt, wrout=wrout, we1t=we1t, we3t=we3t,
                  we2t=we2t, ws1t=ws1t, ws3t=ws3t, ws2t=ws2t, wqt=wqt, wkt=wkt,
                  wvt=wvt, wot=wot, wltt=wltt, identm=ident, onesr=onesr)

    cbase = np.zeros((128, 96), dtype=np.float32)
    cbase[:, 0:16] = _col16(I["L0_attn_rms_w"])
    cbase[:, 16:32] = _col16(I["L0_attn_Wf_b"])
    cbase[:, 32:48] = _col16(I["L0_attn_Wout_b"])
    cbase[:, 48:64] = _col16(I["L1_attn_bo"])
    cbase[:, 64:80] = _col16(I["L1_ffn_bias"])
    cbase[:, 90] = 1.0
    cbase[:, 91] = 1e-5
    cbase[:, 92] = SCALE

    in_maps = []
    for c in range(8):
        bq, q = c // 4, c % 4
        x0tc = np.ascontiguousarray(x[bq, q * 1024:(q + 1) * 1024, :].T)
        cp = cbase.copy()
        for j in range(3):
            cp[:, 80 + j] = 1.0 if q > j else 0.0
            cp[:, 83 + j] = 0.0 if q > j else 1.0
        for r in range(4):
            cp[:, 86 + r] = 1.0 if (q > 0 and r == q - 1) else 0.0
        pos = np.arange(1024, dtype=np.float32) + np.float32(q * 1024)
        invf = np.exp(-np.log(np.float32(10000.0)) *
                      (np.arange(0, 64, 2, dtype=np.float32) /
                       np.float32(64.0))).astype(np.float32)
        theta = (pos[None, :] * invf[:, None]).astype(np.float32)
        ropetc = np.ascontiguousarray(
            np.concatenate([np.cos(theta), np.sin(theta)], axis=0).astype(np.float32))
        masktc = np.ascontiguousarray(np.stack(
            [mdiag, mfar, (mfar if q > 0 else mneg), (mzero if q > 0 else mneg)],
            axis=1))
        m = dict(shared)
        m.update(x0t=x0tc, cpack=cp, ropet=ropetc, maskt=masktc)
        in_maps.append(m)
    return in_maps


def kernel(**inputs):
    in_maps = make_inputs(inputs)
    nc = build()
    res = run_bass_kernel_spmd(nc, in_maps, core_ids=list(range(8)))
    out = np.empty((2, 4096, 2048), dtype=np.float32)
    for c in range(8):
        bq, q = c // 4, c % 4
        out[bq, q * 1024:(q + 1) * 1024, :] = res.results[c]["y"].T
    return out


# version 04 (final)
# speedup vs baseline: 1.1606x; 1.0498x over previous
import numpy as np
import concourse.bass as bass
import concourse.mybir as mybir
from concourse.bass_utils import run_bass_kernel_spmd

F32 = mybir.dt.float32
BF16 = mybir.dt.bfloat16
AF = mybir.ActivationFunctionType
AL = mybir.AluOpType
AX = mybir.AxisListType

D = 2048
DC = 16
TL = 1024
SC = 16
HMC = 8
E = 16
HSC = 32
HD = 128
NH = 16
NEG = -1.0e9
SCALE = float(HD) ** -0.5

ENGS = ("sync", "tensor", "vector", "scalar", "gpsimd")
ESEM = {"tensor": "tms", "vector": "vv", "scalar": "ss", "gpsimd": "gg"}
SERIAL = ("vector", "scalar", "gpsimd")


class B:
    def __init__(self, nc, sems):
        self.nc = nc
        self.st = {e: [] for e in ENGS}
        self.sems = sems
        self.cnt = {k: 0 for k in sems}

    def w(self, eng, sem, val):
        if val is None or val <= 0:
            return
        s = self.sems[sem]
        self.st[eng].append(lambda EE, s=s, v=val: EE.wait_ge(s, v))

    def op(self, eng, meth, *a, sig="auto", sigamt=1, serial=True, **kw):
        if eng in SERIAL and serial:
            self.w(eng, ESEM[eng], self.cnt[ESEM[eng]])
        if sig == "auto":
            sig = ESEM.get(eng)
        tgt, s = None, None
        if sig is not None:
            self.cnt[sig] += sigamt
            tgt = self.cnt[sig]
            s = self.sems[sig]

        import traceback
        tb = traceback.extract_stack()[-3]

        def f(EE, meth=meth, a=a, kw=kw, s=s, sigamt=sigamt, tb=tb):
            try:
                inst = getattr(EE, meth)(*a, **kw)
            except Exception:
                print(f"EMIT-FAIL {meth} from {tb.filename}:{tb.lineno}")
                raise
            if s is not None:
                inst.then_inc(s, sigamt)

        self.st[eng].append(f)
        return tgt

    def dma(self, eng, out, in_, sem=None):
        if sem is None:
            sem = "dw" if eng == "sync" else "dg"
        self.cnt[sem] += 16
        tgt = self.cnt[sem]
        s = self.sems[sem]

        def f(EE, out=out, in_=in_, s=s):
            EE.dma_start(out=out, in_=in_).then_inc(s, 16)

        self.st[eng].append(f)
        return tgt

    def barrier(self):
        snap = dict(self.cnt)
        for e in ENGS:
            for sem, val in snap.items():
                if sem != "cc":
                    self.w(e, sem, val)

    def now(self, sem):
        return self.cnt[sem]


class WS:

    def __init__(self, b, slots, lands):
        self.b = b
        self.slots = slots
        self.lands = lands
        self.loaded = []
        self.released = []
        self.land_free = [("tms", 0), ("tms", 0)]

    def _slot_wait(self, i):
        if i >= 2:
            j = i - 2
            assert j < len(self.released), f"WS: load {i} before release {j}"
            self.b.w("sync", *self.released[j])

    def load_raw(self, src_ap):
        i = len(self.loaded)
        self._slot_wait(i)
        ncols = 1
        for d in src_ap.shape[1:]:
            ncols *= d
        dst = self.slots[i % 2].ap().bitcast(F32)[:, 0:ncols]
        t = self.b.dma("sync", dst, src_ap, sem=("dwa", "dwb")[i % 2])
        self.loaded.append((("dwa", "dwb")[i % 2], t, "raw"))
        return i

    def load_split(self, src_ap):
        i = len(self.loaded)
        self._slot_wait(i)
        ncols = 1
        for d in src_ap.shape[1:]:
            ncols *= d
        li = i % 2
        self.b.w("sync", *self.land_free[li])
        t = self.b.dma("sync", self.lands[li][:, 0:ncols], src_ap,
                       sem=("dwc", "dwd")[li])
        slot = self.slots[i % 2]
        if i >= 2:
            self.b.w("scalar", *self.released[i - 2])
        self.b.w("scalar", ("dwc", "dwd")[li], t)
        hi = self.b.op("scalar", "copy", out=slot[:, 0:ncols],
                       in_=self.lands[li][:, 0:ncols])
        self.b.w("vector", "ss", hi)
        lo = self.b.op("vector", "tensor_sub", out=slot[:, 1024:1024 + ncols],
                       in0=self.lands[li][:, 0:ncols], in1=slot[:, 0:ncols])
        self.land_free[li] = ("vv", lo)
        self.loaded.append(("vv", lo, "split"))
        return i

    def use(self, i, eng="tensor"):
        sem, t, kind = self.loaded[i]
        self.b.w(eng, sem, t)
        return self.slots[i % 2]

    def useF(self, i, eng="tensor"):
        sem, t, kind = self.loaded[i]
        assert kind == "raw"
        self.b.w(eng, sem, t)
        return self.slots[i % 2].ap().bitcast(F32)

    def release(self, i, sem, val):
        assert i == len(self.released), "WS: release out of order"
        self.released.append((sem, val))


def build(tap=()):
    nc = bass.Bass("TRN2", target_bir_lowering=False)

    def dram_in(name, shape):
        return nc.dram_tensor(name, list(shape), F32, kind="ExternalInput")

    x0t = dram_in("x0t", (D, TL))
    wl0 = dram_in("wl0", (5, SC, 128, D))
    woutt = dram_in("woutt", (SC, 128, D))
    wrout = dram_in("wrout", (128, DC * 16))
    we1t = dram_in("we1t", (E, HMC, 128, D))
    we3t = dram_in("we3t", (E, HMC, 128, D))
    we2t = dram_in("we2t", (E, DC, 128, HMC * 128))
    ws1t = dram_in("ws1t", (HSC, 128, D))
    ws3t = dram_in("ws3t", (HSC, 128, D))
    ws2t = dram_in("ws2t", (2, HSC, 128, 8 * 128))
    wqt = dram_in("wqt", (NH, 128, D))
    wkt = dram_in("wkt", (128, D))
    wvt = dram_in("wvt", (128, D))
    wot = dram_in("wot", (DC, 128, D))
    wltt = dram_in("wltt", (DC, 128, D))
    cpack = dram_in("cpack", (128, 96))
    ident = dram_in("identm", (128, 128))
    onesr = dram_in("onesr", (1, 128))
    ropet = dram_in("ropet", (64, TL))
    maskt = dram_in("maskt", (128, 4, 128))

    y = nc.dram_tensor("y", [D, TL], F32, kind="ExternalOutput")
    taps = {t: nc.dram_tensor(t, [D, TL], F32, kind="ExternalOutput")
            for t in tap if not t.startswith("q")}
    if "xdbg" in tap:
        taps["xh"] = nc.dram_tensor("xh", [D, TL], F32, kind="ExternalOutput")
        taps["xl"] = nc.dram_tensor("xl", [D, TL], F32, kind="ExternalOutput")
        taps["og0"] = nc.dram_tensor("og0", [128, TL], F32, kind="ExternalOutput")
        taps["f0"] = nc.dram_tensor("f0", [128, TL], F32, kind="ExternalOutput")
    if "qdbg" in tap:
        taps["qdbg"] = nc.dram_tensor("qdbg", [128, 2048], F32, kind="ExternalOutput")
        taps["thdbg"] = nc.dram_tensor("thdbg", [128, 128], F32, kind="ExternalOutput")
        taps["adbg"] = nc.dram_tensor("adbg", [128, 2048], F32, kind="ExternalOutput")
        taps["bdbg"] = nc.dram_tensor("bdbg", [128, 2048], F32, kind="ExternalOutput")

    cc1i = nc.dram_tensor("cc1i", [128, 32], F32)
    cc1o = nc.dram_tensor("cc1o", [512, 32], F32)
    cc2i = nc.dram_tensor("cc2i", [8, 128, 128], F32)
    cc2o = nc.dram_tensor("cc2o", [32, 128, 128], F32)

    C_RMSW, C_WFB, C_WOUTB, C_BO, C_LB = 0, 16, 32, 48, 64
    C_SCM, C_SCM1, C_HSEL, C_ONE = 80, 83, 86, 90

    import contextlib
    es = contextlib.ExitStack()
    with es:
        resA = es.enter_context(nc.sbuf_tensor("resA", [128, DC, TL], F32))
        bigA = es.enter_context(nc.sbuf_tensor("bigA", [128, 24, TL], F32))
        pool = es.enter_context(nc.sbuf_tensor("pool", [128, 5, TL], F32))
        wsl0 = es.enter_context(nc.sbuf_tensor("wsl0", [128, 2048], BF16))
        wsl1 = es.enter_context(nc.sbuf_tensor("wsl1", [128, 2048], BF16))
        wlg0 = es.enter_context(nc.sbuf_tensor("wlg0", [128, 1024], F32))
        wlg1 = es.enter_context(nc.sbuf_tensor("wlg1", [128, 1024], F32))
        rowA = es.enter_context(nc.sbuf_tensor("rowA", [1, TL], F32))
        rowB = es.enter_context(nc.sbuf_tensor("rowB", [1, TL], F32))
        smisc = es.enter_context(nc.sbuf_tensor("smisc", [128, 112], F32))
        cpk = es.enter_context(nc.sbuf_tensor("cpk", [128, 96], F32))
        identS = es.enter_context(nc.sbuf_tensor("identS", [128, 128], F32))
        onesrS = es.enter_context(nc.sbuf_tensor("onesrS", [1, 128], F32))
        rtmp = es.enter_context(nc.sbuf_tensor("rtmp", [64, 128], F32))
        wrS = es.enter_context(nc.sbuf_tensor("wrS", [128, DC * 16], F32))
        cwS = es.enter_context(nc.sbuf_tensor("cwS", [128, 8, 16], F32))
        P0 = es.enter_context(nc.psum_tensor("P0", [128, 512], F32))
        P1 = es.enter_context(nc.psum_tensor("P1", [128, 512], F32))
        P2 = es.enter_context(nc.psum_tensor("P2", [128, 512], F32))
        P3 = es.enter_context(nc.psum_tensor("P3", [128, 512], F32))
        P4 = es.enter_context(nc.psum_tensor("P4", [128, 512], F32))
        P5 = es.enter_context(nc.psum_tensor("P5", [128, 512], F32))
        P6 = es.enter_context(nc.psum_tensor("P6", [128, 512], F32))
        P7 = es.enter_context(nc.psum_tensor("P7", [128, 512], F32))
        s_dw = es.enter_context(nc.semaphore("dw"))
        s_dwa = es.enter_context(nc.semaphore("dwa"))
        s_dwb = es.enter_context(nc.semaphore("dwb"))
        s_dwc = es.enter_context(nc.semaphore("dwc"))
        s_dwd = es.enter_context(nc.semaphore("dwd"))
        s_dg = es.enter_context(nc.semaphore("dg"))
        s_tms = es.enter_context(nc.semaphore("tms"))
        s_vv = es.enter_context(nc.semaphore("vv"))
        s_ss = es.enter_context(nc.semaphore("ss"))
        s_gg = es.enter_context(nc.semaphore("gg"))
        s_cc = es.enter_context(nc.semaphore("cc"))
        block = es.enter_context(nc.Block())

        P = [P0, P1, P2, P3, P4, P5, P6, P7]
        sems = dict(dw=s_dw, dwa=s_dwa, dwb=s_dwb, dwc=s_dwc, dwd=s_dwd, dg=s_dg, tms=s_tms, vv=s_vv, ss=s_ss, gg=s_gg, cc=s_cc)
        b = B(nc, sems)
        ws = WS(b, [wsl0, wsl1], [wlg0, wlg1])

        bf3 = bigA.ap().rearrange("p a t -> p (a t)").bitcast(BF16) \
            .rearrange("p (a t) -> p a t", a=48)
        ogT = bf3[:, 0:SC, :]
        f16 = bf3[:, SC:2 * SC, :]
        z16 = bf3[:, 2 * SC:3 * SC, :]
        auxB = bigA[:, 0:8, :].rearrange("p a t -> p (a t)")

        def aux(off, n):
            return auxB[:, off:off + n]

        sq_t = pool[:, 0, :]
        mbc = pool[:, 1, :]
        rbc = pool[:, 2, :]
        srmsbc = pool[:, 3, :]
        x0r = pool[:, 4, :]
        stg = pool[:, 0, :]
        eptm = pool[:, 1, :]
        scanb = pool[:, 2, :].rearrange("p t -> p t").bitcast(BF16)
        gsmA = pool[:, 4, :]
        cwT = pool[0:16, 3, :]
        cwbc = pool[:, 4, :]
        ropeS = pool[0:64, 3, :]
        maskC = pool[:, 4, 0:512].rearrange("p (a m) -> p a m", a=4)
        gsm = pool[:, 4, 512:1024]

        rmsw = cpk[:, C_RMSW:C_RMSW + 16]
        wfb = cpk[:, C_WFB:C_WFB + 16]
        woutb = cpk[:, C_WOUTB:C_WOUTB + 16]
        bo = cpk[:, C_BO:C_BO + 16]
        lb = cpk[:, C_LB:C_LB + 16]
        scm = cpk[:, C_SCM:C_SCM + 3]
        scm1 = cpk[:, C_SCM1:C_SCM1 + 3]
        hsel = cpk[:, C_HSEL:C_HSEL + 4]
        onesc = cpk[:, C_ONE:C_ONE + 1]
        nc.const_aps.register(F32, 1e-5, cpk[:, 91:92])
        nc.const_aps.register(F32, SCALE, cpk[:, 92:93])
        invw = smisc[:, 0:16]
        hsum = smisc[:, 16:32]
        psum_ = smisc[:, 32:48]
        carry = smisc[:, 48:64]
        ctmp1 = smisc[:, 64:80]
        ctmp2 = smisc[:, 80:96]
        ctmp3 = smisc[:, 96:112]

        resT = [resA[:, c, :] for c in range(DC)]
        resBF = resA.ap().rearrange("p a t -> p (a t)").bitcast(BF16) \
            .rearrange("p (a t) -> p a t", a=32)
        resT2 = [bigA[:, 8 + c, :] for c in range(DC)]
        xnH = [resBF[:, 2 * c, :] for c in range(DC)]
        xnL = [resBF[:, 2 * c + 1, :] for c in range(DC)]

        def wl0_wout(sc, dhalf):
            return woutt.ap()[sc, :, dhalf * 1024:(dhalf + 1) * 1024]

        t_x0 = b.dma("sync", resA[:], x0t.ap().rearrange("(c p) n -> p c n", p=128))
        t_cp = b.dma("gpsimd", cpk[:], cpack.ap())
        t_id = b.dma("gpsimd", identS[:], ident.ap())
        t_or = b.dma("gpsimd", onesrS[:], onesr.ap())
        t_wr = b.dma("gpsimd", wrS[:], wrout.ap())
        b.w("vector", "dg", t_wr)
        b.op("vector", "reciprocal", out=invw, in_=rmsw)
        for e in ENGS:
            b.w(e, "dw", t_x0)
            b.w(e, "dg", t_wr)

        def layernorm(src, mode):
            lastsq = None
            for c in range(DC):
                if lastsq is not None:
                    b.w("scalar", "tms", lastsq)
                sst = b.op("scalar", "activation", out=sq_t, in_=src[c], func=AF.Square)
                for hf in range(2):
                    b.op("tensor", "matmul", out=P[4 + hf][0:1, :], lhsT=onesc,
                         rhs=src[c][:, hf * 512:(hf + 1) * 512],
                         start=(c == 0), stop=(c == DC - 1), sig=None)
                b.w("tensor", "ss", sst)
                for hf in range(2):
                    lastsq = b.op("tensor", "matmul", out=P[6 + hf][0:1, :], lhsT=onesc,
                                  rhs=sq_t[:, hf * 512:(hf + 1) * 512],
                                  start=(c == 0), stop=(c == DC - 1))
            b.w("vector", "tms", lastsq)
            for hf in range(2):
                b.op("vector", "tensor_copy", out=rowA[0:1, hf * 512:(hf + 1) * 512],
                     in_=P[4 + hf][0:1, :])
                b.op("vector", "tensor_copy", out=rowB[0:1, hf * 512:(hf + 1) * 512],
                     in_=P[6 + hf][0:1, :])
            b.op("vector", "tensor_scalar_mul", out=rowA[0:1, :], in0=rowA[0:1, :],
                 scalar1=1.0 / D)
            b.op("vector", "tensor_scalar_mul", out=rowB[0:1, :], in0=rowB[0:1, :],
                 scalar1=1.0 / D)
            b.op("vector", "tensor_mul", out=sq_t[0:1, :], in0=rowA[0:1, :],
                 in1=rowA[0:1, :])
            v1 = b.op("vector", "tensor_sub", out=rowB[0:1, :], in0=rowB[0:1, :],
                      in1=sq_t[0:1, :])
            b.w("scalar", "vv", v1)
            s1 = b.op("scalar", "activation", out=rowB[0:1, :], in_=rowB[0:1, :],
                      func=AF.Sqrt, bias=1e-5)
            b.w("vector", "ss", s1)
            v2 = b.op("vector", "reciprocal", out=rowB[0:1, :], in_=rowB[0:1, :])
            b.w("tensor", "vv", v2)
            for hf in range(2):
                b.op("tensor", "matmul", out=P[4 + hf][:, :], lhsT=onesrS[:],
                     rhs=rowA[0:1, hf * 512:(hf + 1) * 512], start=True, stop=True,
                     sig=None)
                tb_ = b.op("tensor", "matmul", out=P[6 + hf][:, :], lhsT=onesrS[:],
                           rhs=rowB[0:1, hf * 512:(hf + 1) * 512], start=True,
                           stop=True)
            b.w("vector", "tms", tb_)
            for hf in range(2):
                b.op("vector", "tensor_copy", out=mbc[:, hf * 512:(hf + 1) * 512],
                     in_=P[4 + hf][:, :])
                vbc = b.op("vector", "tensor_copy", out=rbc[:, hf * 512:(hf + 1) * 512],
                           in_=P[6 + hf][:, :])
            if mode == "f32":
                b.w("gpsimd", "vv", vbc)
                for c in range(DC):
                    eng = "vector" if c % 2 == 0 else "gpsimd"
                    b.op(eng, "tensor_sub", out=resA[:, c, :], in0=src[c], in1=mbc)
                    b.op(eng, "tensor_mul", out=resA[:, c, :], in0=resA[:, c, :],
                         in1=rbc)
            else:
                for c in range(DC):
                    b.op("vector", "tensor_sub", out=sq_t, in0=src[c], in1=mbc)
                    v1 = b.op("vector", "tensor_mul", out=sq_t, in0=sq_t, in1=rbc)
                    b.w("scalar", "vv", v1)
                    hi = b.op("scalar", "copy", out=xnH[c], in_=sq_t)
                    b.w("vector", "ss", hi)
                    b.op("vector", "tensor_sub", out=xnL[c], in0=sq_t, in1=xnH[c])
            return b.now("vv"), b.now("gg")

        vA, gA = layernorm(resT, "f32")

        b.w("scalar", "vv", vA)
        b.w("scalar", "gg", gA)
        b.w("tensor", "vv", vA)
        b.w("tensor", "gg", gA)
        lastsq = None
        for c in range(DC):
            if lastsq is not None:
                b.w("scalar", "tms", lastsq)
            sst = b.op("scalar", "activation", out=sq_t, in_=resT[c], func=AF.Square)
            b.w("tensor", "ss", sst)
            for hf in range(2):
                lastsq = b.op("tensor", "matmul", out=P[6 + hf][0:1, :], lhsT=onesc,
                              rhs=sq_t[:, hf * 512:(hf + 1) * 512],
                              start=(c == 0), stop=(c == DC - 1))
        b.w("vector", "tms", lastsq)
        for hf in range(2):
            b.op("vector", "tensor_copy", out=rowB[0:1, hf * 512:(hf + 1) * 512],
                 in_=P[6 + hf][0:1, :])
        v1 = b.op("vector", "tensor_scalar_mul", out=rowB[0:1, :], in0=rowB[0:1, :],
                  scalar1=1.0 / D)
        b.w("scalar", "vv", v1)
        s1 = b.op("scalar", "activation", out=rowA[0:1, :], in_=rowB[0:1, :],
                  func=AF.Sqrt, bias=1e-5)
        b.w("vector", "ss", s1)
        v2 = b.op("vector", "reciprocal", out=rowB[0:1, :], in_=rowA[0:1, :])
        b.w("tensor", "vv", v2)
        for hf in range(2):
            b.op("tensor", "matmul", out=P[4 + hf][:, :], lhsT=onesrS[:],
                 rhs=rowA[0:1, hf * 512:(hf + 1) * 512], start=True, stop=True,
                 sig=None)
            tb_ = b.op("tensor", "matmul", out=P[6 + hf][:, :], lhsT=onesrS[:],
                       rhs=rowB[0:1, hf * 512:(hf + 1) * 512], start=True, stop=True)
        b.w("vector", "tms", tb_)
        for hf in range(2):
            b.op("vector", "tensor_copy", out=srmsbc[:, hf * 512:(hf + 1) * 512],
                 in_=P[4 + hf][:, :])
            vbc = b.op("vector", "tensor_copy", out=rbc[:, hf * 512:(hf + 1) * 512],
                       in_=P[6 + hf][:, :])
        for c in range(DC):
            v1 = b.op("vector", "scalar_tensor_tensor", out=sq_t, in0=resT[c],
                      scalar=rmsw[:, c:c + 1], in1=rbc, op0=AL.mult, op1=AL.mult)
            b.w("scalar", "vv", v1)
            hi = b.op("scalar", "copy", out=resBF[:, 2 * c, :], in_=sq_t)
            b.w("vector", "ss", hi)
            b.op("vector", "tensor_sub", out=resBF[:, 2 * c + 1, :], in0=sq_t,
                 in1=resBF[:, 2 * c, :])
        xnV, xnG = b.now("vv"), b.now("gg")

        NSL = SC * 5 * 2
        bank_free = {i: ("tms", 0) for i in range(6)}
        bank_free[4] = ("vv", vbc)
        bank_free[5] = ("vv", vbc)
        silu_ga = None
        sig_zg = None
        scan_last = 0
        b.w("tensor", "vv", xnV)
        b.w("tensor", "ss", b.now("ss"))
        for s_ in range(SC):
            for p in range(5):
                k = s_ * 5 + p
                bp = (k % 3) * 2
                for half in range(2):
                    kk = k * 2 + half
                    if kk == 0:
                        ws.load_split(wl0.ap()[0, 0, :, 0:1024])
                    if kk + 1 < NSL:
                        k2 = kk + 1
                        p2, s2, h2 = (k2 // 2) % 5, k2 // 10, k2 % 2
                        ws.load_split(wl0.ap()[p2, s2, :, h2 * 1024:(h2 + 1) * 1024])
                    slot = ws.use(kk)
                    for hf in range(2):
                        if half == 0 and hf == 0:
                            b.w("tensor", *bank_free[bp])
                            b.w("tensor", *bank_free[bp + 1])
                        for lc in range(8):
                            c = half * 8 + lc
                            whi = slot[:, lc * 128:(lc + 1) * 128]
                            wlo = slot[:, 1024 + lc * 128:1024 + (lc + 1) * 128]
                            xh = resBF[:, 2 * c, hf * 512:(hf + 1) * 512]
                            xl = resBF[:, 2 * c + 1, hf * 512:(hf + 1) * 512]
                            st = (half == 0 and lc == 0)
                            sp = (half == 1 and lc == 7)
                            b.op("tensor", "matmul", out=P[bp + hf][:, :],
                                 lhsT=whi, rhs=xh, start=st, stop=False, sig=None)
                            b.op("tensor", "matmul", out=P[bp + hf][:, :],
                                 lhsT=whi, rhs=xl, start=False, stop=False, sig=None)
                            b.op("tensor", "matmul", out=P[bp + hf][:, :],
                                 lhsT=wlo, rhs=xh, start=False, stop=sp,
                                 sig=("auto" if (lc == 7 and hf == 1) else None))
                    tmm = b.now("tms")
                    ws.release(kk, "tms", tmm)
                tmm = b.now("tms")
                if p == 0:
                    b.w("scalar", "tms", tmm)
                    b.w("scalar", "vv", b.now("vv"))
                    for hf in range(2):
                        sv = b.op("scalar", "activation",
                                  out=sq_t[:, hf * 512:(hf + 1) * 512],
                                  in_=P[bp + hf][:, :], func=AF.Silu)
                    silu_ga = sv
                    bank_free[bp] = ("ss", sv)
                    bank_free[bp + 1] = ("ss", sv)
                elif p == 1:
                    b.w("vector", "tms", tmm)
                    b.w("vector", "ss", silu_ga)
                    for hf in range(2):
                        vv1 = b.op("vector", "tensor_mul",
                                   out=ogT[:, s_, hf * 512:(hf + 1) * 512],
                                   in0=sq_t[:, hf * 512:(hf + 1) * 512],
                                   in1=P[bp + hf][:, :])
                    bank_free[bp] = ("vv", vv1)
                    bank_free[bp + 1] = ("vv", vv1)
                elif p == 2:
                    b.w("scalar", "tms", tmm)
                    for hf in range(2):
                        sv = b.op("scalar", "activation",
                                  out=f16[:, s_, hf * 512:(hf + 1) * 512],
                                  in_=P[bp + hf][:, :], func=AF.Sigmoid,
                                  bias=wfb[:, s_:s_ + 1])
                    bank_free[bp] = ("ss", sv)
                    bank_free[bp + 1] = ("ss", sv)
                elif p == 3:
                    b.w("scalar", "tms", tmm)
                    b.w("scalar", "vv", b.now("vv"))
                    for hf in range(2):
                        sv = b.op("scalar", "activation",
                                  out=mbc[:, hf * 512:(hf + 1) * 512],
                                  in_=P[bp + hf][:, :], func=AF.Sigmoid)
                    sig_zg = sv
                    bank_free[bp] = ("ss", sv)
                    bank_free[bp + 1] = ("ss", sv)
                else:
                    b.w("vector", "tms", tmm)
                    b.w("vector", "ss", sig_zg)
                    for hf in range(2):
                        vv1 = b.op("vector", "tensor_mul",
                                   out=z16[:, s_, hf * 512:(hf + 1) * 512],
                                   in0=mbc[:, hf * 512:(hf + 1) * 512],
                                   in1=P[bp + hf][:, :])
                    bank_free[bp] = ("vv", vv1)
                    bank_free[bp + 1] = ("vv", vv1)
                    b.op("vector", "tensor_tensor_scan", out=scanb[:, 0:TL],
                         data0=f16[:, s_, :], data1=z16[:, s_, :], initial=0.0,
                         op0=AL.mult, op1=AL.add)
                    b.op("vector", "tensor_copy", out=hsum[:, s_:s_ + 1],
                         in_=scanb[:, TL - 1:TL])
                    b.op("vector", "tensor_tensor_scan", out=scanb[:, TL:2 * TL],
                         data0=f16[:, s_, :], data1=f16[:, s_, :], initial=1.0,
                         op0=AL.mult, op1=AL.bypass)
                    scan_last = b.op("vector", "tensor_copy", out=psum_[:, s_:s_ + 1],
                                     in_=scanb[:, 2 * TL - 1:2 * TL])

        if "xh" in taps:
            b.w("gpsimd", "vv", b.now("vv"))
            b.w("gpsimd", "ss", b.now("ss"))
            tvx = None
            for c in range(DC):
                tvx = b.dma("gpsimd", taps["xh"].ap().rearrange(
                    "(c p) n -> p c n", p=128)[:, c, :], resBF[:, 2 * c, :])
                tvx = b.dma("gpsimd", taps["xl"].ap().rearrange(
                    "(c p) n -> p c n", p=128)[:, c, :], resBF[:, 2 * c + 1, :])
            tvx = b.dma("gpsimd", taps["og0"].ap(), ogT[:, 0, :])
            tvx = b.dma("gpsimd", taps["f0"].ap(), f16[:, 0, :])
            b.w("gpsimd", "dg", tvx)
        b.w("gpsimd", "vv", scan_last)
        tgi = b.dma("gpsimd", cc1i.ap(), smisc[:, 16:48])
        b.w("gpsimd", "dg", tgi)
        ccv = b.op("gpsimd", "collective_compute", "AllGather", AL.bypass,
                   replica_groups=[[0, 1, 2, 3], [4, 5, 6, 7]],
                   ins=[cc1i.ap().opt()], outs=[cc1o.ap().opt()],
                   sig="cc", serial=False)
        b.w("gpsimd", "cc", ccv)
        tg = None
        for r in range(3):
            tg = b.dma("gpsimd", gsmA[:, r * 32:(r + 1) * 32],
                       cc1o.ap().rearrange("(r p) c -> p r c", p=128)[:, r, :])
        b.w("vector", "dg", tg)
        b.op("vector", "tensor_scalar_mul", out=carry, in0=gsmA[:, 0:16],
             scalar1=scm[:, 0:1])
        cdone = None
        for j in (1, 2):
            b.op("vector", "tensor_scalar_mul", out=ctmp1, in0=gsmA[:, j * 32:j * 32 + 16],
                 scalar1=scm[:, j:j + 1])
            b.op("vector", "tensor_scalar", out=ctmp2, in0=gsmA[:, j * 32 + 16:j * 32 + 32],
                 scalar1=scm[:, j:j + 1], scalar2=scm1[:, j:j + 1],
                 op0=AL.mult, op1=AL.add)
            b.op("vector", "tensor_mul", out=carry, in0=carry, in1=ctmp2)
            cdone = b.op("vector", "tensor_add", out=carry, in0=carry, in1=ctmp1)

        gate_done = []
        for s in range(SC):
            b.op("vector", "tensor_tensor_scan", out=z16[:, s, :],
                 data0=f16[:, s, :], data1=z16[:, s, :],
                 initial=carry[:, s:s + 1], op0=AL.mult, op1=AL.add)
            gate_done.append(b.op("vector", "tensor_mul", out=ogT[:, s, :],
                                  in0=ogT[:, s, :], in1=z16[:, s, :]))

        x0v = x0t.ap().rearrange("(c p) n -> p c n", p=128)
        b.w("gpsimd", "vv", cdone)
        ep_v = 0
        wk_i = len(ws.loaded)
        for dhalf in range(2):
            for tb in range(2):
                tok = slice(tb * 512, (tb + 1) * 512)
                lastd = None
                for sc in range(SC):
                    kk = wk_i + (dhalf * 2 + tb) * SC + sc
                    if kk == wk_i:
                        ws.load_split(wl0_wout(0, dhalf))
                    if sc + 1 < SC or dhalf * 2 + tb < 3:
                        nidx = (dhalf * 2 + tb) * SC + sc + 1
                        nd, nsc = divmod(nidx, SC)
                        ws.load_split(wl0_wout(nsc, (nd // 2)))
                    slot = ws.use(kk)
                    b.w("tensor", "vv", gate_done[sc])
                    if sc == 0:
                        b.w("tensor", "vv", ep_v)
                    for di in range(8):
                        b.op("tensor", "matmul", out=P[di][:, :],
                             lhsT=slot[:, di * 128:(di + 1) * 128],
                             rhs=ogT[:, sc, tok],
                             start=(sc == 0), stop=False, sig=None)
                        lastd = b.op("tensor", "matmul", out=P[di][:, :],
                                     lhsT=slot[:, 1024 + di * 128:1024 + (di + 1) * 128],
                                     rhs=ogT[:, sc, tok],
                                     start=False, stop=(sc == SC - 1),
                                     sig=(None if di < 7 else "auto"))
                    ws.release(kk, "tms", lastd)
                for di in range(8):
                    d = dhalf * 8 + di
                    b.w("gpsimd", "vv", b.now("vv"))
                    tx0 = b.dma("gpsimd", x0r[:, (di % 2) * 512:(di % 2 + 1) * 512],
                                x0v[:, d, tok])
                    g0 = b.op("vector", "tensor_add", out=eptm[:, 0:512],
                              in0=resBF[:, 2 * d, tok], in1=resBF[:, 2 * d + 1, tok])
                    g1 = b.op("vector", "scalar_tensor_tensor", out=eptm[:, 0:512],
                              in0=eptm[:, 0:512], scalar=invw[:, d:d + 1],
                              in1=srmsbc[:, tok], op0=AL.mult, op1=AL.mult)
                    b.w("vector", "dg", tx0)
                    g2 = b.op("vector", "tensor_add", out=eptm[:, 0:512],
                              in0=eptm[:, 0:512],
                              in1=x0r[:, (di % 2) * 512:(di % 2 + 1) * 512])
                    b.w("vector", "tms", lastd)
                    ep_v = b.op("vector", "scalar_tensor_tensor",
                                out=resT2[d][:, tok],
                                in0=P[di][:, :], scalar=woutb[:, d:d + 1],
                                in1=eptm[:, 0:512], op0=AL.add, op1=AL.add)
        b.barrier()
        if "x1" in taps:
            tv = None
            for c in range(DC):
                tv = b.dma("gpsimd", taps["x1"].ap().rearrange(
                    "(c p) n -> p c n", p=128)[:, c, :], resT2[c])
            b.w("gpsimd", "dg", tv)
            b.barrier()

        layernorm(resT2, "split")
        b.barrier()

        wrHL = pool[:, 2, 0:256].rearrange("p t -> p t").bitcast(BF16)
        h1 = b.op("scalar", "copy", out=wrHL[:, 0:256], in_=wrS[:])
        b.w("vector", "ss", h1)
        h2 = b.op("vector", "tensor_sub", out=wrHL[:, 256:512], in0=wrS[:],
                  in1=wrHL[:, 0:256])

        vcwT = 0
        b.w("tensor", "vv", h2)
        for tt in range(8):
            pb = P[6 + (tt % 2)]
            b.w("tensor", "vv", b.now("vv"))
            for c in range(DC):
                xh = xnH[c][:, tt * 128:(tt + 1) * 128]
                xl = xnL[c][:, tt * 128:(tt + 1) * 128]
                wh = wrHL[:, c * 16:(c + 1) * 16]
                wl = wrHL[:, 256 + c * 16:256 + (c + 1) * 16]
                b.op("tensor", "matmul", out=pb[:, 0:16], lhsT=xh, rhs=wh,
                     start=(c == 0), stop=False, sig=None)
                b.op("tensor", "matmul", out=pb[:, 0:16], lhsT=xh, rhs=wl,
                     start=False, stop=False, sig=None)
                b.op("tensor", "matmul", out=pb[:, 0:16], lhsT=xl, rhs=wh,
                     start=False, stop=(c == DC - 1),
                     sig=(None if c < DC - 1 else "auto"))
            tmm = b.now("tms")
            b.w("vector", "tms", tmm)
            cwt = cwS[:, tt, :]
            mx = ctmp1[:, 0:1]
            b.op("vector", "reduce_max", out=mx, in_=pb[:, 0:16], axis=AX.X,
                 negate=True)
            vv1 = b.op("vector", "tensor_copy", out=cwt, in_=pb[:, 0:16])
            b.w("scalar", "vv", vv1)
            sx = b.op("scalar", "activation", out=cwt, in_=cwt, func=AF.Exp, bias=mx)
            b.w("vector", "ss", sx)
            sm = ctmp1[:, 1:2]
            b.op("vector", "reduce_sum", out=sm, in_=cwt, axis=AX.X)
            b.op("vector", "reciprocal", out=sm, in_=sm)
            b.op("vector", "tensor_scalar_mul", out=cwt, in0=cwt, scalar1=sm)
            cur = ctmp2
            b.op("vector", "tensor_copy", out=cur, in_=cwt)
            th = ctmp1[:, 2:3]
            for it in range(4):
                b.op("vector", "reduce_max", out=th, in_=cur, axis=AX.X)
                if it < 3:
                    b.op("vector", "tensor_single_scalar", out=ctmp3, in_=cur,
                         scalar=th, op=AL.is_ge)
                    b.op("vector", "tensor_scalar_mul", out=ctmp3, in0=ctmp3,
                         scalar1=1e30)
                    b.op("vector", "tensor_sub", out=cur, in0=cur, in1=ctmp3)
            b.op("vector", "tensor_single_scalar", out=cur, in_=cwt, scalar=th,
                 op=AL.is_ge)
            b.op("vector", "tensor_mul", out=cwt, in0=cwt, in1=cur)
            s4 = ctmp1[:, 1:2]
            b.op("vector", "reduce_sum", out=s4, in_=cwt, axis=AX.X)
            b.op("vector", "reciprocal", out=s4, in_=s4)
            vcw = b.op("vector", "tensor_scalar_mul", out=cwt, in0=cwt, scalar1=s4)
            b.w("tensor", "vv", vcw)
            ttr = b.op("tensor", "transpose", out=P[4 + (tt % 2)][0:16, 0:128],
                       in_=cwt, identity=identS[:])
            b.w("vector", "tms", ttr)
            vcwT = b.op("vector", "tensor_copy", out=cwT[:, tt * 128:(tt + 1) * 128],
                        in_=P[4 + (tt % 2)][0:16, 0:128])
        b.barrier()

        gbufF = aux(0, 8 * TL)
        gbufBF = gbufF.bitcast(BF16)
        gful = pool[:, 2, :]
        ep_sh = ("vv", 0)
        for tq in range(4):
            tok = slice(tq * 256, (tq + 1) * 256)
            glu_done = [0] * HSC
            for hs in range(HSC):
                k1 = ws.load_split(ws1t.ap()[hs, :, 0:1024])
                k1b = ws.load_split(ws1t.ap()[hs, :, 1024:2048])
                for ki, half in ((k1, 0), (k1b, 1)):
                    slot = ws.use(ki)
                    if half == 0:
                        b.w("tensor", *ep_sh)
                    for lc in range(8):
                        c = half * 8 + lc
                        wh = slot[:, lc * 128:(lc + 1) * 128]
                        wl = slot[:, 1024 + lc * 128:1024 + (lc + 1) * 128]
                        b.op("tensor", "matmul", out=P0[:, 0:256], lhsT=wh,
                             rhs=xnH[c][:, tok], start=(c == 0), stop=False, sig=None)
                        b.op("tensor", "matmul", out=P0[:, 0:256], lhsT=wh,
                             rhs=xnL[c][:, tok], start=False, stop=False, sig=None)
                        b.op("tensor", "matmul", out=P0[:, 0:256], lhsT=wl,
                             rhs=xnH[c][:, tok], start=False,
                             stop=(c == DC - 1),
                             sig=("auto" if lc == 7 else None))
                    ws.release(ki, "tms", b.now("tms"))
                t1 = b.now("tms")
                k3 = ws.load_split(ws3t.ap()[hs, :, 0:1024])
                k3b = ws.load_split(ws3t.ap()[hs, :, 1024:2048])
                for ki, half in ((k3, 0), (k3b, 1)):
                    slot = ws.use(ki)
                    for lc in range(8):
                        c = half * 8 + lc
                        wh = slot[:, lc * 128:(lc + 1) * 128]
                        wl = slot[:, 1024 + lc * 128:1024 + (lc + 1) * 128]
                        b.op("tensor", "matmul", out=P1[:, 0:256], lhsT=wh,
                             rhs=xnH[c][:, tok], start=(c == 0), stop=False, sig=None)
                        b.op("tensor", "matmul", out=P1[:, 0:256], lhsT=wh,
                             rhs=xnL[c][:, tok], start=False, stop=False, sig=None)
                        b.op("tensor", "matmul", out=P1[:, 0:256], lhsT=wl,
                             rhs=xnH[c][:, tok], start=False,
                             stop=(c == DC - 1),
                             sig=("auto" if lc == 7 else None))
                    ws.release(ki, "tms", b.now("tms"))
                t3 = b.now("tms")
                b.w("scalar", "tms", t1)
                b.w("scalar", "vv", b.now("vv"))
                ssl = b.op("scalar", "activation", out=sq_t[:, 0:256],
                           in_=P0[:, 0:256], func=AF.Silu)
                b.w("vector", "ss", ssl)
                b.w("vector", "tms", t3)
                vg = b.op("vector", "tensor_mul", out=gful[:, 0:256],
                          in0=sq_t[:, 0:256], in1=P1[:, 0:256])
                b.w("scalar", "vv", vg)
                gh = b.op("scalar", "copy", out=gbufBF[:, hs * 512:hs * 512 + 256],
                          in_=gful[:, 0:256])
                b.w("vector", "ss", gh)
                glu_done[hs] = b.op("vector", "tensor_sub",
                                    out=gbufBF[:, hs * 512 + 256:hs * 512 + 512],
                                    in0=gful[:, 0:256],
                                    in1=gbufBF[:, hs * 512:hs * 512 + 256])
                ep_sh = ("vv", glu_done[hs])
            for dh in range(2):
                lastd = None
                for hs in range(HSC):
                    ki = ws.load_split(ws2t.ap()[dh, hs, :, 0:1024])
                    slot = ws.use(ki)
                    b.w("tensor", "vv", glu_done[hs])
                    if hs == 0:
                        b.w("tensor", "vv", b.now("vv"))
                        b.w("tensor", "ss", b.now("ss"))
                    gh = gbufBF[:, hs * 512:hs * 512 + 256]
                    gl = gbufBF[:, hs * 512 + 256:hs * 512 + 512]
                    for di in range(8):
                        wh = slot[:, di * 128:(di + 1) * 128]
                        wl = slot[:, 1024 + di * 128:1024 + (di + 1) * 128]
                        b.op("tensor", "matmul", out=P[di][:, 0:256], lhsT=wh,
                             rhs=gh, start=(hs == 0), stop=False, sig=None)
                        b.op("tensor", "matmul", out=P[di][:, 0:256], lhsT=wh,
                             rhs=gl, start=False, stop=False, sig=None)
                        lastd = b.op("tensor", "matmul", out=P[di][:, 0:256],
                                     lhsT=wl, rhs=gh, start=False,
                                     stop=(hs == HSC - 1),
                                     sig=(None if di < 7 else "auto"))
                    ws.release(ki, "tms", lastd)
                for di in range(8):
                    d = dh * 8 + di
                    b.w("vector", "tms", lastd)
                    epv = b.op("vector", "scalar_tensor_tensor",
                               out=resT2[d][:, tok],
                               in0=P[di][:, 0:256], scalar=0.2,
                               in1=resT2[d][:, tok], op0=AL.mult, op1=AL.add)
                ep_sh = ("vv", b.now("vv"))
                b.w("tensor", "vv", b.now("vv"))
        b.barrier()

        for e in range(E):
            b.w("tensor", "vv", b.now("vv"))
            b.w("tensor", "gg", b.now("gg"))
            tsel = None
            for hf in range(2):
                tsel = b.op("tensor", "matmul", out=P[6 + hf][0:1, :],
                            lhsT=identS[0:16, e:e + 1],
                            rhs=cwT[:, hf * 512:(hf + 1) * 512],
                            start=True, stop=True)
            b.w("vector", "tms", tsel)
            b.w("vector", "gg", b.now("gg"))
            vsel = None
            for hf in range(2):
                vsel = b.op("vector", "tensor_copy",
                            out=rowA[0:1, hf * 512:(hf + 1) * 512],
                            in_=P[6 + hf][0:1, :])
            b.w("tensor", "vv", vsel)
            tbc = None
            for hf in range(2):
                tbc = b.op("tensor", "matmul", out=P[6 + hf][:, :], lhsT=onesrS[:],
                           rhs=rowA[0:1, hf * 512:(hf + 1) * 512],
                           start=True, stop=True)
            b.w("vector", "tms", tbc)
            vbc = None
            for hf in range(2):
                vbc = b.op("vector", "tensor_copy",
                           out=cwbc[:, hf * 512:(hf + 1) * 512], in_=P[6 + hf][:, :])
            gb_last = 0
            ht_done = {}

            def exp_mm(slot, bank0, half):
                for hf in range(2):
                    for lc in range(8):
                        c = half * 8 + lc
                        wh = slot[:, lc * 128:(lc + 1) * 128]
                        wl = slot[:, 1024 + lc * 128:1024 + (lc + 1) * 128]
                        tks = slice(hf * 512, (hf + 1) * 512)
                        st = (half == 0 and lc == 0)
                        sp = (half == 1 and lc == 7)
                        b.op("tensor", "matmul", out=P[bank0 + hf][:, :], lhsT=wh,
                             rhs=xnH[c][:, tks], start=st, stop=False, sig=None)
                        b.op("tensor", "matmul", out=P[bank0 + hf][:, :], lhsT=wh,
                             rhs=xnL[c][:, tks], start=False, stop=False, sig=None)
                        b.op("tensor", "matmul", out=P[bank0 + hf][:, :], lhsT=wl,
                             rhs=xnH[c][:, tks], start=False, stop=sp,
                             sig=(None if not sp else "auto"))

            for ht in range(HMC):
                ab_base = 0 if ht % 2 == 0 else 4
                if ht - 2 in ht_done:
                    b.w("tensor", "ss", ht_done[ht - 2][0])
                    b.w("tensor", "vv", ht_done[ht - 2][1])
                elif ht < 2:
                    b.w("tensor", "ss", b.now("ss"))
                    b.w("tensor", "vv", b.now("vv"))
                for half in range(2):
                    k1 = ws.load_split(we1t.ap()[e, ht, :, half * 1024:(half + 1) * 1024])
                    slot = ws.use(k1)
                    exp_mm_half(slot, ab_base, half) if False else None
                    for hf in range(2):
                        for lc in range(8):
                            c = half * 8 + lc
                            wh = slot[:, lc * 128:(lc + 1) * 128]
                            wl = slot[:, 1024 + lc * 128:1024 + (lc + 1) * 128]
                            tks = slice(hf * 512, (hf + 1) * 512)
                            st = (half == 0 and lc == 0)
                            sp = (half == 1 and lc == 7)
                            b.op("tensor", "matmul", out=P[ab_base + hf][:, :],
                                 lhsT=wh, rhs=xnH[c][:, tks], start=st, stop=False,
                                 sig=None)
                            b.op("tensor", "matmul", out=P[ab_base + hf][:, :],
                                 lhsT=wh, rhs=xnL[c][:, tks], start=False,
                                 stop=False, sig=None)
                            b.op("tensor", "matmul", out=P[ab_base + hf][:, :],
                                 lhsT=wl, rhs=xnH[c][:, tks], start=False, stop=sp,
                                 sig=("auto" if (lc == 7 and hf == 1) else None))
                    ws.release(k1, "tms", b.now("tms"))
                t1 = b.now("tms")
                for half in range(2):
                    k3 = ws.load_split(we3t.ap()[e, ht, :, half * 1024:(half + 1) * 1024])
                    slot = ws.use(k3)
                    for hf in range(2):
                        for lc in range(8):
                            c = half * 8 + lc
                            wh = slot[:, lc * 128:(lc + 1) * 128]
                            wl = slot[:, 1024 + lc * 128:1024 + (lc + 1) * 128]
                            tks = slice(hf * 512, (hf + 1) * 512)
                            st = (half == 0 and lc == 0)
                            sp = (half == 1 and lc == 7)
                            b.op("tensor", "matmul", out=P[ab_base + 2 + hf][:, :],
                                 lhsT=wh, rhs=xnH[c][:, tks], start=st, stop=False,
                                 sig=None)
                            b.op("tensor", "matmul", out=P[ab_base + 2 + hf][:, :],
                                 lhsT=wh, rhs=xnL[c][:, tks], start=False,
                                 stop=False, sig=None)
                            b.op("tensor", "matmul", out=P[ab_base + 2 + hf][:, :],
                                 lhsT=wl, rhs=xnH[c][:, tks], start=False, stop=sp,
                                 sig=("auto" if (lc == 7 and hf == 1) else None))
                    ws.release(k3, "tms", b.now("tms"))
                t3 = b.now("tms")
                b.w("scalar", "tms", t1)
                b.w("scalar", "vv", b.now("vv"))
                ssl = None
                for hf in range(2):
                    ssl = b.op("scalar", "activation",
                               out=sq_t[:, hf * 512:(hf + 1) * 512],
                               in_=P[ab_base + hf][:, :], func=AF.Silu)
                b.w("vector", "ss", ssl)
                b.w("vector", "tms", t3)
                b.w("vector", "gg", b.now("gg"))
                vg1 = None
                for hf in range(2):
                    vg1 = b.op("vector", "tensor_mul",
                               out=mbc[:, hf * 512:(hf + 1) * 512],
                               in0=sq_t[:, hf * 512:(hf + 1) * 512],
                               in1=P[ab_base + 2 + hf][:, :])
                b.w("gpsimd", "vv", vg1)
                b.w("gpsimd", "tms", b.now("tms"))
                gfl = b.op("gpsimd", "tensor_mul", out=gful[:],
                           in0=mbc[:, 0:TL], in1=cwbc[:])
                b.w("scalar", "gg", gfl)
                gh = b.op("scalar", "copy",
                          out=gbufBF[:, ht * 2 * TL:ht * 2 * TL + TL], in_=gful[:])
                b.w("vector", "ss", gh)
                gb_last = b.op("vector", "tensor_sub",
                               out=gbufBF[:, ht * 2 * TL + TL:(ht + 1) * 2 * TL],
                               in0=gful[:],
                               in1=gbufBF[:, ht * 2 * TL:ht * 2 * TL + TL])
                ht_done[ht] = (ssl, vg1)
            ep_rec = {}
            for dt in range(DC):
                ki = ws.load_split(we2t.ap()[e, dt, :, :])
                slot = ws.use(ki)
                b.w("tensor", "vv", gb_last)
                if dt - 2 in ep_rec:
                    b.w("tensor", "vv", ep_rec[dt - 2])
                elif dt < 2:
                    b.w("tensor", "vv", b.now("vv"))
                    b.w("tensor", "ss", b.now("ss"))
                tmm2 = {}
                for hf in range(2):
                    pb = P[(dt % 2) * 2 + hf + 4]
                    for hc in range(HMC):
                        wh = slot[:, hc * 128:(hc + 1) * 128]
                        wl = slot[:, 1024 + hc * 128:1024 + (hc + 1) * 128]
                        gh = gbufBF[:, hc * 2 * TL + hf * 512:hc * 2 * TL + (hf + 1) * 512]
                        gl = gbufBF[:, hc * 2 * TL + TL + hf * 512:hc * 2 * TL + TL + (hf + 1) * 512]
                        b.op("tensor", "matmul", out=pb[:, :], lhsT=wh, rhs=gh,
                             start=(hc == 0), stop=False, sig=None)
                        b.op("tensor", "matmul", out=pb[:, :], lhsT=wh, rhs=gl,
                             start=False, stop=False, sig=None)
                        b.op("tensor", "matmul", out=pb[:, :], lhsT=wl, rhs=gh,
                             start=False, stop=(hc == HMC - 1),
                             sig=(None if hc < HMC - 1 else "auto"))
                    tmm2[hf] = b.now("tms")
                epv = None
                for hf in range(2):
                    pb = P[(dt % 2) * 2 + hf + 4]
                    b.w("vector", "tms", tmm2[hf])
                    epv = b.op("vector", "scalar_tensor_tensor",
                               out=resT2[dt][:, hf * 512:(hf + 1) * 512],
                               in0=pb[:, :], scalar=0.8,
                               in1=resT2[dt][:, hf * 512:(hf + 1) * 512],
                               op0=AL.mult, op1=AL.add)
                ep_rec[dt] = epv
                ws.release(ki, "tms", tmm2[1])
        b.barrier()
        layernorm(resT2, "f32")
        b.barrier()
        xnC = [resA[:, c, :] for c in range(DC)]

        t_rp = b.dma("gpsimd", ropeS[:], ropet.ap())
        t_mk = b.dma("gpsimd", maskC[:], maskt.ap())
        for e_ in ("tensor", "vector", "scalar"):
            b.w(e_, "dg", t_mk)
        kT = aux(0, 12 * 128).rearrange("p (a m) -> p a m", a=12)
        vtm = aux(12 * 128, 12 * 128).rearrange("p (a m) -> p a m", a=12)
        attb = aux(24 * 128, 16 * 128).rearrange("p (a m) -> p a m", a=16)
        exps = aux(40 * 128, 5 * 128).rearrange("p (a m) -> p a m", a=5)
        qrope = aux(45 * 128, 128)
        rbc2 = aux(46 * 128, 128)
        rsb = rowA[0:1, 0:128]
        cosT = ropeS[0:32, :]
        sinT = ropeS[32:64, :]

        def kv_block(blk):
            t0 = (blk - 4) * 128
            tokb = slice(t0, t0 + 128)
            pb = P[blk % 2]
            b.w("tensor", "vv", b.now("vv"))
            for half in range(2):
                ki = ws.load_raw(wkt.ap()[:, half * 1024:(half + 1) * 1024])
                slot = ws.useF(ki)
                for lc in range(8):
                    c = half * 8 + lc
                    b.op("tensor", "matmul", out=pb[:, 0:128],
                         lhsT=slot[:, lc * 128:(lc + 1) * 128],
                         rhs=xnC[c][:, tokb],
                         start=(c == 0), stop=(c == DC - 1),
                         sig=("auto" if lc == 7 else None))
                ws.release(ki, "tms", b.now("tms"))
            tk = b.now("tms")
            b.w("vector", "tms", tk)
            cs = cosT[:, tokb]
            sn = sinT[:, tokb]
            kb_ = kT[:, blk, :]
            b.op("vector", "tensor_mul", out=kb_[0:32, :], in0=pb[0:32, 0:128], in1=cs)
            b.op("vector", "tensor_mul", out=rtmp[0:32, :], in0=pb[32:64, 0:128],
                 in1=sn)
            b.op("vector", "tensor_sub", out=kb_[0:32, :], in0=kb_[0:32, :],
                 in1=rtmp[0:32, :])
            b.op("vector", "tensor_mul", out=kb_[32:64, :], in0=pb[0:32, 0:128],
                 in1=sn)
            b.op("vector", "tensor_mul", out=rtmp[32:64, :], in0=pb[32:64, 0:128],
                 in1=cs)
            b.op("vector", "tensor_add", out=kb_[32:64, :], in0=kb_[32:64, :],
                 in1=rtmp[32:64, :])
            vk = b.op("vector", "tensor_copy", out=kb_[64:128, :],
                      in_=pb[64:128, 0:128])
            pv = P[2 + blk % 2]
            b.w("tensor", "vv", vk)
            for half in range(2):
                ki = ws.load_raw(wvt.ap()[:, half * 1024:(half + 1) * 1024])
                slot = ws.useF(ki)
                for lc in range(8):
                    c = half * 8 + lc
                    b.op("tensor", "matmul", out=pv[:, 0:128],
                         lhsT=xnC[c][:, tokb],
                         rhs=slot[:, lc * 128:(lc + 1) * 128],
                         start=(c == 0), stop=(c == DC - 1),
                         sig=("auto" if lc == 7 else None))
                ws.release(ki, "tms", b.now("tms"))
            tv_ = b.now("tms")
            b.w("vector", "tms", tv_)
            return b.op("vector", "tensor_copy", out=vtm[:, blk, :], in_=pv[:, 0:128])

        for blk in (8, 9, 10, 11):
            kv_tail = kv_block(blk)
        b.w("gpsimd", "vv", kv_tail)
        tg1 = b.dma("gpsimd", cc2i.ap()[0:4, :, :].rearrange("a p m -> p a m"),
                    kT[:, 8:12, :])
        tg2 = b.dma("gpsimd", cc2i.ap()[4:8, :, :].rearrange("a p m -> p a m"),
                    vtm[:, 8:12, :])
        b.w("gpsimd", "dg", tg2)
        ccv2 = b.op("gpsimd", "collective_compute", "AllGather", AL.bypass,
                    replica_groups=[[0, 1, 2, 3], [4, 5, 6, 7]],
                    ins=[cc2i.ap().opt()], outs=[cc2o.ap().opt()],
                    sig="cc", serial=False)
        for blk in (4, 5, 6, 7):
            kv_own = kv_block(blk)
        b.w("gpsimd", "cc", ccv2)
        vprev = kv_own
        for part in range(2):
            dst = aux(part * 12 * 128, 4 * 128)
            for r in range(4):
                b.w("gpsimd", "vv", vprev)
                tgr = b.dma("gpsimd", gsm[:, :],
                            cc2o.ap()[r * 8 + part * 4:r * 8 + part * 4 + 4, :, :]
                            .rearrange("a p m -> p a m"))
                b.w("vector", "dg", tgr)
                if r == 0:
                    vprev = b.op("vector", "tensor_scalar_mul", out=dst,
                                 in0=gsm[:].rearrange("p (a m) -> p a m", a=4)
                                 .rearrange("p a m -> p (a m)"),
                                 scalar1=hsel[:, 0:1])
                else:
                    vprev = b.op("vector", "scalar_tensor_tensor", out=dst,
                                 in0=gsm[:], scalar=hsel[:, r:r + 1], in1=dst,
                                 op0=AL.mult, op1=AL.add)
        b.barrier()

        qr2 = aux(47 * 128, 128)
        for qt in range(8):
            tokq = slice(qt * 128, (qt + 1) * 128)
            att_last = 0
            tse = 0
            qrb = [qrope, qr2]

            def qproj(h):
                for half in range(2):
                    ki = ws.load_raw(wqt.ap()[h, :, half * 1024:(half + 1) * 1024])
                    slot = ws.useF(ki)
                    for lc in range(8):
                        c = half * 8 + lc
                        b.op("tensor", "matmul", out=P0[:, 0:128],
                             lhsT=slot[:, lc * 128:(lc + 1) * 128],
                             rhs=xnC[c][:, tokq],
                             start=(c == 0), stop=(c == DC - 1),
                             sig=("auto" if lc == 7 else None))
                    ws.release(ki, "tms", b.now("tms"))
                tq_ = b.now("tms")
                qr = qrb[h % 2]
                b.w("vector", "tms", tq_)
                cs = cosT[:, tokq]
                sn = sinT[:, tokq]
                b.op("vector", "tensor_mul", out=qr[0:32, :], in0=P0[0:32, 0:128],
                     in1=cs)
                b.op("vector", "tensor_mul", out=rtmp[0:32, :], in0=P0[32:64, 0:128],
                     in1=sn)
                b.op("vector", "tensor_sub", out=qr[0:32, :], in0=qr[0:32, :],
                     in1=rtmp[0:32, :])
                b.op("vector", "tensor_mul", out=qr[32:64, :], in0=P0[0:32, 0:128],
                     in1=sn)
                b.op("vector", "tensor_mul", out=rtmp[32:64, :], in0=P0[32:64, 0:128],
                     in1=cs)
                b.op("vector", "tensor_add", out=qr[32:64, :], in0=qr[32:64, :],
                     in1=rtmp[32:64, :])
                return b.op("vector", "tensor_copy", out=qr[64:128, :],
                            in_=P0[64:128, 0:128])

            b.w("tensor", "vv", b.now("vv"))
            b.w("tensor", "ss", b.now("ss"))
            vq = qproj(0)
            vq_next = None
            sb = [P1, P2, P6]
            exp_rec = [0] * 5
            for h in range(NH):
                qr = qrb[h % 2]
                b.w("tensor", "vv", vq)
                for i in range(5):
                    kb = qt + i
                    pb = sb[i % 3]
                    b.w("tensor", "ss", exp_rec[[3, 4, 2, 0, 1][i]])
                    tsc = b.op("tensor", "matmul", out=pb[:, 0:128],
                               lhsT=kT[:, kb, :], rhs=qr, start=True, stop=True)
                    if i == 0:
                        mi = 2 if qt < 4 else 1
                    elif i == 4:
                        mi = 0
                    elif qt + i < 4:
                        mi = 3
                    else:
                        mi = None
                    if mi is not None:
                        b.w("vector", "tms", tsc)
                        vmk = b.op("vector", "tensor_add", out=pb[:, 0:128],
                                   in0=pb[:, 0:128], in1=maskC[:, mi, :])
                        b.w("scalar", "vv", vmk)
                    else:
                        b.w("scalar", "tms", tsc)
                    b.w("scalar", "tms", tse)
                    exp_rec[i] = b.op("scalar", "activation", out=exps[:, i, :],
                                      in_=pb[:, 0:128], func=AF.Exp, scale=SCALE)
                if h + 1 < NH:
                    vq_next = qproj(h + 1)
                b.w("tensor", "vv", att_last)
                for i in range(5):
                    kb = qt + i
                    b.w("tensor", "ss", exp_rec[i])
                    b.op("tensor", "matmul", out=P3[0:1, 0:128], lhsT=onesc,
                         rhs=exps[:, i, :], start=(i == 0), stop=(i == 4), sig=None)
                    b.op("tensor", "matmul", out=P5[:, 0:128], lhsT=vtm[:, kb, :],
                         rhs=exps[:, i, :], start=(i == 0), stop=(i == 4),
                         sig=(None if i < 4 else "auto"))
                tse = b.now("tms")
                b.w("vector", "tms", tse)
                vrc = b.op("vector", "reciprocal", out=rsb, in_=P3[0:1, 0:128])
                b.w("tensor", "vv", vrc)
                tbc = b.op("tensor", "matmul", out=P4[:, 0:128], lhsT=onesrS[:],
                           rhs=rsb, start=True, stop=True)
                b.w("vector", "tms", tbc)
                b.op("vector", "tensor_copy", out=rbc2, in_=P4[:, 0:128])
                att_last = b.op("vector", "tensor_mul", out=attb[:, h, :],
                                in0=P5[:, 0:128], in1=rbc2)
                vq = vq_next
            for dh in range(2):
                lastd = None
                for di in range(8):
                    d = dh * 8 + di
                    if di == 0:
                        b.w("tensor", "vv", att_last)
                        b.w("tensor", "gg", b.now("gg"))
                        b.w("tensor", "vv", b.now("vv"))
                    for half in range(2):
                        ki = ws.load_raw(wot.ap()[d, :, half * 1024:(half + 1) * 1024])
                        slot = ws.useF(ki)
                        for lh in range(8):
                            hc = half * 8 + lh
                            lastd = b.op("tensor", "matmul", out=P[di][:, 0:128],
                                         lhsT=slot[:, lh * 128:(lh + 1) * 128],
                                         rhs=attb[:, hc, :],
                                         start=(hc == 0), stop=(hc == NH - 1),
                                         sig=("auto" if lh == 7 else None))
                        ws.release(ki, "tms", b.now("tms"))
                for di in range(8):
                    d = dh * 8 + di
                    b.w("vector", "tms", lastd)
                    b.op("vector", "scalar_tensor_tensor", out=resT2[d][:, tokq],
                         in0=P[di][:, 0:128], scalar=bo[:, d:d + 1],
                         in1=resT2[d][:, tokq], op0=AL.add, op1=AL.add)
        b.barrier()
        if "x3" in taps:
            tv = None
            for c in range(DC):
                tv = b.dma("gpsimd", taps["x3"].ap().rearrange(
                    "(c p) n -> p c n", p=128)[:, c, :], resT2[c])
            b.w("gpsimd", "dg", tv)
            b.barrier()

        layernorm(resT2, "split")
        b.barrier()

        abuf = aux(0, D)
        thb = aux(D, 128)
        auxBF = auxB.bitcast(BF16)
        qsl2 = [auxBF[:, 2 * (D + 128) + i * D:2 * (D + 128) + (i + 1) * D]
                for i in range(2)]
        store_t = None
        prep_rec = {}

        def prep(dt):
            kis = [ws.load_raw(wltt.ap()[dt, :, 0:1024]),
                   ws.load_raw(wltt.ap()[dt, :, 1024:2048])]
            qsl = qsl2[dt % 2]
            pb = 6 if dt % 2 == 0 else 4
            for half in range(2):
                slot = ws.useF(kis[half], eng="scalar")
                b.w("vector", *ws.loaded[kis[half]][:2])
                if dt - 1 in prep_rec:
                    b.w("scalar", "tms", prep_rec[dt - 1][1])
                    b.w("scalar", "vv", prep_rec[dt - 1][0])
                else:
                    b.w("scalar", "tms", b.now("tms"))
                sab = b.op("scalar", "activation", out=bigA[:, half, :],
                           in_=slot[:, 0:1024], func=AF.Abs)
                b.w("tensor", "ss", sab)
                if half == 0:
                    b.w("tensor", "vv", b.now("vv"))
                for lc in range(8):
                    c = half * 8 + lc
                    b.op("tensor", "matmul", out=P[pb][0:1, 0:128], lhsT=onesc,
                         rhs=abuf[:, c * 128:(c + 1) * 128],
                         start=(c == 0), stop=(c == DC - 1),
                         sig=(None if c < DC - 1 else "auto"))
            tsa = b.now("tms")
            b.w("vector", "tms", tsa)
            vth = b.op("vector", "tensor_scalar_mul", out=rsb,
                       in0=P[pb][0:1, 0:128], scalar1=0.7 / D)
            b.w("tensor", "vv", vth)
            tbc = b.op("tensor", "matmul", out=P[pb + 1][:, 0:128], lhsT=onesrS[:],
                       rhs=rsb, start=True, stop=True)
            b.w("vector", "tms", tbc)
            vthb = b.op("vector", "tensor_copy", out=thb, in_=P[pb + 1][:, 0:128])
            thbb = thb.rearrange("p (a m) -> p a m", a=1).broadcast_to([128, 8, 128])
            vq = None
            for half in range(2):
                slot = ws.useF(kis[half], eng="vector")
                bufv = slot[:, 0:1024].rearrange("p (a m) -> p a m", a=8)
                qslv = qsl[:, half * 1024:(half + 1) * 1024] \
                    .rearrange("p (a m) -> p a m", a=8)
                abv = abuf[:, half * 1024:(half + 1) * 1024] \
                    .rearrange("p (a m) -> p a m", a=8)
                b.op("vector", "tensor_tensor", out=abv, in0=bufv, in1=thbb,
                     op=AL.add)
                b.op("vector", "tensor_tensor", out=qslv, in0=bufv, in1=thbb,
                     op=AL.is_gt)
                b.op("vector", "tensor_single_scalar",
                     out=abuf[:, half * 1024:(half + 1) * 1024],
                     in_=abuf[:, half * 1024:(half + 1) * 1024], scalar=0.0,
                     op=AL.is_lt)
                vq = b.op("vector", "tensor_sub",
                          out=qsl[:, half * 1024:(half + 1) * 1024],
                          in0=qsl[:, half * 1024:(half + 1) * 1024],
                          in1=abuf[:, half * 1024:(half + 1) * 1024])
                ws.release(kis[half], "vv", vq)
            prep_rec[dt] = (vq, tsa)
            return vq

        prep(0)
        for dt in range(DC):
            if dt + 1 < DC:
                prep(dt + 1)
            qsl = qsl2[dt % 2]
            b.w("tensor", "vv", prep_rec[dt][0])
            for hf in range(2):
                for c in range(DC):
                    qc = qsl[:, c * 128:(c + 1) * 128]
                    b.op("tensor", "matmul", out=P[dt % 2][:, :],
                         lhsT=qc, rhs=xnH[c][:, hf * 512:(hf + 1) * 512],
                         start=(c == 0), stop=False, sig=None)
                    b.op("tensor", "matmul", out=P[dt % 2][:, :],
                         lhsT=qc, rhs=xnL[c][:, hf * 512:(hf + 1) * 512],
                         start=False, stop=(c == DC - 1),
                         sig=(None if c < DC - 1 else "auto"))
                lastq = b.now("tms")
                b.w("vector", "tms", lastq)
                epd = b.op("vector", "scalar_tensor_tensor",
                           out=resT2[dt][:, hf * 512:(hf + 1) * 512],
                           in0=P[dt % 2][:, :], scalar=lb[:, dt:dt + 1],
                           in1=resT2[dt][:, hf * 512:(hf + 1) * 512],
                           op0=AL.add, op1=AL.add)
                b.w("tensor", "vv", epd)
            b.w("gpsimd", "vv", b.now("vv"))
            store_t = b.dma("gpsimd",
                            y.ap().rearrange("(c p) n -> p c n", p=128)[:, dt, :],
                            resT2[dt])
        b.w("gpsimd", "dg", store_t)

        @block.sync
        def _(sync):
            for fn in b.st["sync"]:
                fn(sync)

        @block.tensor
        def _(tensor):
            for fn in b.st["tensor"]:
                fn(tensor)

        @block.vector
        def _(vector):
            for fn in b.st["vector"]:
                fn(vector)

        @block.scalar
        def _(scalar):
            for fn in b.st["scalar"]:
                fn(scalar)

        @block.gpsimd
        def _(gpsimd):
            for fn in b.st["gpsimd"]:
                fn(gpsimd)

    return nc



def _tile_w_sD(W):
    S_, D_ = W.shape
    return np.ascontiguousarray(
        W.reshape(S_ // 128, 128, D_ // 128, 128).transpose(0, 3, 2, 1)
    ).reshape(S_ // 128, 128, D_)


def _tile_lhsT(Wt):
    K, M = Wt.shape
    return np.ascontiguousarray(
        Wt.reshape(K // 128, 128, M // 128, 128).transpose(2, 1, 0, 3)
    ).reshape(M // 128, 128, (K // 128) * 128)


def _col16(v):
    return np.ascontiguousarray(np.asarray(v, dtype=np.float32).reshape(16, 128).T)


def make_inputs(I):
    x = np.asarray(I["L0_attn_x"], dtype=np.float32)

    wl0 = np.ascontiguousarray(np.stack([
        _tile_w_sD(np.asarray(I["L0_attn_Wga_w"], dtype=np.float32)),
        _tile_w_sD(np.asarray(I["L0_attn_Wgb_w"], dtype=np.float32)),
        _tile_w_sD(np.asarray(I["L0_attn_Wf_w"], dtype=np.float32)),
        _tile_w_sD(np.asarray(I["L0_attn_Wzg_w"], dtype=np.float32)),
        _tile_w_sD(np.asarray(I["L0_attn_Wz_w"], dtype=np.float32)),
    ]))
    woutt = np.ascontiguousarray(
        np.asarray(I["L0_attn_Wout_w"], dtype=np.float32)
        .reshape(16, 128, 16, 128).transpose(2, 3, 0, 1)).reshape(16, 128, 2048)
    wrout = np.ascontiguousarray(
        np.asarray(I["L0_ffn_w_router"], dtype=np.float32)
        .reshape(16, 128, 16).transpose(1, 0, 2)).reshape(128, 256)
    we1t = np.ascontiguousarray(
        np.asarray(I["L0_ffn_we1"], dtype=np.float32)
        .reshape(16, 16, 128, 8, 128).transpose(0, 3, 2, 1, 4)).reshape(16, 8, 128, 2048)
    we3t = np.ascontiguousarray(
        np.asarray(I["L0_ffn_we3"], dtype=np.float32)
        .reshape(16, 16, 128, 8, 128).transpose(0, 3, 2, 1, 4)).reshape(16, 8, 128, 2048)
    we2t = np.ascontiguousarray(
        np.asarray(I["L0_ffn_we2"], dtype=np.float32)
        .reshape(16, 8, 128, 16, 128).transpose(0, 3, 2, 1, 4)).reshape(16, 16, 128, 1024)
    ws1t = _tile_lhsT(np.asarray(I["L0_ffn_ws1"], dtype=np.float32))
    ws3t = _tile_lhsT(np.asarray(I["L0_ffn_ws3"], dtype=np.float32))
    ws2t = np.ascontiguousarray(
        np.asarray(I["L0_ffn_ws2"], dtype=np.float32)
        .reshape(32, 128, 2, 8, 128).transpose(2, 0, 1, 3, 4)).reshape(2, 32, 128, 1024)
    wqt = _tile_lhsT(np.asarray(I["L1_attn_Wq"], dtype=np.float32))
    wkt = np.ascontiguousarray(
        np.asarray(I["L1_attn_Wk"], dtype=np.float32)
        .reshape(16, 128, 128).transpose(1, 0, 2)).reshape(128, 2048)
    wvt = np.ascontiguousarray(
        np.asarray(I["L1_attn_Wv"], dtype=np.float32)
        .reshape(16, 128, 128).transpose(1, 0, 2)).reshape(128, 2048)
    wot = _tile_lhsT(np.asarray(I["L1_attn_Wo"], dtype=np.float32))
    wltt = np.ascontiguousarray(
        np.asarray(I["L1_ffn_weight"], dtype=np.float32)
        .reshape(16, 128, 16, 128).transpose(0, 3, 2, 1)).reshape(16, 128, 2048)

    ident = np.eye(128, dtype=np.float32)
    onesr = np.ones((1, 128), dtype=np.float32)

    a = np.arange(128)
    mdiag = np.where(a[None, :] >= a[:, None], 0.0, NEG).astype(np.float32)
    mfar = np.where(a[None, :] < a[:, None], 0.0, NEG).astype(np.float32)
    mneg = np.full((128, 128), NEG, dtype=np.float32)
    mzero = np.zeros((128, 128), dtype=np.float32)

    shared = dict(wl0=wl0, woutt=woutt, wrout=wrout, we1t=we1t, we3t=we3t,
                  we2t=we2t, ws1t=ws1t, ws3t=ws3t, ws2t=ws2t, wqt=wqt, wkt=wkt,
                  wvt=wvt, wot=wot, wltt=wltt, identm=ident, onesr=onesr)

    cbase = np.zeros((128, 96), dtype=np.float32)
    cbase[:, 0:16] = _col16(I["L0_attn_rms_w"])
    cbase[:, 16:32] = _col16(I["L0_attn_Wf_b"])
    cbase[:, 32:48] = _col16(I["L0_attn_Wout_b"])
    cbase[:, 48:64] = _col16(I["L1_attn_bo"])
    cbase[:, 64:80] = _col16(I["L1_ffn_bias"])
    cbase[:, 90] = 1.0
    cbase[:, 91] = 1e-5
    cbase[:, 92] = SCALE

    in_maps = []
    for c in range(8):
        bq, q = c // 4, c % 4
        x0tc = np.ascontiguousarray(x[bq, q * 1024:(q + 1) * 1024, :].T)
        cp = cbase.copy()
        for j in range(3):
            cp[:, 80 + j] = 1.0 if q > j else 0.0
            cp[:, 83 + j] = 0.0 if q > j else 1.0
        for r in range(4):
            cp[:, 86 + r] = 1.0 if (q > 0 and r == q - 1) else 0.0
        pos = np.arange(1024, dtype=np.float32) + np.float32(q * 1024)
        invf = np.exp(-np.log(np.float32(10000.0)) *
                      (np.arange(0, 64, 2, dtype=np.float32) /
                       np.float32(64.0))).astype(np.float32)
        theta = (pos[None, :] * invf[:, None]).astype(np.float32)
        ropetc = np.ascontiguousarray(
            np.concatenate([np.cos(theta), np.sin(theta)], axis=0).astype(np.float32))
        masktc = np.ascontiguousarray(np.stack(
            [mdiag, mfar, (mfar if q > 0 else mneg), (mzero if q > 0 else mneg)],
            axis=1))
        m = dict(shared)
        m.update(x0t=x0tc, cpack=cp, ropet=ropetc, maskt=masktc)
        in_maps.append(m)
    return in_maps


def kernel(**inputs):
    in_maps = make_inputs(inputs)
    nc = build()
    res = run_bass_kernel_spmd(nc, in_maps, core_ids=list(range(8)))
    out = np.empty((2, 4096, 2048), dtype=np.float32)
    for c in range(8):
        bq, q = c // 4, c % 4
        out[bq, q * 1024:(q + 1) * 1024, :] = res.results[c]["y"].T
    return out
